# version 27
# baseline (speedup 1.0000x reference)
import numpy as np
import concourse.bass as bass
import concourse.mybir as mybir

F32 = mybir.dt.float32
F16 = mybir.dt.float16
BF16 = mybir.dt.bfloat16
I32 = mybir.dt.int32
I8 = mybir.dt.int8
AF = mybir.ActivationFunctionType
ALU = mybir.AluOpType

D = 2048
S = 2048
NB = 4
BLK = 512
NT = 16
NH = 16
KVH = 4
HD = 128
LAT = 512
EPS_LN = 1e-5
EPS_A = 1e-6
SMAX = float(1.0 / np.sqrt(np.float32(128.0)))
PI_HI = 6.28125
PI_LO = float(2.0 * np.pi - 6.28125)
INV2PI = float(1.0 / (2.0 * np.pi))
HALFPI = float(np.pi / 2.0)
LN1E4 = float(np.log(10000.0))


class Mat:
    def __init__(self, t, nt, C):
        self.t_, self.nt, self.C = t, nt, C

    def t(self, j):
        return self.t_[:, j * self.C:(j + 1) * self.C]

    def tc(self, j, c0, c1):
        return self.t_[:, j * self.C + c0: j * self.C + c1]


def build(debug=(), stop_after=None):
    import contextlib
    nc = bass.Bass('TRN2', target_bir_lowering=False, debug=False)

    xT = nc.dram_tensor('xT', [D, S], F32, kind='ExternalInput').ap()
    wqT = nc.dram_tensor('wqT', [D, D], F32, kind='ExternalInput').ap()
    wkT = nc.dram_tensor('wkT', [D, 512], F32, kind='ExternalInput').ap()
    wvT = nc.dram_tensor('wvT', [D, 512], F32, kind='ExternalInput').ap()
    woaT = nc.dram_tensor('woaT', [D, D], F32, kind='ExternalInput').ap()
    wqdT = nc.dram_tensor('wqdT', [D, LAT], F32, kind='ExternalInput').ap()
    wkvdT = nc.dram_tensor('wkvdT', [D, LAT], F32, kind='ExternalInput').ap()
    wquT = nc.dram_tensor('wquT', [LAT, D], F32, kind='ExternalInput').ap()
    wknT = nc.dram_tensor('wknT', [LAT, 1024], F32, kind='ExternalInput').ap()
    wkrT = nc.dram_tensor('wkrT', [LAT, 1024], F32, kind='ExternalInput').ap()
    wvuT = nc.dram_tensor('wvuT', [LAT, D], F32, kind='ExternalInput').ap()
    wobT = nc.dram_tensor('wobT', [D, D], F32, kind='ExternalInput').ap()
    wf8T = nc.dram_tensor('wf8T', [D, D], I8, kind='ExternalInput').ap()
    c_bqd = nc.dram_tensor('c_bqd', [128, 4], F32, kind='ExternalInput').ap()
    c_bkvd = nc.dram_tensor('c_bkvd', [128, 4], F32, kind='ExternalInput').ap()
    c_bqu = nc.dram_tensor('c_bqu', [128, 16], F32, kind='ExternalInput').ap()
    c_bkn = nc.dram_tensor('c_bkn', [128, 8], F32, kind='ExternalInput').ap()
    c_bkr = nc.dram_tensor('c_bkr', [128, 8], F32, kind='ExternalInput').ap()
    c_bo = nc.dram_tensor('c_bo', [128, 16], F32, kind='ExternalInput').ap()
    c_fb = nc.dram_tensor('c_fb', [128, 16], F32, kind='ExternalInput').ap()
    c_ws = nc.dram_tensor('c_ws', [128, 16], F32, kind='ExternalInput').ap()
    r_bvu = nc.dram_tensor('r_bvu', [1, D], F32, kind='ExternalInput').ap()
    identin = nc.dram_tensor('identin', [128, 128], BF16, kind='ExternalInput').ap()
    maskin = nc.dram_tensor('maskin', [128, 128], BF16, kind='ExternalInput').ap()
    onesttin = nc.dram_tensor('onesttin', [128, 32], BF16, kind='ExternalInput').ap()
    os32fin = nc.dram_tensor('os32fin', [32, 128], F32, kind='ExternalInput').ap()
    os32bin = nc.dram_tensor('os32bin', [32, 128], BF16, kind='ExternalInput').ap()
    onecolbin = nc.dram_tensor('onecolbin', [128, 1], BF16, kind='ExternalInput').ap()
    iotacolin = nc.dram_tensor('iotacolin', [128, 1], F32, kind='ExternalInput').ap()
    eArowin = nc.dram_tensor('eArowin', [1, 128], F32, kind='ExternalInput').ap()
    eBcolin = nc.dram_tensor('eBcolin', [128, 4], F32, kind='ExternalInput').ap()
    signAin = nc.dram_tensor('signAin', [1, 128], F32, kind='ExternalInput').ap()
    identfin = nc.dram_tensor('identfin', [128, 128], F32, kind='ExternalInput').ap()
    iota32in = nc.dram_tensor('iota32in', [32, S], F32, kind='ExternalInput').ap()
    tabAcin = nc.dram_tensor('tabAcin', [128, 2048], BF16, kind='ExternalInput').ap()
    tabAsin = nc.dram_tensor('tabAsin', [128, 2048], BF16, kind='ExternalInput').ap()
    tabBcin = nc.dram_tensor('tabBcin', [128, 8192], BF16, kind='ExternalInput').ap()
    tabBsin = nc.dram_tensor('tabBsin', [128, 8192], BF16, kind='ExternalInput').ap()

    xoT = nc.dram_tensor('xoT', [D, S], F32, kind='ExternalOutput').ap()
    x2s = nc.dram_tensor('x2s', [D, S], F32, kind='Internal').ap()
    kAs = nc.dram_tensor('kAs', [D, S], BF16, kind='Internal').ap()
    aoas = nc.dram_tensor('aoas', [D, S], BF16, kind='Internal').ap()

    dbg_list = []
    dbg_t = {}
    def dbg(name, shape, dtype=F32):
        if name in debug:
            dbg_t[name] = nc.dram_tensor('dbg_' + name, shape, dtype, kind='ExternalOutput').ap()
            dbg_list.append('dbg_' + name)
            return dbg_t[name]
        return None

    ctx = contextlib.ExitStack()

    def sbt(name, shape, dt, st=None):
        return (st or ctx).enter_context(nc.sbuf_tensor(name, shape, dt))

    def pst(name, shape, dt, st=None):
        return (st or ctx).enter_context(nc.psum_tensor(name, shape, dt))

    ident = sbt('ident', [128, 128], BF16)
    maskT = sbt('maskT', [128, 128], BF16)
    onestt = sbt('onestt', [128, 32], BF16)
    os32f = sbt('os32f', [32, 128], F32)
    os32b = sbt('os32b', [32, 128], BF16)
    onecolb = sbt('onecolb', [128, 1], BF16)
    iotacol = sbt('iotacol', [128, 1], F32)
    eArow = sbt('eArow', [1, 128], F32)
    eBcol = sbt('eBcol', [128, 4], F32)
    signA = sbt('signA', [1, 128], F32)
    identf = sbt('identf', [128, 128], F32)
    cbqd = sbt('cbqd', [128, 4], F32)
    cbkvd = sbt('cbkvd', [128, 4], F32)
    cbqu = sbt('cbqu', [128, 16], F32)
    cbkn = sbt('cbkn', [128, 8], F32)
    cbkr = sbt('cbkr', [128, 8], F32)
    cbo = sbt('cbo', [128, 16], F32)
    cfb = sbt('cfb', [128, 16], F32)
    cws = sbt('cws', [128, 16], F32)
    bvu32 = sbt('bvu32', [32, D], BF16)
    cpool = sbt('cpool', [128, 8], F32)
    CP_ROUND, CP_QUANT, CP_EPSLN, CP_NHALF, CP_SMAX, CP_EPSA, CP_MU = 0, 1, 2, 3, 4, 5, 6
    c1col = sbt('c1col', [128, 16], F32)
    itmp = sbt('itmp', [128, 512], I32)
    itmp2 = sbt('itmp2', [128, 512], I32)
    rows32 = sbt('rows32', [32, 1536], F32)
    rowt = sbt('rowt', [1, 1536], F32)
    RW_M, RW_RSTD, RW_T1 = 0, 512, 1024
    inv32 = sbt('inv32', [32, 128], F32)
    invfgnb = sbt('invfgnb', [128, 128], F32)
    invfb = sbt('invfb', [128, 128], F32)
    invBcol = sbt('invBcol', [128, 4], F32)
    gst = Mat(sbt('gst', [128, 2048], F32), 2, 1024)
    mstP = sbt('mstP', [128, 2 * 512], BF16)
    mstP2 = sbt('mstP2', [128, 2 * 512], BF16)
    f16stP = sbt('f16stP', [128, 512], F16)

    s_dW = ctx.enter_context(nc.semaphore('s_dW'))
    s_dG = ctx.enter_context(nc.semaphore('s_dG'))
    s_dO = ctx.enter_context(nc.semaphore('s_dO'))
    s_T = ctx.enter_context(nc.semaphore('s_T'))
    s_V = ctx.enter_context(nc.semaphore('s_V'))
    s_S = ctx.enter_context(nc.semaphore('s_S'))

    blk = ctx.enter_context(nc.Block())

    CN = {'T': 0, 'V': 0, 'S': 0, 'dW': 0, 'dG': 0, 'dO': 0}
    SEM = {'T': s_T, 'V': s_V, 'S': s_S, 'dW': s_dW, 'dG': s_dG, 'dO': s_dO}

    def sec(engine_name, f, inc=None):
        dec = {'T': blk.tensor, 'V': blk.vector, 'S': blk.scalar,
               'G': blk.gpsimd, 'Y': blk.sync}[engine_name]
        def wrapper(eng):
            last = f(eng)
            if inc is not None:
                assert last is not None, f'section for {engine_name} returned no instr'
                CN[inc] += 1
                last.then_inc(SEM[inc], 1)
        dec(wrapper)

    def T(f):
        sec('T', f, 'T')
    def V(f):
        sec('V', f, 'V')
    def Sc(f):
        sec('S', f, 'S')
    def G(f):
        sec('G', f, None)
    def Y(f):
        sec('Y', f, None)

    def dma(eng, out, in_, sem_key):
        eng.dma_start(out=out, in_=in_).then_inc(SEM[sem_key], 16)
        CN[sem_key] += 16

    def _loads(sy):
        for dst, src in [(ident[:], identin), (maskT[:], maskin), (onestt[:], onesttin),
                         (os32f[:], os32fin), (os32b[:], os32bin), (onecolb[:], onecolbin),
                         (iotacol[:], iotacolin), (eArow[:], eArowin),
                         (eBcol[:], eBcolin), (signA[:], signAin),
                         (cbqd[:], c_bqd), (cbkvd[:], c_bkvd), (cbqu[:], c_bqu),
                         (cbkn[:], c_bkn), (cbkr[:], c_bkr), (cbo[:], c_bo),
                         (cfb[:], c_fb), (cws[:], c_ws), (rowt[0:1, 0:D] if D <= 1536 else None, r_bvu)]:
            pass
        return None

    def _loads2(sy):
        for dst, src in [(ident[:], identin), (maskT[:], maskin), (onestt[:], onesttin),
                         (os32f[:], os32fin), (os32b[:], os32bin), (onecolb[:], onecolbin),
                         (iotacol[:], iotacolin), (eArow[:], eArowin),
                         (eBcol[:], eBcolin), (signA[:], signAin),
                         (cbqd[:], c_bqd), (cbkvd[:], c_bkvd), (cbqu[:], c_bqu),
                         (cbkn[:], c_bkn), (cbkr[:], c_bkr), (cbo[:], c_bo),
                         (cfb[:], c_fb), (cws[:], c_ws), (identf[:], identfin),
                         (gst.t_[0:1, 0:D], r_bvu)]:
            dma(sy, dst, src, 'dW')
        return None
    Y(_loads2)
    DW0 = CN['dW']

    def _setup(v):
        v.wait_ge(s_dW, DW0)
        v.memset(bvu32[:], 0.0)
        v.memset(rows32[:], 0.0)
        v.memset(inv32[:], 0.0)
        v.memset(cpool[:], 0.0)
        v.drain()
        v.memset(cpool[:, CP_EPSLN:CP_EPSLN + 1], EPS_LN)
        v.memset(cpool[:, CP_NHALF:CP_NHALF + 1], -0.5)
        v.memset(cpool[:, CP_SMAX:CP_SMAX + 1], SMAX)
        v.memset(cpool[:, CP_EPSA:CP_EPSA + 1], EPS_A)
        v.memset(cpool[:, CP_MU:CP_MU + 1], 0.7)
        v.tensor_copy(bvu32[0:1, 0:D], gst.t_[0:1, 0:D])
        v.drain()
        v.tensor_copy(itmp[:, 0:1], cpool[:, CP_MU:CP_MU + 1])
        v.drain()
        v.tensor_copy(cpool[:, CP_MU:CP_MU + 1], itmp[:, 0:1])
        v.drain()
        v.tensor_scalar(cpool[:, CP_ROUND:CP_ROUND + 1], cpool[:, CP_MU:CP_MU + 1], -0.5, 0.5, ALU.mult, ALU.add)
        v.tensor_scalar(cpool[:, CP_QUANT:CP_QUANT + 1], cpool[:, CP_MU:CP_MU + 1], -0.5, 128.5, ALU.mult, ALU.add)
        return v.drain()
    V(_setup)
    V_SETUP = CN['V']

    def _invf(sc):
        sc.wait_ge(s_V, V_SETUP)
        sc.activation(rowt[0:1, RW_T1:RW_T1 + 128], eArow[:], AF.Exp, scale=LN1E4)
        sc.activation(invBcol[:], eBcol[:], AF.Exp, scale=LN1E4)
        return sc.drain()
    Sc(_invf)
    sN = CN['S']

    def _inv1(v, sN=sN):
        v.wait_ge(s_S, sN)
        v.tensor_copy(inv32[0:1, :], rowt[0:1, RW_T1:RW_T1 + 128])
        return v.drain()
    V(_inv1)

    st_setup = ctx.enter_context(contextlib.ExitStack())
    psX = pst('psX', [128, 512], F32, st_setup)

    vN = CN['V']
    def _invb1(te, vN=vN):
        te.wait_ge(s_V, vN)
        return te.matmul(psX[:, 0:128], os32f[:], inv32[:], start=True, stop=True)
    T(_invb1)

    tN = CN['T']
    def _inv2(v, tN=tN):
        v.wait_ge(s_T, tN)
        v.tensor_copy(invfb[:], psX[:, 0:128])
        v.tensor_mul(inv32[0:1, :], rowt[0:1, RW_T1:RW_T1 + 128], signA[:])
        return v.drain()
    V(_inv2)

    vN = CN['V']
    def _invb2(te, vN=vN):
        te.wait_ge(s_V, vN)
        return te.matmul(psX[:, 0:128], os32f[:], inv32[:], start=True, stop=True)
    T(_invb2)

    tN = CN['T']
    def _inv3(v, tN=tN):
        v.wait_ge(s_T, tN)
        v.tensor_copy(invfgnb[:], psX[:, 0:128])
        return v.drain()
    V(_inv3)
    V_INV = CN['V']

    def tab_red(v, ang, b):
        p = 1 if b >= 1024 else 0
        u = gst.t_[:, b + 256:b + 384]
        r1 = gst.t_[:, b + 384:b + 512]
        ic = itmp[:, p * 128:(p + 1) * 128]
        v.tensor_scalar(u, ang, INV2PI, cpool[:, CP_ROUND:CP_ROUND + 1], ALU.mult, ALU.add)
        v.drain()
        v.tensor_copy(ic, u)
        v.drain()
        v.tensor_copy(u, ic)
        v.drain()
        v.scalar_tensor_tensor(r1, u, -PI_HI, ang, ALU.mult, ALU.add)
        v.drain()
        v.scalar_tensor_tensor(ang, u, -PI_LO, r1, ALU.mult, ALU.add)
        v.drain()

    return_env = dict(locals())
    return _stages(return_env, debug=debug, stop_after=stop_after)


def _stages(env, debug=(), stop_after=None):
    import contextlib
    g = env
    nc = g['nc']
    T, V, Sc, G, Y = g['T'], g['V'], g['Sc'], g['G'], g['Y']
    sec = g['sec']
    CN, SEM, dma = g['CN'], g['SEM'], g['dma']
    s_dW, s_dG, s_dO = g['s_dW'], g['s_dG'], g['s_dO']
    s_T, s_V, s_S = g['s_T'], g['s_V'], g['s_S']
    sbt, pst = g['sbt'], g['pst']
    gst, itmp, cpool, rowt, rows32 = g['gst'], g['itmp'], g['cpool'], g['rowt'], g['rows32']
    itmp2 = g['itmp2']
    RW_M, RW_RSTD, RW_T1 = g['RW_M'], g['RW_RSTD'], g['RW_T1']
    invfb, invfgnb, invBcol = g['invfb'], g['invfgnb'], g['invBcol']
    identf = g['identf']
    iotacol, ident, maskT, onestt, os32f, os32b, onecolb = (
        g['iotacol'], g['ident'], g['maskT'], g['onestt'], g['os32f'], g['os32b'], g['onecolb'])
    cbqd, cbkvd, cbqu, cbkn, cbkr, cbo, cfb, cws, bvu32, c1col = (
        g['cbqd'], g['cbkvd'], g['cbqu'], g['cbkn'], g['cbkr'], g['cbo'], g['cfb'], g['cws'],
        g['bvu32'], g['c1col'])
    mst = g['mstP']
    mst2 = g['mstP2']
    f16st = g['f16stP']
    tab_red = g['tab_red']
    dbg = g['dbg']
    dbg_list = g['dbg_list']
    ctx = g['ctx']
    CPR, CPQ, CPE, CPH, CPS, CPA, CPM = (g['CP_ROUND'], g['CP_QUANT'], g['CP_EPSLN'],
                                          g['CP_NHALF'], g['CP_SMAX'], g['CP_EPSA'], g['CP_MU'])
    xT, xoT, x2s, kAs, aoas = g['xT'], g['xoT'], g['x2s'], g['kAs'], g['aoas']
    wqT, wkT, wvT, woaT = g['wqT'], g['wkT'], g['wvT'], g['woaT']
    wqdT, wkvdT, wquT, wknT, wkrT, wvuT, wobT, wf8T = (
        g['wqdT'], g['wkvdT'], g['wquT'], g['wknT'], g['wkrT'], g['wvuT'], g['wobT'], g['wf8T'])
    iota32in = g['iota32in']
    tabAcin, tabAsin = g['tabAcin'], g['tabAsin']
    tabBcin, tabBsin = g['tabBcin'], g['tabBsin']
    psX = g['psX']
    st_setup = g['st_setup']
    S_ = S
    V_INV = g['V_INV']

    SEMH = {'T': s_T, 'V': s_V, 'S': s_S, 'dW': s_dW, 'dG': s_dG, 'dO': s_dO}


    def mkap(h, extra_off, free_dims):
        a = h[:, :] if not isinstance(h, bass.AP) else h
        return bass.AP(a.tensor, a.offset + extra_off, [list(a.ap[0])] + [list(d) for d in free_dims])

    def w(eng, *deps):
        for d in deps:
            if CN[d] > 0:
                eng.wait_ge(SEMH[d], CN[d])

    MK = {}
    def mset(name, key):
        MK[name] = (key, CN[key])
    def mwait(eng, *names):
        for nm in names:
            if nm in MK:
                k, c = MK[nm]
                if c > 0:
                    eng.wait_ge(SEMH[k], c)

    wsA2t = sbt('wsA2', [128, NT * 512], BF16)

    stA_scope = ctx.enter_context(contextlib.ExitStack())
    tabAc = sbt('tabAc', [128, 16 * 128], BF16, stA_scope)
    tabAs = sbt('tabAs', [128, 16 * 128], BF16, stA_scope)
    Xa = Mat(sbt('Xa', [128, NT * 512], F32, stA_scope), NT, 512)
    ybf = Mat(sbt('ybf', [128, NT * 512], BF16, stA_scope), NT, 512)
    qfB = Mat(sbt('qfB', [128, 4 * 2048], BF16, stA_scope), 4, 2048)
    qfA = Mat(sbt('qfA', [128, NT * 512], BF16, stA_scope), NT, 512)
    kfB = Mat(sbt('kfB', [128, 4 * 512], BF16, stA_scope), 4, 512)
    kfA = Mat(sbt('kfA', [128, 4 * 512], BF16, stA_scope), 4, 512)
    vaug = Mat(sbt('vaug', [128, 4 * 516], BF16, stA_scope), 4, 516)
    stLa = sbt('stLa', [128, KVH * 129], F32, stA_scope)
    stbf = sbt('stbf', [128, KVH * 129], BF16, stA_scope)
    ws0 = Mat(sbt('ws0A', [128, NT * 512], BF16, stA_scope), NT, 512)
    ws1 = Mat(sbt('ws1A', [128, NT * 512], BF16, stA_scope), NT, 512)
    WS = [ws0, ws1]

    def _tabinit(v):
        w(v, 'V')
        v.memset(stLa[:], 0.0)
        v.memset(stbf[:], 0.0)
        v.memset(vaug.t_[:, :], 0.0)
        v.drain()
        for tt in range(4):
            for h in range(KVH):
                c0 = tt * 516 + h * 129 + 128
                v.memset(vaug.t_[:, c0:c0 + 1], 1.0)
        return v.drain()
    V(_tabinit)

    def _ldtabA(sy):
        dma(sy, tabAc[:, :], tabAcin, 'dW')
        dma(sy, tabAs[:, :], tabAsin, 'dW')
        return None
    Y(_ldtabA)

    if stop_after == 'setup':
        def _fin(sy):
            w(sy, 'S', 'V', 'T')
            for nm, src, dt in (('tabAc', tabAc[:, :], BF16), ('tabAs', tabAs[:, :], BF16), ('cpool', cpool[:], F32)):
                d = dbg(nm, [128, src.shape[1]], dt)
                if d is not None:
                    dma(sy, d, src, 'dO')
            sy.wait_ge(s_dO, CN['dO'])
            return None
        Y(_fin)
        ctx.close()
        return nc, dbg_list

    st_setup.close()
    psP = [pst('psP0', [128, 512], F32), pst('psP1', [128, 512], F32)]
    psS_ = [pst('psS0', [128, 512], F32), pst('psS1', [128, 512], F32)]
    psV_ = [pst('psV0', [128, 512], F32), pst('psV1', [128, 512], F32)]
    psU = pst('psU', [128, 512], F32)
    psT_ = pst('psTt', [128, 1024], BF16)

    def ln_stats(Xsrc, ybf_dst, precopied=False, strip_dw=None):
        if strip_dw is not None:
            for st4 in range(4):
                def _cps(v, st4=st4, dwv=strip_dw[st4]):
                    v.wait_ge(s_dW, dwv)
                    w(v, 'V', 'T')
                    for j in range(st4 * 4, st4 * 4 + 4):
                        v.tensor_copy(ybf_dst.t(j), Xsrc.t(j))
                    return v.drain()
                V(_cps)
                def _mm_sum_s(te, st4=st4):
                    w(te, 'V')
                    last = None
                    for j in range(st4 * 4, st4 * 4 + 4):
                        last = te.matmul(psP[0][0:32, :], onestt[:], ybf_dst.t(j), start=(j == 0), stop=(j == NT - 1))
                    return last
                T(_mm_sum_s)
                for j2 in (st4 * 4, st4 * 4 + 2):
                    def _sq(v, j2=j2):
                        w(v, 'V', 'T')
                        v.tensor_mul(mst[:, 0:512], ybf_dst.t(j2), ybf_dst.t(j2))
                        v.tensor_mul(mst[:, 512:1024], ybf_dst.t(j2 + 1), ybf_dst.t(j2 + 1))
                        return v.drain()
                    V(_sq)
                    def _mm_sq(te, j2=j2):
                        w(te, 'V')
                        te.matmul(psP[1][0:32, :], onestt[:], mst[:, 0:512], start=(j2 == 0), stop=False)
                        return te.matmul(psP[1][0:32, :], onestt[:], mst[:, 512:1024], start=False, stop=(j2 == NT - 2))
                    T(_mm_sq)
        else:
            if not precopied:
                def _cp(v):
                    w(v, 'V', 'T')
                    for j in range(NT):
                        v.tensor_copy(ybf_dst.t(j), Xsrc.t(j))
                    return v.drain()
                V(_cp)
            def _mm_sum(te):
                w(te, 'V')
                last = None
                for j in range(NT):
                    last = te.matmul(psP[0][0:32, :], onestt[:], ybf_dst.t(j), start=(j == 0), stop=(j == NT - 1))
                return last
            T(_mm_sum)
            for j2 in range(0, NT, 2):
                def _sq(v, j2=j2):
                    w(v, 'V', 'T')
                    v.tensor_mul(mst[:, 0:512], ybf_dst.t(j2), ybf_dst.t(j2))
                    v.tensor_mul(mst[:, 512:1024], ybf_dst.t(j2 + 1), ybf_dst.t(j2 + 1))
                    return v.drain()
                V(_sq)
                def _mm_sq(te, j2=j2):
                    w(te, 'V')
                    te.matmul(psP[1][0:32, :], onestt[:], mst[:, 0:512], start=(j2 == 0), stop=False)
                    return te.matmul(psP[1][0:32, :], onestt[:], mst[:, 512:1024], start=False, stop=(j2 == NT - 2))
                T(_mm_sq)
        def _rowm(v):
            w(v, 'T')
            v.tensor_scalar(rowt[0:1, RW_M:RW_M + 512], psP[0][0:1, :], 1.0 / 2048.0, None, ALU.mult)
            v.tensor_scalar(rowt[0:1, RW_T1:RW_T1 + 512], psP[1][0:1, :], 1.0 / 2048.0, None, ALU.mult)
            v.drain()
            v.tensor_mul(rowt[0:1, RW_RSTD:RW_RSTD + 512], rowt[0:1, RW_M:RW_M + 512], rowt[0:1, RW_M:RW_M + 512])
            v.drain()
            v.tensor_sub(rowt[0:1, RW_T1:RW_T1 + 512], rowt[0:1, RW_T1:RW_T1 + 512], rowt[0:1, RW_RSTD:RW_RSTD + 512])
            return v.drain()
        V(_rowm)
        def _rstd(sc):
            w(sc, 'V')
            sc.activation(rowt[0:1, RW_RSTD:RW_RSTD + 512], rowt[0:1, RW_T1:RW_T1 + 512], AF.Ln, bias=cpool[0:1, CPE:CPE + 1])
            sc.drain()
            sc.activation(rowt[0:1, RW_RSTD:RW_RSTD + 512], rowt[0:1, RW_RSTD:RW_RSTD + 512], AF.Exp, scale=cpool[0:1, CPH:CPH + 1])
            return sc.drain()
        Sc(_rstd)

    def ln_apply(Xsrc, out_mat, bb=None):
        if bb is None:
            bb = (psV_[0], psV_[1])
        def _rows(v):
            w(v, 'S', 'T')
            v.scalar_tensor_tensor(rowt[0:1, RW_T1:RW_T1 + 512], rowt[0:1, RW_M:RW_M + 512], -1.0,
                                   rowt[0:1, RW_RSTD:RW_RSTD + 512], ALU.mult, ALU.mult)
            v.drain()
            v.tensor_copy(rows32[0:1, 0:512], rowt[0:1, RW_RSTD:RW_RSTD + 512])
            v.tensor_copy(rows32[0:1, 512:1024], rowt[0:1, RW_T1:RW_T1 + 512])
            return v.drain()
        V(_rows)
        def _b12(te):
            w(te, 'V')
            te.matmul(bb[0][:, :], os32f[:], rows32[:, 0:512], start=True, stop=True)
            return te.matmul(bb[1][:, :], os32f[:], rows32[:, 512:1024], start=True, stop=True)
        T(_b12)
        marks = []
        for j2 in range(0, NT, 2):
            def _apply(v, j2=j2):
                w(v, 'T')
                for j in (j2, j2 + 1):
                    p = (j % 2) * 1024
                    v.tensor_mul(gst.t_[:, p:p + 512], Xsrc.t(j), bb[0][:, :])
                    v.drain()
                    v.tensor_add(out_mat.t(j), gst.t_[:, p:p + 512], bb[1][:, :])
                    v.drain()
                return v.drain()
            V(_apply)
            marks.append(CN['V'])
        return marks

    def stream_weight(sy, src_ap, dst_mat, n_dtiles, cols, col0, sem_key='dG'):
        src = src_ap[0:n_dtiles * 128, col0:col0 + cols]
        src3 = src.rearrange('(n p) c -> p n c', p=128)
        dst3 = dst_mat.t_[:, 0:n_dtiles * cols].rearrange('p (n c) -> p n c', c=cols)
        dma(sy, dst3, src3, sem_key)

    def stream_weight_rt(sy, src_ap, dst_mat, sd0, n, dd0, cols, col0, sem_key='dG'):
        src = src_ap[sd0 * 128:(sd0 + n) * 128, col0:col0 + cols]
        src3 = src.rearrange('(n p) c -> p n c', p=128)
        dst3 = dst_mat.t_[:, dd0 * cols:(dd0 + n) * cols].rearrange('p (n c) -> p n c', c=cols)
        dma(sy, dst3, src3, sem_key)

    def rope_elu(v, pp, dst, b, tt, kind):
        cosb = tabAc[:, (b * 4 + tt) * 128:(b * 4 + tt + 1) * 128]
        sinb = tabAs[:, (b * 4 + tt) * 128:(b * 4 + tt + 1) * 128]
        cos4 = mkap(cosb, 0, [[0, 4], [64, 2], [1, 64]])
        sin4 = mkap(sinb, 0, [[0, 4], [64, 2], [1, 64]])
        swp = mkap(pp, 64, [[128, 4], [-64, 2], [1, 64]])
        pp4 = mkap(pp, 0, [[128, 4], [64, 2], [1, 64]])
        m0 = mkap(mst, 0, [[128, 4], [64, 2], [1, 64]])
        m1 = mkap(mst, 512, [[128, 4], [64, 2], [1, 64]])
        v.tensor_tensor(m0, pp4, cos4, ALU.mult)
        v.tensor_tensor(m1, swp, sin4, ALU.mult)
        v.drain()
        v.tensor_add(mst[:, 0:512], mst[:, 0:512], mst[:, 512:1024])
        v.drain()
        v.tensor_scalar(mst[:, 512:1024], mst[:, 0:512], 0.0, None, ALU.min)
        v.drain()

    def rope_elu_fin(v, dst):
        v.scalar_tensor_tensor(dst, mst[:, 0:512], 0.0, mst[:, 512:1024], ALU.max, ALU.add)
        v.drain()

    def emit_ffn(Xcur, ybuf, ws, first_block, prefused=False):
        BKF = [psS_[0], psS_[1], psP[0], psP[1]]
        dqm = []
        ln_stats(Xcur, ybuf, precopied=prefused)
        if not prefused:
            def _tree(v):
                w(v, 'T', 'S', 'V')
                v.tensor_max(gst.t_[:, 0:512], Xcur.t(0), Xcur.t(1))
                v.tensor_tensor(gst.t_[:, 1024:1536], Xcur.t(0), Xcur.t(1), ALU.min)
                v.drain()
                for j in range(2, NT):
                    v.tensor_max(gst.t_[:, 0:512], gst.t_[:, 0:512], Xcur.t(j))
                    v.tensor_tensor(gst.t_[:, 1024:1536], gst.t_[:, 1024:1536], Xcur.t(j), ALU.min)
                    v.drain()
                return v.drain()
            V(_tree)
        def _trc(te):
            w(te, 'V')
            last = None
            for c4 in range(4):
                last = te.matmul(psS_[0][:, c4 * 128:(c4 + 1) * 128],
                                 gst.t_[:, c4 * 128:(c4 + 1) * 128], identf[:, :],
                                 start=True, stop=True)
                last = te.matmul(psS_[1][:, c4 * 128:(c4 + 1) * 128],
                                 gst.t_[:, c4 * 128 + 1024:(c4 + 1) * 128 + 1024], identf[:, :],
                                 start=True, stop=True)
            return last
        T(_trc)
        def _red(v):
            w(v, 'T')
            for c4 in range(4):
                v.reduce_max(gst.t_[:, 512 + c4:512 + c4 + 1], psS_[0][:, c4 * 128:(c4 + 1) * 128], mybir.AxisListType.X)
                v.tensor_reduce(gst.t_[:, 516 + c4:516 + c4 + 1], psS_[1][:, c4 * 128:(c4 + 1) * 128], mybir.AxisListType.X, ALU.min)
            return v.drain()
        V(_red)
        def _trr(te):
            w(te, 'V')
            last = None
            for c4 in range(4):
                last = te.matmul(psS_[0][0:1, c4 * 128:(c4 + 1) * 128], gst.t_[:, 512 + c4:513 + c4],
                                 identf[:, :], start=True, stop=True)
                last = te.matmul(psS_[1][0:1, c4 * 128:(c4 + 1) * 128], gst.t_[:, 516 + c4:517 + c4],
                                 identf[:, :], start=True, stop=True)
            return last
        T(_trr)
        def _rows_pre(v):
            w(v, 'T', 'S')
            v.tensor_copy(rows32[0:1, 1024:1536], rowt[0:1, RW_M:RW_M + 512])
            v.tensor_copy(rows32[0:1, 512:1024], rowt[0:1, RW_RSTD:RW_RSTD + 512])
            return v.drain()
        V(_rows_pre)
        def _trm(te):
            w(te, 'V')
            last = None
            for c4 in range(4):
                last = te.transpose(psV_[0][:, c4 * 32:(c4 + 1) * 32],
                                    rows32[0:32, 1024 + c4 * 128:1024 + (c4 + 1) * 128], identf[0:32, 0:32])
                last = te.transpose(psV_[1][:, c4 * 32:(c4 + 1) * 32],
                                    rows32[0:32, 512 + c4 * 128:512 + (c4 + 1) * 128], identf[0:32, 0:32])
            return last
        T(_trm)
        def _rows_col(v):
            w(v, 'T')
            mcol = mkap(psV_[0], 0, [[32, 4]])
            rcol = mkap(psV_[1], 0, [[32, 4]])
            tmax = gst.t_[:, 512:516]
            tmin = gst.t_[:, 516:520]
            a1 = gst.t_[:, 520:524]
            a2 = gst.t_[:, 524:528]
            a3 = gst.t_[:, 528:532]
            rqc = gst.t_[:, 532:536]
            inscc = gst.t_[:, 536:540]
            nmc = gst.t_[:, 540:544]
            v.tensor_tensor(a1, tmax, mcol, ALU.subtract)
            v.tensor_tensor(a2, mcol, tmin, ALU.subtract)
            v.drain()
            v.tensor_max(a1, a1, a2)
            v.drain()
            v.reciprocal(a2, a1)
            v.tensor_tensor(a3, a1, rcol, ALU.mult)
            v.drain()
            v.tensor_scalar(rqc, a2, 127.0, None, ALU.mult)
            v.tensor_scalar(inscc, a3, 1e-7, 1.0 / 127.0, ALU.max, ALU.mult)
            v.drain()
            v.tensor_tensor(nmc, rqc, mcol, ALU.mult)
            v.drain()
            v.tensor_scalar(nmc, nmc, -1.0, None, ALU.mult)
            v.drain()
            v.tensor_scalar(nmc, nmc, cpool[:, CPQ:CPQ + 1], None, ALU.add)
            return v.drain()
        V(_rows_col)
        def _trr2(te):
            w(te, 'V')
            last = None
            for c4 in range(4):
                last = te.transpose(psS_[0][0:1, c4 * 128:(c4 + 1) * 128], gst.t_[:, 532 + c4:533 + c4], identf[:, :])
                last = te.transpose(psS_[1][0:1, c4 * 128:(c4 + 1) * 128], gst.t_[:, 540 + c4:541 + c4], identf[:, :])
                last = te.transpose(psV_[0][0:1, c4 * 128:(c4 + 1) * 128], gst.t_[:, 536 + c4:537 + c4], identf[:, :])
            return last
        T(_trr2)
        def _rows_fin(v):
            w(v, 'T')
            v.tensor_copy(rows32[0:1, 0:512], psS_[0][0:1, 0:512])
            v.tensor_copy(rows32[0:1, 512:1024], psS_[1][0:1, 0:512])
            v.tensor_copy(rows32[0:1, 1024:1536], psV_[0][0:1, 0:512])
            return v.drain()
        V(_rows_fin)
        mset('ppqf1', 'V')
        def _bx(te):
            w(te, 'V')
            te.matmul(psP[0][:, :], os32f[:], rows32[:, 0:512], start=True, stop=True)
            te.matmul(psP[1][:, :], os32f[:], rows32[:, 512:1024], start=True, stop=True)
            return te.matmul(psS_[0][:, :], os32f[:], rows32[:, 1024:1536], start=True, stop=True)
        T(_bx)
        def _b3c(v):
            w(v, 'T')
            v.tensor_copy(gst.t_[:, 0:512], psS_[0][:, :])
            return v.drain()
        V(_b3c)
        mset('ppqf0', 'V')
        qbase_s = CN['S']
        vq_marks = []
        for j in range(NT):
            def _qz(v, j=j, qs=qbase_s):
                w(v, 'T')
                if j >= 2:
                    v.wait_ge(SEMH['S'], qs + j - 1)
                p = 1024
                v.tensor_mul(gst.t_[:, p:p + 512], Xcur.t(j), psP[0][:, :])
                v.drain()
                it = itmp if j % 2 == 0 else itmp2
                v.tensor_tensor(it[:, 0:512], gst.t_[:, p:p + 512], psP[1][:, :], ALU.add)
                return v.drain()
            V(_qz)
            vq_marks.append(CN['V'])
        def _qcasts(sc, vq_marks=tuple(vq_marks)):
            for j in range(NT):
                sc.wait_ge(SEMH['V'], vq_marks[j])
                it = itmp if j % 2 == 0 else itmp2
                sc.copy(ybuf.t(j), it[:, 0:512]).then_inc(SEMH['S'], 1)
            return None
        sec('S', _qcasts, None)
        CN['S'] += NT
        mset('q128', 'S')
        mset('ppqf2', 'V')
        mset('ppqf3', 'V')
        for s_ in range(4):
            def _ldf(gp, s_=s_):
                mwait(gp, 'wsb%d' % (s_ % 2))
                stream_weight(gp, wf8T, ws[s_ % 2], NT, 512, s_ * 512)
                return None
            G(_ldf)
            for ot in range(4):
                og = s_ * 4 + ot
                def _mmf(te, s_=s_, ot=ot, og=og, first_block=first_block, qs=qbase_s):
                    w(te, 'dG')
                    mwait(te, 'ppqf%d' % (og % 4))
                    pp = BKF[og % 4]
                    last = None
                    if s_ == 0:
                        for j in range(NT):
                            te.wait_ge(SEMH['S'], qs + j + 1)
                            last = te.matmul(pp[:, :], ws[s_ % 2].tc(j, ot * 128, (ot + 1) * 128),
                                             ybuf.t(j), start=(j == 0), stop=(j == NT - 1))
                    else:
                        mwait(te, 'q128')
                        for j in range(NT):
                            last = te.matmul(pp[:, :], ws[s_ % 2].tc(j, ot * 128, (ot + 1) * 128),
                                             ybuf.t(j), start=(j == 0), stop=(j == NT - 1))
                    if first_block:
                        for j in range(NT):
                            last = te.matmul(psU[:, og:og + 1], ws[s_ % 2].tc(j, ot * 128, (ot + 1) * 128),
                                             onecolb[:], start=(j == 0), stop=(j == NT - 1))
                    return last
                T(_mmf)
                if first_block:
                    def _c1(v, og=og):
                        w(v, 'T')
                        v.tensor_scalar(c1col[:, og:og + 1], psU[:, og:og + 1], -128.0, None, ALU.mult)
                        return v.drain()
                    V(_c1)
                    mset('psU', 'V')
                def _dq(v, og=og):
                    w(v, 'T')
                    pp = BKF[og % 4]
                    v.scalar_tensor_tensor(gst.t_[:, 1024:1536], pp[:, :], c1col[:, og:og + 1], gst.t_[:, 0:512], ALU.add, ALU.mult)
                    v.drain()
                    v.tensor_scalar(f16st[:], gst.t_[:, 1024:1536], cws[:, og:og + 1], cfb[:, og:og + 1], ALU.mult, ALU.add)
                    v.drain()
                    v.tensor_copy(gst.t_[:, 1024:1536], f16st[:])
                    v.drain()
                    v.tensor_add(Xcur.t(og), Xcur.t(og), gst.t_[:, 1024:1536])
                    return v.drain()
                V(_dq)
                dqm.append(CN['V'])
                mset('ppqf%d' % (og % 4), 'V')
                if ot == 3:
                    mset('wsb%d' % (s_ % 2), 'T')
        mset('ppq0', 'V')
        mset('ppq1', 'V')
        return [dqm[st4 * 4 + 3] for st4 in range(4)]


    def emit_stageA_head(b, xdw=None):
        if xdw is None:
            def _ldx(sy, b=b):
                src3 = xT[:, b * BLK:(b + 1) * BLK].rearrange('(n p) c -> p n c', p=128)
                dst3 = Xa.t_[:, :].rearrange('p (n c) -> p n c', c=512)
                dma(sy, dst3, src3, 'dW')
                return None
            Y(_ldx)

            def _wx(v):
                w(v, 'V', 'dW')
                return v.drain()
            V(_wx)

            ln_stats(Xa, ybf)
        else:
            ln_stats(Xa, ybf, strip_dw=xdw)
        aplA = ln_apply(Xa, ybf)
        mset('yA', 'V')

        def eluq(q):
            t = mst if q < 2 else mst2
            return t[:, (q % 2) * 512:(q % 2) * 512 + 512]

        def rope2(v, pp, dst, tt, q):
            cosb = tabAc[:, (b * 4 + tt) * 128:(b * 4 + tt + 1) * 128]
            sinb = tabAs[:, (b * 4 + tt) * 128:(b * 4 + tt + 1) * 128]
            nh4 = dst.shape[1] // 128
            cos4 = mkap(cosb, 0, [[0, nh4], [64, 2], [1, 64]])
            sin4 = mkap(sinb, 0, [[0, nh4], [64, 2], [1, 64]])
            swp = mkap(pp, 64, [[128, nh4], [-64, 2], [1, 64]])
            pp4 = mkap(pp, 0, [[128, nh4], [64, 2], [1, 64]])
            d4 = mkap(dst, 0, [[128, nh4], [64, 2], [1, 64]])
            mh = eluq(q)
            m1 = mkap(mh, 0, [[128, nh4], [64, 2], [1, 64]])
            v.tensor_tensor(d4, pp4, cos4, ALU.mult)
            v.tensor_tensor(m1, swp, sin4, ALU.mult)
            v.drain()
            v.tensor_add(dst, dst, mh)
            v.drain()
            v.tensor_scalar(mh, dst, 0.0, None, ALU.min)
            v.drain()

        for s in range(4):
            def _ldw(gp, s=s):
                mwait(gp, 'wsb%d' % (s % 2))
                stream_weight(gp, wqT, WS[s % 2], NT, 512, s * 512)
                return None
            G(_ldw)
            vmarks = []
            sbase_e = CN['S']
            for tt in range(4):
                def _mmq(te, s=s, tt=tt, se=sbase_e, apl=tuple(aplA)):
                    w(te, 'dG')
                    if s == 0:
                        mwait(te, 'ppq%d' % (tt % 2))
                    else:
                        mwait(te, 'yA', 'ppq%d' % (tt % 2))
                    last = None
                    pp = psP[tt % 2]
                    for j in range(NT):
                        if s == 0:
                            te.wait_ge(SEMH['V'], apl[j // 2])
                        last = te.matmul(pp[:, :], ybf.tc(j, tt * 128, (tt + 1) * 128), WS[s % 2].t(j),
                                         start=(j == 0), stop=(j == NT - 1))
                    return last
                T(_mmq)
                def _rope_q(v, s=s, tt=tt, se=sbase_e):
                    w(v, 'T')
                    rope2(v, psP[tt % 2], qfB.tc(tt, s * 512, (s + 1) * 512), tt, tt)
                    return v.drain()
                V(_rope_q)
                mset('ppq%d' % (tt % 2), 'V')
                vmarks.append(CN['V'])
            def _ees(sc, vmarks=tuple(vmarks)):
                for tt in range(4):
                    sc.wait_ge(SEMH['V'], vmarks[tt])
                    sc.activation(eluq(tt), eluq(tt), AF.Exp).then_inc(SEMH['S'], 1)
                return None
            sec('S', _ees, None)
            CN['S'] += 4
            def _efs(v, s=s, se=sbase_e):
                for tt in range(4):
                    v.wait_ge(SEMH['S'], se + tt + 1)
                    v.scalar_tensor_tensor(qfB.tc(tt, s * 512, (s + 1) * 512),
                                           qfB.tc(tt, s * 512, (s + 1) * 512), 0.0,
                                           eluq(tt), ALU.max, ALU.add)
                    v.drain()
                return v.drain()
            V(_efs)
            mset('wsb%d' % (s % 2), 'T')

        def _ldk(gp):
            mwait(gp, 'wsb0')
            stream_weight(gp, wkT, WS[0], NT, 512, 0)
            return None
        G(_ldk)
        vmarks = []
        sbase_e = CN['S']
        for tt in range(4):
            def _mmk(te, tt=tt, se=sbase_e):
                w(te, 'dG')
                mwait(te, 'yA', 'ppq%d' % (tt % 2))
                last = None
                pp = psP[tt % 2]
                for j in range(NT):
                    last = te.matmul(pp[:, :], ybf.tc(j, tt * 128, (tt + 1) * 128), WS[0].t(j),
                                     start=(j == 0), stop=(j == NT - 1))
                return last
            T(_mmk)
            def _rope_k(v, tt=tt, se=sbase_e):
                w(v, 'T')
                rope2(v, psP[tt % 2], kfB.t(tt), tt, tt)
                return v.drain()
            V(_rope_k)
            mset('ppq%d' % (tt % 2), 'V')
            vmarks.append(CN['V'])
        def _eeks(sc, vmarks=tuple(vmarks)):
            for tt in range(4):
                sc.wait_ge(SEMH['V'], vmarks[tt])
                sc.activation(eluq(tt), eluq(tt), AF.Exp).then_inc(SEMH['S'], 1)
            return None
        sec('S', _eeks, None)
        CN['S'] += 4
        def _efks(v, se=sbase_e):
            for tt in range(4):
                v.wait_ge(SEMH['S'], se + tt + 1)
                v.scalar_tensor_tensor(kfB.t(tt), kfB.t(tt), 0.0,
                                       eluq(tt), ALU.max, ALU.add)
                v.drain()
            return v.drain()
        V(_efks)

        mset('wsb0', 'T')
        def _ldv(gp):
            mwait(gp, 'wsb1')
            stream_weight(gp, wvT, WS[1], NT, 512, 0)
            return None
        G(_ldv)
        for tt in range(4):
            def _mmv(te, tt=tt):
                w(te, 'dG')
                mwait(te, 'yA', 'ppq%d' % (tt % 2))
                last = None
                pp = psP[tt % 2]
                for j in range(NT):
                    last = te.matmul(pp[:, :], ybf.tc(j, tt * 128, (tt + 1) * 128), WS[1].t(j),
                                     start=(j == 0), stop=(j == NT - 1))
                return last
            T(_mmv)
            def _ev_v(v, tt=tt):
                w(v, 'T')
                pp = psP[tt % 2]
                src = pp[:, :].rearrange('p (h c) -> p h c', c=128)
                dstv = mkap(vaug.t_, tt * 516, [[129, 4], [1, 128]])
                v.tensor_copy(dstv, src)
                return v.drain()
            V(_ev_v)
            mset('ppq%d' % (tt % 2), 'V')

        mset('wsb1', 'T')
        for h2 in range(0, NH, 2):
            def _tr_q(te, h2=h2):
                w(te, 'V')
                last = None
                for hh in range(2):
                    for tt in range(4):
                        last = te.transpose(psT_[:, hh * 512 + tt * 128:hh * 512 + (tt + 1) * 128],
                                            qfB.tc(tt, (h2 + hh) * 128, (h2 + hh + 1) * 128), ident[:])
                return last
            T(_tr_q)
            def _ev_q(v, h2=h2):
                w(v, 'T')
                v.tensor_copy(qfA.t(h2), psT_[:, 0:512])
                v.tensor_copy(qfA.t(h2 + 1), psT_[:, 512:1024])
                return v.drain()
            V(_ev_q)
        for h2 in range(0, KVH, 2):
            def _tr_k(te, h2=h2):
                w(te, 'V')
                last = None
                for hh in range(2):
                    for tt in range(4):
                        last = te.transpose(psT_[:, hh * 512 + tt * 128:hh * 512 + (tt + 1) * 128],
                                            kfB.tc(tt, (h2 + hh) * 128, (h2 + hh + 1) * 128), ident[:])
                return last
            T(_tr_k)
            def _ev_k(v, h2=h2):
                w(v, 'T')
                v.tensor_copy(kfA.t(h2), psT_[:, 0:512])
                v.tensor_copy(kfA.t(h2 + 1), psT_[:, 512:1024])
                return v.drain()
            V(_ev_k)
        mset('qkvA', 'V')



    scdiv = [0, 0, 0, 0]
    scS_pred = [0, 0]
    def emit_scan(bglob):
        AVB = [psV_[0], psV_[1], psP[0], psP[1]]
        for cc in range(4):
            sdbase = CN['S']
            tbase = CN['T']
            vbase = CN['V']
            vupd = CN['V']
            sdw = [(sdbase + hg - 3) if hg >= 4 else scdiv[hg % 4] for hg in range(8)]
            scw = list(scS_pred)
            def _scanT(te, cc=cc, tb=tbase, vb=vbase, sdw=tuple(sdw), scw=tuple(scw)):
                mwait(te, 'qkvA')
                for hg in range(8):
                    h0 = hg * 2
                    kv0, kv1 = h0 // 4, (h0 + 1) // 4
                    sw = (vb + 2 * (hg - 2) + 1) if hg >= 2 else scw[hg % 2]
                    if sw > 0:
                        te.wait_ge(SEMH['V'], sw)
                    pp = psS_[hg % 2]
                    te.matmul(pp[:, 0:128], kfA.tc(kv0, cc * 128, (cc + 1) * 128),
                              qfA.tc(h0, cc * 128, (cc + 1) * 128), start=True, stop=True)
                    te.matmul(pp[:, 128:256], kfA.tc(kv1, cc * 128, (cc + 1) * 128),
                              qfA.tc(h0 + 1, cc * 128, (cc + 1) * 128), start=True,
                              stop=True).then_inc(SEMH['T'], 1)
                    te.wait_ge(SEMH['V'], vb + 2 * hg + 1)
                    if sdw[hg] > 0:
                        te.wait_ge(SEMH['S'], sdw[hg])
                    ppa = AVB[hg % 4]
                    ms0 = mst[:, (hg % 2) * 512:(hg % 2) * 512 + 128]
                    ms1 = mst[:, (hg % 2) * 512 + 128:(hg % 2) * 512 + 256]
                    te.matmul(ppa[:, 0:129], ms0,
                              vaug.t_[:, cc * 516 + kv0 * 129: cc * 516 + kv0 * 129 + 129],
                              start=True, stop=False)
                    te.matmul(ppa[:, 0:129], qfA.tc(h0, cc * 128, (cc + 1) * 128),
                              stbf[:, kv0 * 129:(kv0 + 1) * 129], start=False, stop=True)
                    te.matmul(ppa[:, 256:385], ms1,
                              vaug.t_[:, cc * 516 + kv1 * 129: cc * 516 + kv1 * 129 + 129],
                              start=True, stop=False)
                    te.matmul(ppa[:, 256:385], qfA.tc(h0 + 1, cc * 128, (cc + 1) * 128),
                              stbf[:, kv1 * 129:(kv1 + 1) * 129], start=False,
                              stop=True).then_inc(SEMH['T'], 1)
                return None
            sec('T', _scanT, None)
            CN['T'] += 16
            for hg in range(8):
                def _mask(v, hg=hg, tb=tbase):
                    v.wait_ge(SEMH['T'], tb + 2 * hg + 1)
                    pp = psS_[hg % 2]
                    src = mkap(pp, 0, [[128, 2], [1, 128]])
                    mk = mkap(maskT, 0, [[0, 2], [1, 128]])
                    dst = mkap(mst, (hg % 2) * 512, [[128, 2], [1, 128]])
                    v.tensor_tensor(dst, src, mk, ALU.mult)
                    return v.drain()
                V(_mask)
                def _recip(v, cc=cc, hg=hg, tb=tbase):
                    v.wait_ge(SEMH['T'], tb + 2 * hg + 2)
                    pp = AVB[hg % 4]
                    dn = mkap(pp, 128, [[256, 2], [1, 1]])
                    v.tensor_scalar(gst.t_[:, 1536 + hg * 4:1538 + hg * 4], dn, EPS_A, None, ALU.add)
                    v.drain()
                    v.reciprocal(gst.t_[:, 1538 + hg * 4:1540 + hg * 4], gst.t_[:, 1536 + hg * 4:1538 + hg * 4])
                    return v.drain()
                V(_recip)
            for hg in range(8):
                scS_pred[hg % 2] = vbase + 2 * hg + 1
            def _sdivs(sc, cc=cc, vb=vbase):
                for hg in range(8):
                    h0 = hg * 2
                    sc.wait_ge(SEMH['V'], vb + 2 * hg + 2)
                    pp = AVB[hg % 4]
                    for i in range(2):
                        ins = sc.activation(qfB.tc(cc, (h0 + i) * 128, (h0 + i + 1) * 128),
                                            pp[:, i * 256:i * 256 + 128], AF.Identity,
                                            scale=gst.t_[:, 1538 + hg * 4 + i:1539 + hg * 4 + i])
                        if i == 1:
                            ins.then_inc(SEMH['S'], 1)
                return None
            sec('S', _sdivs, None)
            CN['S'] += 8
            for hg in range(8):
                scdiv[hg % 4] = sdbase + hg + 1
            for rnd in range(2):
                def _mmu(te, cc=cc, rnd=rnd):
                    mwait(te, 'psU')
                    kv0, kv1 = rnd * 2, rnd * 2 + 1
                    te.matmul(psU[:, 0:129], kfB.tc(cc, kv0 * 128, (kv0 + 1) * 128),
                              vaug.t_[:, cc * 516 + kv0 * 129: cc * 516 + kv0 * 129 + 129],
                              start=True, stop=True)
                    return te.matmul(psU[:, 256:385], kfB.tc(cc, kv1 * 128, (kv1 + 1) * 128),
                                     vaug.t_[:, cc * 516 + kv1 * 129: cc * 516 + kv1 * 129 + 129],
                                     start=True, stop=True)
                T(_mmu)
                def _upd(v, rnd=rnd):
                    w(v, 'T')
                    kv0, kv1 = rnd * 2, rnd * 2 + 1
                    v.tensor_add(stLa[:, kv0 * 129:(kv0 + 1) * 129], stLa[:, kv0 * 129:(kv0 + 1) * 129], psU[:, 0:129])
                    v.tensor_add(stLa[:, kv1 * 129:(kv1 + 1) * 129], stLa[:, kv1 * 129:(kv1 + 1) * 129], psU[:, 256:385])
                    v.drain()
                    v.tensor_copy(stbf[:, kv0 * 129:(kv0 + 1) * 129], stLa[:, kv0 * 129:(kv0 + 1) * 129])
                    v.tensor_copy(stbf[:, kv1 * 129:(kv1 + 1) * 129], stLa[:, kv1 * 129:(kv1 + 1) * 129])
                    return v.drain()
                V(_upd)
                mset('psU', 'V')

    def emit_out_wo(bglob):
        BKO = [psP[0], psP[1], psS_[0], psS_[1]]
        for h2 in range(0, NH, 2):
            def _tr_o(te, h2=h2):
                w(te, 'V', 'S')
                last = None
                for hh in range(2):
                    for tt in range(4):
                        last = te.transpose(psT_[:, hh * 512 + tt * 128:hh * 512 + (tt + 1) * 128],
                                            qfB.tc(tt, (h2 + hh) * 128, (h2 + hh + 1) * 128), ident[:])
                return last
            T(_tr_o)
            def _ev_o(v, h2=h2):
                w(v, 'T')
                v.tensor_copy(qfA.t(h2), psT_[:, 0:512])
                v.tensor_copy(qfA.t(h2 + 1), psT_[:, 512:1024])
                return v.drain()
            V(_ev_o)
        mset('outA', 'V')
        for s_ in range(4):
            def _ldwo(gp, s_=s_):
                mwait(gp, 'wsb%d' % (s_ % 2))
                stream_weight(gp, woaT, WS[s_ % 2], NT, 512, s_ * 512)
                return None
            G(_ldwo)
            for dt_ in range(4):
                dmt = s_ * 4 + dt_
                def _mmo(te, s_=s_, dt_=dt_, dmt=dmt):
                    w(te, 'dG')
                    if dmt < 2:
                        mwait(te, 'outA', 'ppq%d' % (dmt % 2))
                    else:
                        mwait(te, 'outA', 'ppqo%d' % (dmt % 4))
                    pp = BKO[dmt % 4]
                    last = None
                    for j in range(NT):
                        last = te.matmul(pp[:, :], WS[s_ % 2].tc(j, dt_ * 128, (dt_ + 1) * 128),
                                         qfA.t(j), start=(j == 0), stop=(j == NT - 1))
                    return last
                T(_mmo)
                def _res(v, dmt=dmt):
                    w(v, 'T')
                    v.tensor_add(Xa.t(dmt), Xa.t(dmt), BKO[dmt % 4][:, :])
                    v.drain()
                    v.tensor_copy(ybf.t(dmt), Xa.t(dmt))
                    if dmt == 0:
                        v.tensor_copy(gst.t_[:, 0:512], Xa.t(0))
                        v.tensor_copy(gst.t_[:, 1024:1536], Xa.t(0))
                    else:
                        v.tensor_max(gst.t_[:, 0:512], gst.t_[:, 0:512], Xa.t(dmt))
                        v.tensor_tensor(gst.t_[:, 1024:1536], gst.t_[:, 1024:1536], Xa.t(dmt), ALU.min)
                    return v.drain()
                V(_res)
                mset('ppqo%d' % (dmt % 4), 'V')
            mset('wsb%d' % (s_ % 2), 'T')


    def emit_stageA(milestone):
        xdw = None
        for b in range(NB):
            emit_stageA_head(b, xdw)
            if milestone == 'proj' and b == 0:
                return 'stop'
            emit_scan(b)
            mset('ppq0', 'S')
            mset('ppq1', 'S')
            mset('ppqo2', 'V')
            mset('ppqo3', 'V')
            emit_out_wo(b)
            if milestone == 'attnA' and b == 0:
                return 'stop'
            dqm = emit_ffn(Xa, ybf, WS, first_block=(b == 0), prefused=True)
            if milestone == 'ffn1' and b == 0:
                return 'stop'
            xdw = [] if b + 1 < NB else None
            for st4 in range(4):
                def _stx(sy, b=b, st4=st4, vw=dqm[st4]):
                    sy.wait_ge(s_V, vw)
                    src3 = Xa.t_[:, st4 * 2048:(st4 + 1) * 2048].rearrange('p (n c) -> p n c', c=512)
                    dst3 = x2s[st4 * 512:(st4 + 1) * 512, b * BLK:(b + 1) * BLK].rearrange('(n p) c -> p n c', p=128)
                    dma(sy, dst3, src3, 'dO')
                    return None
                Y(_stx)
                if b + 1 < NB:
                    def _ldxs(sy, b=b, st4=st4):
                        src3 = xT[st4 * 512:(st4 + 1) * 512, (b + 1) * BLK:(b + 2) * BLK].rearrange('(n p) c -> p n c', p=128)
                        dst3 = Xa.t_[:, st4 * 2048:(st4 + 1) * 2048].rearrange('p (n c) -> p n c', c=512)
                        dma(sy, dst3, src3, 'dW')
                        return None
                    Y(_ldxs)
                    xdw.append(CN['dW'])
        return None

    r = emit_stageA(stop_after)
    if r == 'stop':
        if stop_after == 'proj':
            def _fin(sy):
                w(sy, 'S', 'V', 'T')
                for nm, src, sh, dt in (('qfA', qfA.t_[:, :], [128, NT * 512], BF16),
                                    ('kfA', kfA.t_[:, :], [128, 4 * 512], BF16),
                                    ('vaug', vaug.t_[:, :], [128, 4 * 516], BF16),
                                    ('ybf', ybf.t_[:, :], [128, NT * 512], BF16),
                                    ('m', rowt[0:1, 0:512], [1, 512], F32),
                                    ('rstd', rowt[0:1, 512:1024], [1, 512], F32)):
                    d = dbg(nm, sh, dt)
                    if d is not None:
                        dma(sy, d, src, 'dO')
                sy.wait_ge(s_dO, CN['dO'])
                return None
            Y(_fin)
            ctx.close()
            return nc, dbg_list


        if stop_after == 'attnA':
            def _fin(sy):
                w(sy, 'S', 'V', 'T')
                d = dbg('X1', [128, NT * 512], F32)
                if d is not None:
                    dma(sy, d, Xa.t_[:, :], 'dO')
                sy.wait_ge(s_dO, CN['dO'])
                return None
            Y(_fin)
            ctx.close()
            return nc, dbg_list


        if stop_after == 'ffn1':
            def _fin(sy):
                w(sy, 'S', 'V', 'T')
                d = dbg('X2', [128, NT * 512], F32)
                if d is not None:
                    dma(sy, d, Xa.t_[:, :], 'dO')
                sy.wait_ge(s_dO, CN['dO'])
                return None
            Y(_fin)
            ctx.close()
            return nc, dbg_list


    if stop_after == 'stageA':
        def _fin(sy):
            w(sy, 'S', 'V', 'T')
            sy.wait_ge(SEMH['dO'], CN['dO'])
            d = dbg('X2full', [D, S_], F32)
            if d is not None:
                dma(sy, d, x2s[:, :], 'dO')
            sy.wait_ge(s_dO, CN['dO'])
            return None
        Y(_fin)
        ctx.close()
        return nc, dbg_list

    stA_scope.close()
    stAB = ctx.enter_context(contextlib.ExitStack())
    latq = Mat(sbt('latq', [128, 4 * 2048], BF16, stAB), 4, 2048)
    vaugB = Mat(sbt('vaugB', [128, 16 * 2064], BF16, stAB), 16, 2064)

    stA2 = ctx.enter_context(contextlib.ExitStack())
    Xa2 = Mat(sbt('Xa2', [128, NT * 512], F32, stA2), NT, 512)
    y3 = Mat(sbt('y3', [128, NT * 512], BF16, stA2), NT, 512)
    kvlat = Mat(sbt('kvlat', [128, 4 * 512], BF16, stA2), 4, 512)
    tabBc = sbt('tabBc', [128, 4 * 512], BF16, stA2)
    tabBs = sbt('tabBs', [128, 4 * 512], BF16, stA2)
    wsA2 = Mat(wsA2t, NT, 512)
    ws_kn8 = Mat(wsA2t, 8, 1024)
    ws_vu = Mat(wsA2t, 4, 2048)
    BK4 = [psP[0], psP[1], psS_[0], psS_[1]]

    def _a2init(v):
        w(v, 'V', 'T', 'S', 'dO', 'dW', 'dG')
        v.memset(mkap(vaugB.t_, 128, [[2064, 16], [129, 16], [1, 1]]), 1.0)
        return v.drain()
    V(_a2init)
    for b in range(NB):
        def _ldtb(sy, b=b):
            if b > 0:
                mwait(sy, 'ekrdone')
            dma(sy, tabBc[:, :], tabBcin[:, b * 2048:(b + 1) * 2048], 'dW')
            dma(sy, tabBs[:, :], tabBsin[:, b * 2048:(b + 1) * 2048], 'dW')
            return None
        Y(_ldtb)
        if b == 0:
            def _ldx2(sy, b=b):
                w(sy, 'V', 'dO')
                src3 = x2s[:, b * BLK:(b + 1) * BLK].rearrange('(n p) c -> p n c', p=128)
                dst3 = Xa2.t_[:, :].rearrange('p (n c) -> p n c', c=512)
                dma(sy, dst3, src3, 'dW')
                return None
            Y(_ldx2)
        def _wx2(v):
            w(v, 'V', 'dW')
            return v.drain()
        V(_wx2)
        ln_stats(Xa2, y3)
        vmst_a2 = CN['V']
        aplA2 = ln_apply(Xa2, y3)
        mset('yA', 'V')
        if b < NB - 1:
            def _ldx2n(sy, b=b):
                mwait(sy, 'yA')
                src3 = x2s[:, (b + 1) * BLK:(b + 2) * BLK].rearrange('(n p) c -> p n c', p=128)
                dst3 = Xa2.t_[:, :].rearrange('p (n c) -> p n c', c=512)
                dma(sy, dst3, src3, 'dW')
                return None
            Y(_ldx2n)
        for wsrc, bias, dst in ((wqdT, cbqd, 'q'), (wkvdT, cbkvd, 'kv')):
            def _ldd0(gp, wsrc=wsrc):
                mwait(gp, 'a2h0')
                stream_weight_rt(gp, wsrc, wsA2, 0, 8, 0, 512, 0)
                return None
            G(_ldd0)
            dg0 = CN['dG']
            def _ldd1(gp, wsrc=wsrc):
                mwait(gp, 'a2h1')
                stream_weight_rt(gp, wsrc, wsA2, 8, 8, 8, 512, 0)
                return None
            G(_ldd1)
            dg1 = CN['dG']
            for lt in range(4):
                def _mmd0(te, lt=lt, dg0=dg0, first=(dst == 'q'), apl=tuple(aplA2)):
                    te.wait_ge(SEMH['dG'], dg0)
                    if first:
                        mwait(te, 'ppq%d' % lt)
                    else:
                        mwait(te, 'yA', 'ppq%d' % lt)
                    pp = BK4[lt]
                    last = None
                    for j in range(8):
                        if first:
                            te.wait_ge(SEMH['V'], apl[j // 2])
                        last = te.matmul(pp[:, :], wsA2.tc(j, lt * 128, (lt + 1) * 128), y3.t(j),
                                         start=(j == 0), stop=False)
                    return last
                T(_mmd0)
            mset('a2h0', 'T')
            for lt in range(4):
                def _mmd1(te, lt=lt, dg1=dg1, first=(dst == 'q'), apl=tuple(aplA2)):
                    te.wait_ge(SEMH['dG'], dg1)
                    pp = BK4[lt]
                    last = None
                    for j in range(8, NT):
                        if first:
                            te.wait_ge(SEMH['V'], apl[j // 2])
                        last = te.matmul(pp[:, :], wsA2.tc(j, lt * 128, (lt + 1) * 128), y3.t(j),
                                         start=False, stop=(j == NT - 1))
                    return last
                T(_mmd1)
                def _evd(v, lt=lt, bias=bias, dst=dst, b=b):
                    w(v, 'T')
                    pp = BK4[lt]
                    if dst == 'q':
                        v.tensor_scalar(latq.tc(lt, b * 512, (b + 1) * 512), pp[:, :], bias[:, lt:lt + 1], None, ALU.add)
                    else:
                        v.tensor_scalar(kvlat.t(lt), pp[:, :], bias[:, lt:lt + 1], None, ALU.add)
                    return v.drain()
                V(_evd)
                mset('ppq%d' % lt, 'V')
            mset('a2h1', 'T')
        mset('kvlat', 'V')
        def _ldkn(gp):
            mwait(gp, 'a2h0')
            stream_weight_rt(gp, wknT, ws_kn8, 0, 4, 0, 1024, 0)
            return None
        G(_ldkn)
        dgm_kn = CN['dG']
        sbase_kn = CN['S']
        tpred_kn = []
        for ot in range(8):
            def _mkn(te, ot=ot, dgm=dgm_kn, sb=sbase_kn):
                te.wait_ge(SEMH['dG'], dgm)
                if ot >= 4:
                    te.wait_ge(SEMH['S'], sb + ot - 3)
                    mwait(te, 'kvlat')
                else:
                    mwait(te, 'kvlat', 'ppq%d' % (ot % 4))
                pp = BK4[ot % 4]
                last = None
                for lt in range(4):
                    last = te.matmul(pp[:, :], ws_kn8.tc(lt, ot * 128, (ot + 1) * 128), kvlat.t(lt),
                                     start=(lt == 0), stop=(lt == 3))
                return last
            T(_mkn)
            tpred_kn.append(CN['T'])
            if ot == 7:
                mset('a2h0', 'T')
        dO0_kn = CN['dO']
        def _ekns(sc, tpred=tuple(tpred_kn), sb=sbase_kn, dO0=dO0_kn, vm=vmst_a2):
            sc.wait_ge(SEMH['V'], vm)
            sc.wait_ge(s_dO, dO0)
            for ot in range(8):
                sc.wait_ge(SEMH['T'], tpred[ot])
                if ot >= 2:
                    sc.wait_ge(s_dO, dO0 + (ot - 1) * 16)
                sc.activation(mst[:, (ot % 2) * 512:(ot % 2) * 512 + 512], BK4[ot % 4][:, :], AF.Identity,
                              bias=cbkn[:, ot:ot + 1]).then_inc(SEMH['S'], 1)
            return None
        sec('S', _ekns, None)
        CN['S'] += 8
        for ot in range(4, 8):
            MK['ppq%d' % (ot % 4)] = ('S', sbase_kn + ot + 1)
        for ot in range(8):
            def _skn(sy, ot=ot, b=b, sb=sbase_kn):
                sy.wait_ge(SEMH['S'], sb + ot + 1)
                dma(sy, kAs[ot * 128:(ot + 1) * 128, b * BLK:(b + 1) * BLK], mst[:, (ot % 2) * 512:(ot % 2) * 512 + 512], 'dO')
                return None
            Y(_skn)
        def _ldkr(gp):
            mwait(gp, 'a2h1')
            stream_weight_rt(gp, wkrT, ws_kn8, 0, 4, 4, 1024, 0)
            return None
        G(_ldkr)
        dgm_kr = CN['dG']
        for jt in range(4):
            def _mkr(te, jt=jt, dgm=dgm_kr):
                te.wait_ge(SEMH['dG'], dgm)
                mwait(te, 'kvlat', 'ppq0', 'ppq1')
                last = None
                for lt in range(4):
                    last = te.matmul(psP[0][:, :], ws_kn8.tc(4 + lt, jt * 128, (jt + 1) * 128), kvlat.t(lt),
                                     start=(lt == 0), stop=(lt == 3))
                for lt in range(4):
                    last = te.matmul(psP[1][:, :], ws_kn8.tc(4 + lt, (jt + 4) * 128, (jt + 5) * 128), kvlat.t(lt),
                                     start=(lt == 0), stop=(lt == 3))
                return last
            T(_mkr)
            if jt == 3:
                mset('a2h1', 'T')
            def _ekr(v, jt=jt):
                w(v, 'T', 'dO', 'S', 'dW')
                v.tensor_scalar(gst.t_[:, 0:512], psP[0][:, :], cbkr[:, jt:jt + 1], None, ALU.add)
                v.tensor_scalar(gst.t_[:, 512:1024], psP[1][:, :], cbkr[:, jt + 4:jt + 5], None, ALU.add)
                v.drain()
                v.tensor_mul(gst.t_[:, 1024:1536], gst.t_[:, 0:512], tabBc[:, jt * 512:(jt + 1) * 512])
                v.tensor_mul(gst.t_[:, 1536:2048], gst.t_[:, 512:1024], tabBs[:, jt * 512:(jt + 1) * 512])
                v.drain()
                v.tensor_sub(mst[:, 0:512], gst.t_[:, 1024:1536], gst.t_[:, 1536:2048])
                v.drain()
                v.tensor_mul(gst.t_[:, 1024:1536], gst.t_[:, 512:1024], tabBc[:, jt * 512:(jt + 1) * 512])
                v.tensor_mul(gst.t_[:, 1536:2048], gst.t_[:, 0:512], tabBs[:, jt * 512:(jt + 1) * 512])
                v.drain()
                v.tensor_add(mst[:, 512:1024], gst.t_[:, 1024:1536], gst.t_[:, 1536:2048])
                return v.drain()
            V(_ekr)
            mset('ppq0', 'V')
            mset('ppq1', 'V')
            def _skr(sy, jt=jt, b=b):
                w(sy, 'V')
                dma(sy, kAs[(8 + jt) * 128:(9 + jt) * 128, b * BLK:(b + 1) * BLK], mst[:, 0:512], 'dO')
                dma(sy, kAs[(12 + jt) * 128:(13 + jt) * 128, b * BLK:(b + 1) * BLK], mst[:, 512:1024], 'dO')
                return None
            Y(_skr)
            if jt == 3:
                mset('ekrdone', 'V')
        def _ldvu0(gp):
            mwait(gp, 'a2h0')
            stream_weight_rt(gp, wvuT, ws_vu, 0, 2, 0, 2048, 0)
            return None
        G(_ldvu0)
        dgv0 = CN['dG']
        def _ldvu1(gp):
            mwait(gp, 'a2h1')
            stream_weight_rt(gp, wvuT, ws_vu, 2, 2, 2, 2048, 0)
            return None
        G(_ldvu1)
        dgv1 = CN['dG']
        sbase_vu = CN['S']
        tpred_vu = []
        for tt in range(4):
            for sv in range(4):
                bank = (tt * 4 + sv) % 4
                idx = tt * 4 + sv
                def _mvu0(te, tt=tt, sv=sv, bank=bank, idx=idx, dgv0=dgv0, sb=sbase_vu):
                    te.wait_ge(SEMH['dG'], dgv0)
                    if idx >= 4:
                        te.wait_ge(SEMH['S'], sb + idx - 3)
                        mwait(te, 'kvlat')
                    else:
                        mwait(te, 'kvlat', 'ppq%d' % bank)
                    pp = BK4[bank]
                    last = None
                    for lt in range(2):
                        last = te.matmul(pp[:, :], kvlat.tc(lt, tt * 128, (tt + 1) * 128),
                                         ws_vu.tc(lt, sv * 512, (sv + 1) * 512), start=(lt == 0), stop=False)
                    return last
                T(_mvu0)
                if tt == 3 and sv == 3:
                    mset('a2h0', 'T')
                def _mvu1(te, tt=tt, sv=sv, bank=bank, dgv1=dgv1):
                    te.wait_ge(SEMH['dG'], dgv1)
                    pp = BK4[bank]
                    for lt in range(2, 4):
                        te.matmul(pp[:, :], kvlat.tc(lt, tt * 128, (tt + 1) * 128),
                                  ws_vu.tc(lt, sv * 512, (sv + 1) * 512), start=False, stop=False)
                    return te.matmul(pp[:, :], os32b[:], bvu32[:, sv * 512:(sv + 1) * 512], start=False, stop=True)
                T(_mvu1)
                tpred_vu.append(CN['T'])
        mset('a2h1', 'T')
        def _evus(sc, tpred=tuple(tpred_vu), sb=sbase_vu, b=b, vm=vmst_a2):
            sc.wait_ge(SEMH['V'], vm)
            for tt in range(4):
                for sv in range(4):
                    idx = tt * 4 + sv
                    bank = idx % 4
                    sc.wait_ge(SEMH['T'], tpred[idx])
                    pp = BK4[bank]
                    src = pp[:, :].rearrange('p (h c) -> p h c', c=128)
                    dstv = mkap(vaugB.t_, (b * 4 + tt) * 2064 + 4 * sv * 129, [[129, 4], [1, 128]])
                    sc.copy(dstv, src).then_inc(SEMH['S'], 1)
            return None
        sec('S', _evus, None)
        CN['S'] += 16
        for idx in range(12, 16):
            MK['ppq%d' % (idx % 4)] = ('S', sbase_vu + idx + 1)

    stA2.close()

    if stop_after == 'a2':
        def _fin(sy):
            w(sy, 'S', 'V', 'T')
            sy.wait_ge(SEMH['dO'], CN['dO'])
            for nm, src, sh, dt in (('kAfull', kAs[:, :], [D, S_], BF16),
                                     ('latq', latq.t_[:, :], [128, 4 * 2048], BF16),
                                     ('vaugB', vaugB.t_[:, :], [128, 16 * 2064], BF16)):
                d = dbg(nm, sh, dt)
                if d is not None:
                    dma(sy, d, src, 'dO')
            sy.wait_ge(s_dO, CN['dO'])
            return None
        Y(_fin)
        ctx.close()
        return nc, dbg_list

    stB = ctx.enter_context(contextlib.ExitStack())
    wqut = sbt('wqut', [128, 4 * 2048], BF16, stB)
    wqu = Mat(wqut, 4, 2048)
    kAh = sbt('kAh', [128, 2048], BF16, stB)
    kAh2 = sbt('kAh2', [128, 2048], BF16, stB)
    KAHB = [kAh, kAh2]
    qAh = sbt('qAh', [128, 2048], BF16, stB)
    expb = sbt('expb', [128, 16 * 512], BF16, stB)
    aostg = sbt('aostg', [128, 512], BF16, stB)

    def _ldqu(gp):
        w(gp, 'T', 'V', 'S', 'dO')
        stream_weight(gp, wquT, wqu, 4, 2048, 0)
        return None
    G(_ldqu)

    SCB = [psP[0], psP[1], psS_[0], psS_[1]]
    scbmark = [0, 0, 0, 0]
    dw_kh = [0, 0]

    def emit_ldkh(h):
        def _ldkh(sy, h=h):
            mwait(sy, 'kAh%d' % (h % 2))
            w(sy, 'T', 'V', 'S', 'dO') if h == 0 else None
            dma(sy, KAHB[h % 2][:, :], kAs[h * 128:(h + 1) * 128, :], 'dW')
            return None
        Y(_ldkh)
        dw_kh[h % 2] = CN['dW']

    emit_ldkh(0)
    for h in range(NH):
        kbuf = KAHB[h % 2]
        dwm_kh = dw_kh[h % 2]
        for qb in range(4):
            def _mqu(te, h=h, qb=qb):
                w(te, 'dG')
                mwait(te, 'psVav', 'psVq%d' % (qb % 2))
                if h == 0 and qb == 0:
                    w(te, 'V', 'S')
                pp = psV_[qb % 2]
                last = None
                for lt in range(4):
                    last = te.matmul(pp[:, :], wqu.tc(lt, h * 128, (h + 1) * 128),
                                     latq.tc(lt, qb * 512, (qb + 1) * 512), start=(lt == 0), stop=(lt == 3))
                return last
            T(_mqu)
            def _equ(v, h=h, qb=qb):
                w(v, 'T')
                v.tensor_scalar(qAh[:, qb * 512:(qb + 1) * 512], psV_[qb % 2][:, :], cbqu[:, h:h + 1], None, ALU.add)
                return v.drain()
            V(_equ)
            mset('psVq%d' % (qb % 2), 'V')
        mset('qAh', 'V')
        if h + 1 < NH:
            emit_ldkh(h + 1)
        for qb in range(4):
            sbase = CN['S']
            tbase = CN['T']
            smw = [(sbase + kt - 3) if kt >= 4 else scbmark[kt % 4] for kt in range(16)]
            def _mscs(te, h=h, qb=qb, smw=tuple(smw), kbuf=kbuf, dwm_kh=dwm_kh):
                te.wait_ge(SEMH['dW'], dwm_kh)
                mwait(te, 'qAh')
                for kt in range(16):
                    if smw[kt] > 0:
                        te.wait_ge(SEMH['S'], smw[kt])
                    te.matmul(SCB[kt % 4][:, :], kbuf[:, kt * 128:(kt + 1) * 128],
                              qAh[:, qb * 512:(qb + 1) * 512], start=True,
                              stop=True).then_inc(SEMH['T'], 1)
                return None
            sec('T', _mscs, None)
            CN['T'] += 16
            def _exps(sc, h=h, qb=qb, tbase=tbase):
                for kt in range(16):
                    sc.wait_ge(SEMH['T'], tbase + kt + 1)
                    sc.activation(expb[:, kt * 512:(kt + 1) * 512], SCB[kt % 4][:, :], AF.Exp,
                                  scale=cpool[:, CPS:CPS + 1]).then_inc(SEMH['S'], 1)
                return None
            sec('S', _exps, None)
            CN['S'] += 16
            for kt in range(16):
                scbmark[kt % 4] = sbase + kt + 1
            if qb == 3:
                mset('kAh%d' % (h % 2), 'T')
            savb = CN['S']
            def _mavs(te, h=h, qb=qb, savb=savb):
                mwait(te, 'psVav', 'qAh')
                for qt in range(4):
                    pp = psV_[qt // 2]
                    c0 = (qt % 2) * 256
                    for kt in range(16):
                        if qt == 0 or kt == 15:
                            te.wait_ge(SEMH['S'], savb - 16 + kt + 1)
                        last = te.matmul(pp[:, c0:c0 + 129], expb[:, kt * 512 + qt * 128: kt * 512 + (qt + 1) * 128],
                                         vaugB.t_[:, kt * 2064 + h * 129: kt * 2064 + h * 129 + 129],
                                         start=(kt == 0), stop=(kt == 15))
                    last.then_inc(SEMH['T'], 1)
                return None
            sec('T', _mavs, None)
            CN['T'] += 4
            def _divb(v, qb=qb):
                w(v, 'T')
                for bank in range(2):
                    pp = psV_[bank]
                    dn = mkap(pp, 128, [[256, 2], [1, 1]])
                    v.reciprocal(gst.t_[:, 1536 + bank * 2:1538 + bank * 2], dn)
                    v.drain()
                    src = mkap(pp, 0, [[256, 2], [1, 128]])
                    rc = mkap(gst.t_, 1536 + bank * 2, [[1, 2], [0, 128]])
                    dst = mkap(aostg, bank * 256, [[128, 2], [1, 128]])
                    v.tensor_tensor(dst, src, rc, ALU.mult)
                    v.drain()
                return v.drain()
            V(_divb)
            mset('psVav', 'V')
            def _trb(te, qb=qb):
                w(te, 'V')
                last = None
                for qt in range(4):
                    last = te.transpose(psT_[:, qt * 128:(qt + 1) * 128], aostg[:, qt * 128:(qt + 1) * 128], ident[:])
                return last
            T(_trb)
            def _evb(v, qb=qb):
                w(v, 'T', 'dO')
                v.tensor_copy(mst[:, 0:512], psT_[:, 0:512])
                return v.drain()
            V(_evb)
            def _stb(sy, h=h, qb=qb):
                w(sy, 'V')
                dma(sy, aoas[h * 128:(h + 1) * 128, qb * 512:(qb + 1) * 512], mst[:, 0:512], 'dO')
                return None
            Y(_stb)

    stB.close()
    stAB.close()

    if stop_after == 'b':
        def _fin(sy):
            w(sy, 'S', 'V', 'T')
            sy.wait_ge(SEMH['dO'], CN['dO'])
            d = dbg('aofull', [D, S_], BF16)
            if d is not None:
                dma(sy, d, aoas[:, :], 'dO')
            sy.wait_ge(s_dO, CN['dO'])
            return None
        Y(_fin)
        ctx.close()
        return nc, dbg_list

    stC = ctx.enter_context(contextlib.ExitStack())
    Xc = Mat(sbt('Xc', [128, NT * 512], F32, stC), NT, 512)
    Xc2 = Mat(sbt('Xc2', [128, NT * 512], F32, stC), NT, 512)
    ybfC = Mat(sbt('ybfC', [128, NT * 512], BF16, stC), NT, 512)
    aoAc = Mat(sbt('aoAc', [128, NT * 512], BF16, stC), NT, 512)
    aoAc2 = Mat(sbt('aoAc2', [128, NT * 512], BF16, stC), NT, 512)
    ws0C = Mat(sbt('ws0C', [128, NT * 512], BF16, stC), NT, 512)
    ws1C = Mat(sbt('ws1C', [128, NT * 512], BF16, stC), NT, 512)
    WSC = [ws0C, ws1C]
    XCB = [Xc, Xc2]
    AOB = [aoAc, aoAc2]
    dw_xc = [0, 0]

    def emit_ldxc(b):
        def _ldxc(sy, b=b):
            if b == 0:
                w(sy, 'V', 'T', 'S', 'dG', 'dO')
            else:
                mwait(sy, 'xcst%d' % (b % 2), 'aofree%d' % (b % 2))
            src3 = x2s[:, b * BLK:(b + 1) * BLK].rearrange('(n p) c -> p n c', p=128)
            dst3 = XCB[b % 2].t_[:, :].rearrange('p (n c) -> p n c', c=512)
            dma(sy, dst3, src3, 'dW')
            src4 = aoas[:, b * BLK:(b + 1) * BLK].rearrange('(n p) c -> p n c', p=128)
            dst4 = AOB[b % 2].t_[:, :].rearrange('p (n c) -> p n c', c=512)
            dma(sy, dst4, src4, 'dW')
            return None
        Y(_ldxc)
        dw_xc[b % 2] = CN['dW']

    mset('ppq0', 'S')
    mset('ppq1', 'S')
    BKO = [psP[0], psP[1], psS_[0], psS_[1]]
    for _bk in range(4):
        mset('ppqo%d' % _bk, 'S')
    emit_ldxc(0)
    for b in range(NB):
        Xcur = XCB[b % 2]
        aoc = AOB[b % 2]
        dwm_xc = dw_xc[b % 2]
        for s_ in range(4):
            def _ldwoc(gp, s_=s_, first=(b == 0 and s_ < 2)):
                if first:
                    w(gp, 'T', 'V', 'S', 'dO')
                mwait(gp, 'wsb%d' % (s_ % 2))
                stream_weight(gp, wobT, WSC[s_ % 2], NT, 512, s_ * 512)
                return None
            G(_ldwoc)
            for dt_ in range(4):
                dmt = s_ * 4 + dt_
                def _mmoc(te, s_=s_, dt_=dt_, dmt=dmt, aoc=aoc, dwm_xc=dwm_xc, b=b):
                    w(te, 'dG')
                    te.wait_ge(SEMH['dW'], dwm_xc)
                    if dmt < 4:
                        mwait(te, 'ppqf%d' % ([2, 3, 0, 1][dmt]) if b > 0 else 'ppq%d' % (dmt % 2))
                    else:
                        mwait(te, 'ppqo%d' % (dmt % 4))
                    pp = BKO[dmt % 4]
                    last = None
                    for j in range(NT):
                        last = te.matmul(pp[:, :], WSC[s_ % 2].tc(j, dt_ * 128, (dt_ + 1) * 128),
                                         aoc.t(j), start=(j == 0), stop=(j == NT - 1))
                    return last
                T(_mmoc)
                def _resc(v, dmt=dmt, Xcur=Xcur):
                    w(v, 'T')
                    v.scalar_tensor_tensor(Xcur.t(dmt), BKO[dmt % 4][:, :], cbo[:, dmt:dmt + 1], Xcur.t(dmt), ALU.add, ALU.add)
                    v.drain()
                    v.tensor_copy(ybfC.t(dmt), Xcur.t(dmt))
                    if dmt == 0:
                        v.tensor_copy(gst.t_[:, 0:512], Xcur.t(0))
                        v.tensor_copy(gst.t_[:, 1024:1536], Xcur.t(0))
                    else:
                        v.tensor_max(gst.t_[:, 0:512], gst.t_[:, 0:512], Xcur.t(dmt))
                        v.tensor_tensor(gst.t_[:, 1024:1536], gst.t_[:, 1024:1536], Xcur.t(dmt), ALU.min)
                    return v.drain()
                V(_resc)
                mset('ppqo%d' % (dmt % 4), 'V')
            mset('wsb%d' % (s_ % 2), 'T')
        mset('aofree%d' % (b % 2), 'T')
        if b + 1 < NB:
            emit_ldxc(b + 1)
        emit_ffn(Xcur, ybfC, WSC, first_block=False, prefused=True)
        def _stout(sy, b=b, Xcur=Xcur):
            w(sy, 'V', 'T')
            src3 = Xcur.t_[:, :].rearrange('p (n c) -> p n c', c=512)
            dst3 = xoT[:, b * BLK:(b + 1) * BLK].rearrange('(n p) c -> p n c', p=128)
            dma(sy, dst3, src3, 'dO')
            return None
        Y(_stout)
        mset('xcst%d' % (b % 2), 'dO')

    def _final(sy):
        w(sy, 'S', 'V', 'T')
        sy.wait_ge(s_dO, CN['dO'])
        return None
    Y(_final)
    ctx.close()
    return nc, dbg_list


import ml_dtypes


def _host_prep(inputs, core):
    d = {}
    d['xT'] = np.ascontiguousarray(np.asarray(inputs['L0_attn_a_x'])[core].T)
    d['wqT'] = np.ascontiguousarray(np.asarray(inputs['L0_attn_a_Wq']).T)
    d['wkT'] = np.ascontiguousarray(np.asarray(inputs['L0_attn_a_Wk']).T)
    d['wvT'] = np.ascontiguousarray(np.asarray(inputs['L0_attn_a_Wv']).T)
    d['woaT'] = np.ascontiguousarray(np.asarray(inputs['L0_attn_a_Wo']).T)
    d['wqdT'] = np.ascontiguousarray(np.asarray(inputs['L0_attn_b_Wq_down']).T)
    d['wkvdT'] = np.ascontiguousarray(np.asarray(inputs['L0_attn_b_Wkv_down']).T)
    d['wquT'] = np.ascontiguousarray(np.asarray(inputs['L0_attn_b_Wq_up']).T)
    d['wknT'] = np.ascontiguousarray(np.asarray(inputs['L0_attn_b_Wk_nope']).T)
    d['wkrT'] = np.ascontiguousarray(np.asarray(inputs['L0_attn_b_Wk_rope']).T)
    d['wvuT'] = np.ascontiguousarray(np.asarray(inputs['L0_attn_b_Wv_up']).T)
    d['wobT'] = np.ascontiguousarray(np.asarray(inputs['L0_attn_b_Wo']).T)
    d['wf8T'] = np.ascontiguousarray(np.asarray(inputs['L0_ffn_weight']).T)

    def cols(vec, n):
        return np.ascontiguousarray(np.asarray(vec, np.float32).reshape(n, 128).T)
    d['c_bqd'] = cols(inputs['L0_attn_b_bq_down'], 4)
    d['c_bkvd'] = cols(inputs['L0_attn_b_bkv_down'], 4)
    d['c_bqu'] = cols(inputs['L0_attn_b_bq_up'], 16)
    d['c_bkn'] = cols(inputs['L0_attn_b_bk_nope'], 8)
    d['c_bkr'] = cols(inputs['L0_attn_b_bk_rope'], 8)
    d['c_bo'] = cols(inputs['L0_attn_b_bo'], 16)
    d['c_fb'] = cols(inputs['L0_ffn_bias'], 16)
    d['c_ws'] = cols(np.asarray(inputs['L0_ffn_w_scale'])[:, 0], 16)
    d['r_bvu'] = np.asarray(inputs['L0_attn_b_bv_up'], np.float32).reshape(1, -1)

    bf = ml_dtypes.bfloat16
    d['identin'] = np.eye(128, dtype=bf)
    kk = np.arange(128)
    d['maskin'] = (kk[:, None] <= kk[None, :]).astype(bf)
    d['onesttin'] = np.ones((128, 32), dtype=bf)
    o32 = np.zeros((32, 128), np.float32); o32[0, :] = 1.0
    d['os32fin'] = o32
    d['os32bin'] = o32.astype(bf)
    d['onecolbin'] = np.ones((128, 1), dtype=bf)
    d['iotacolin'] = np.arange(128, dtype=np.float32).reshape(128, 1)
    ii = np.arange(128)
    d['eArowin'] = (-2.0 * (ii % 64) / 128.0).astype(np.float32).reshape(1, 128)
    jj = np.arange(512)
    d['eBcolin'] = np.ascontiguousarray((-2.0 * jj / 1024.0).astype(np.float32).reshape(4, 128).T)
    d['signAin'] = np.where(ii % 128 < 64, -1.0, 1.0).astype(np.float32).reshape(1, 128)
    d['identfin'] = np.eye(128, dtype=np.float32)
    io = np.zeros((32, 2048), np.float32); io[0, :] = np.arange(2048)
    d['iota32in'] = io
    t64 = np.arange(2048, dtype=np.float64)
    dA = np.arange(128)
    invA = np.float64(np.float32(10000.0) ** np.float32(-2.0 * (dA % 64) / 128.0))
    angA = np.float64(np.float32(t64[:, None] * invA[None, :]))
    cosA = np.cos(angA)
    sinA = np.sin(angA) * np.where(dA < 64, -1.0, 1.0)[None, :]
    d['tabAcin'] = cosA.reshape(16, 128, 128).transpose(1, 0, 2).reshape(128, 2048).astype(ml_dtypes.bfloat16)
    d['tabAsin'] = sinA.reshape(16, 128, 128).transpose(1, 0, 2).reshape(128, 2048).astype(ml_dtypes.bfloat16)
    jB = np.arange(512)
    invB = np.float64(np.float32(10000.0) ** np.float32(-2.0 * jB / 1024.0))
    angB = np.float64(np.float32(t64[:, None] * invB[None, :]))
    cosB, sinB = np.cos(angB), np.sin(angB)
    d['tabBcin'] = cosB.reshape(4, 512, 4, 128).transpose(3, 0, 2, 1).reshape(128, 8192).astype(ml_dtypes.bfloat16)
    d['tabBsin'] = sinB.reshape(4, 512, 4, 128).transpose(3, 0, 2, 1).reshape(128, 8192).astype(ml_dtypes.bfloat16)
    return d


def kernel(**inputs):
    from concourse.bass_utils import run_bass_kernel_spmd
    nc, _ = build()
    in_maps = [_host_prep(inputs, core) for core in range(8)]
    res = run_bass_kernel_spmd(nc, in_maps, core_ids=list(range(8)))
    out = np.empty((8, 2048, 2048), dtype=np.float32)
    for core in range(8):
        out[core] = np.asarray(res.results[core]['xoT']).T
    return out


# version 28
# speedup vs baseline: 1.0098x; 1.0098x over previous
import numpy as np
import concourse.bass as bass
import concourse.mybir as mybir

F32 = mybir.dt.float32
F16 = mybir.dt.float16
BF16 = mybir.dt.bfloat16
I32 = mybir.dt.int32
I8 = mybir.dt.int8
AF = mybir.ActivationFunctionType
ALU = mybir.AluOpType

D = 2048
S = 2048
NB = 4
BLK = 512
NT = 16
NH = 16
KVH = 4
HD = 128
LAT = 512
EPS_LN = 1e-5
EPS_A = 1e-6
SMAX = float(1.0 / np.sqrt(np.float32(128.0)))
PI_HI = 6.28125
PI_LO = float(2.0 * np.pi - 6.28125)
INV2PI = float(1.0 / (2.0 * np.pi))
HALFPI = float(np.pi / 2.0)
LN1E4 = float(np.log(10000.0))


class Mat:
    def __init__(self, t, nt, C):
        self.t_, self.nt, self.C = t, nt, C

    def t(self, j):
        return self.t_[:, j * self.C:(j + 1) * self.C]

    def tc(self, j, c0, c1):
        return self.t_[:, j * self.C + c0: j * self.C + c1]


def build(debug=(), stop_after=None):
    import contextlib
    nc = bass.Bass('TRN2', target_bir_lowering=False, debug=False)

    xT = nc.dram_tensor('xT', [D, S], F32, kind='ExternalInput').ap()
    wqT = nc.dram_tensor('wqT', [D, D], F32, kind='ExternalInput').ap()
    wkT = nc.dram_tensor('wkT', [D, 512], F32, kind='ExternalInput').ap()
    wvT = nc.dram_tensor('wvT', [D, 512], F32, kind='ExternalInput').ap()
    woaT = nc.dram_tensor('woaT', [D, D], F32, kind='ExternalInput').ap()
    wqdT = nc.dram_tensor('wqdT', [D, LAT], F32, kind='ExternalInput').ap()
    wkvdT = nc.dram_tensor('wkvdT', [D, LAT], F32, kind='ExternalInput').ap()
    wquT = nc.dram_tensor('wquT', [LAT, D], F32, kind='ExternalInput').ap()
    wknT = nc.dram_tensor('wknT', [LAT, 1024], F32, kind='ExternalInput').ap()
    wkrT = nc.dram_tensor('wkrT', [LAT, 1024], F32, kind='ExternalInput').ap()
    wvuT = nc.dram_tensor('wvuT', [LAT, D], F32, kind='ExternalInput').ap()
    wobT = nc.dram_tensor('wobT', [D, D], F32, kind='ExternalInput').ap()
    wf8T = nc.dram_tensor('wf8T', [D, D], I8, kind='ExternalInput').ap()
    c_bqd = nc.dram_tensor('c_bqd', [128, 4], F32, kind='ExternalInput').ap()
    c_bkvd = nc.dram_tensor('c_bkvd', [128, 4], F32, kind='ExternalInput').ap()
    c_bqu = nc.dram_tensor('c_bqu', [128, 16], F32, kind='ExternalInput').ap()
    c_bkn = nc.dram_tensor('c_bkn', [128, 8], F32, kind='ExternalInput').ap()
    c_bkr = nc.dram_tensor('c_bkr', [128, 8], F32, kind='ExternalInput').ap()
    c_bo = nc.dram_tensor('c_bo', [128, 16], F32, kind='ExternalInput').ap()
    c_fb = nc.dram_tensor('c_fb', [128, 16], F32, kind='ExternalInput').ap()
    c_ws = nc.dram_tensor('c_ws', [128, 16], F32, kind='ExternalInput').ap()
    r_bvu = nc.dram_tensor('r_bvu', [1, D], F32, kind='ExternalInput').ap()
    identin = nc.dram_tensor('identin', [128, 128], BF16, kind='ExternalInput').ap()
    maskin = nc.dram_tensor('maskin', [128, 128], BF16, kind='ExternalInput').ap()
    onesttin = nc.dram_tensor('onesttin', [128, 32], BF16, kind='ExternalInput').ap()
    os32fin = nc.dram_tensor('os32fin', [32, 128], F32, kind='ExternalInput').ap()
    os32bin = nc.dram_tensor('os32bin', [32, 128], BF16, kind='ExternalInput').ap()
    onecolbin = nc.dram_tensor('onecolbin', [128, 1], BF16, kind='ExternalInput').ap()
    iotacolin = nc.dram_tensor('iotacolin', [128, 1], F32, kind='ExternalInput').ap()
    eArowin = nc.dram_tensor('eArowin', [1, 128], F32, kind='ExternalInput').ap()
    eBcolin = nc.dram_tensor('eBcolin', [128, 4], F32, kind='ExternalInput').ap()
    signAin = nc.dram_tensor('signAin', [1, 128], F32, kind='ExternalInput').ap()
    identfin = nc.dram_tensor('identfin', [128, 128], F32, kind='ExternalInput').ap()
    iota32in = nc.dram_tensor('iota32in', [32, S], F32, kind='ExternalInput').ap()
    tabAcin = nc.dram_tensor('tabAcin', [128, 2048], BF16, kind='ExternalInput').ap()
    tabAsin = nc.dram_tensor('tabAsin', [128, 2048], BF16, kind='ExternalInput').ap()
    tabBcin = nc.dram_tensor('tabBcin', [128, 8192], BF16, kind='ExternalInput').ap()
    tabBsin = nc.dram_tensor('tabBsin', [128, 8192], BF16, kind='ExternalInput').ap()

    xoT = nc.dram_tensor('xoT', [D, S], F32, kind='ExternalOutput').ap()
    x2s = nc.dram_tensor('x2s', [D, S], F32, kind='Internal').ap()
    kAs = nc.dram_tensor('kAs', [D, S], BF16, kind='Internal').ap()
    aoas = nc.dram_tensor('aoas', [D, S], BF16, kind='Internal').ap()

    dbg_list = []
    dbg_t = {}
    def dbg(name, shape, dtype=F32):
        if name in debug:
            dbg_t[name] = nc.dram_tensor('dbg_' + name, shape, dtype, kind='ExternalOutput').ap()
            dbg_list.append('dbg_' + name)
            return dbg_t[name]
        return None

    ctx = contextlib.ExitStack()

    def sbt(name, shape, dt, st=None):
        return (st or ctx).enter_context(nc.sbuf_tensor(name, shape, dt))

    def pst(name, shape, dt, st=None):
        return (st or ctx).enter_context(nc.psum_tensor(name, shape, dt))

    ident = sbt('ident', [128, 128], BF16)
    maskT = sbt('maskT', [128, 128], BF16)
    onestt = sbt('onestt', [128, 32], BF16)
    os32f = sbt('os32f', [32, 128], F32)
    os32b = sbt('os32b', [32, 128], BF16)
    onecolb = sbt('onecolb', [128, 1], BF16)
    iotacol = sbt('iotacol', [128, 1], F32)
    eArow = sbt('eArow', [1, 128], F32)
    eBcol = sbt('eBcol', [128, 4], F32)
    signA = sbt('signA', [1, 128], F32)
    identf = sbt('identf', [128, 128], F32)
    cbqd = sbt('cbqd', [128, 4], F32)
    cbkvd = sbt('cbkvd', [128, 4], F32)
    cbqu = sbt('cbqu', [128, 16], F32)
    cbkn = sbt('cbkn', [128, 8], F32)
    cbkr = sbt('cbkr', [128, 8], F32)
    cbo = sbt('cbo', [128, 16], F32)
    cfb = sbt('cfb', [128, 16], F32)
    cws = sbt('cws', [128, 16], F32)
    bvu32 = sbt('bvu32', [32, D], BF16)
    cpool = sbt('cpool', [128, 8], F32)
    CP_ROUND, CP_QUANT, CP_EPSLN, CP_NHALF, CP_SMAX, CP_EPSA, CP_MU = 0, 1, 2, 3, 4, 5, 6
    c1col = sbt('c1col', [128, 16], F32)
    itmp = sbt('itmp', [128, 512], I32)
    itmp2 = sbt('itmp2', [128, 512], I32)
    rows32 = sbt('rows32', [32, 1536], F32)
    rowt = sbt('rowt', [1, 1536], F32)
    RW_M, RW_RSTD, RW_T1 = 0, 512, 1024
    inv32 = sbt('inv32', [32, 128], F32)
    invfgnb = sbt('invfgnb', [128, 128], F32)
    invfb = sbt('invfb', [128, 128], F32)
    invBcol = sbt('invBcol', [128, 4], F32)
    gst = Mat(sbt('gst', [128, 2048], F32), 2, 1024)
    mstP = sbt('mstP', [128, 2 * 512], BF16)
    mstP2 = sbt('mstP2', [128, 2 * 512], BF16)
    f16stP = sbt('f16stP', [128, 512], F16)

    s_dW = ctx.enter_context(nc.semaphore('s_dW'))
    s_dG = ctx.enter_context(nc.semaphore('s_dG'))
    s_dO = ctx.enter_context(nc.semaphore('s_dO'))
    s_T = ctx.enter_context(nc.semaphore('s_T'))
    s_V = ctx.enter_context(nc.semaphore('s_V'))
    s_S = ctx.enter_context(nc.semaphore('s_S'))

    blk = ctx.enter_context(nc.Block())

    CN = {'T': 0, 'V': 0, 'S': 0, 'dW': 0, 'dG': 0, 'dO': 0}
    SEM = {'T': s_T, 'V': s_V, 'S': s_S, 'dW': s_dW, 'dG': s_dG, 'dO': s_dO}

    def sec(engine_name, f, inc=None):
        dec = {'T': blk.tensor, 'V': blk.vector, 'S': blk.scalar,
               'G': blk.gpsimd, 'Y': blk.sync}[engine_name]
        def wrapper(eng):
            last = f(eng)
            if inc is not None:
                assert last is not None, f'section for {engine_name} returned no instr'
                CN[inc] += 1
                last.then_inc(SEM[inc], 1)
        dec(wrapper)

    def T(f):
        sec('T', f, 'T')
    def V(f):
        sec('V', f, 'V')
    def Sc(f):
        sec('S', f, 'S')
    def G(f):
        sec('G', f, None)
    def Y(f):
        sec('Y', f, None)

    def dma(eng, out, in_, sem_key):
        eng.dma_start(out=out, in_=in_).then_inc(SEM[sem_key], 16)
        CN[sem_key] += 16

    def _loads(sy):
        for dst, src in [(ident[:], identin), (maskT[:], maskin), (onestt[:], onesttin),
                         (os32f[:], os32fin), (os32b[:], os32bin), (onecolb[:], onecolbin),
                         (iotacol[:], iotacolin), (eArow[:], eArowin),
                         (eBcol[:], eBcolin), (signA[:], signAin),
                         (cbqd[:], c_bqd), (cbkvd[:], c_bkvd), (cbqu[:], c_bqu),
                         (cbkn[:], c_bkn), (cbkr[:], c_bkr), (cbo[:], c_bo),
                         (cfb[:], c_fb), (cws[:], c_ws), (rowt[0:1, 0:D] if D <= 1536 else None, r_bvu)]:
            pass
        return None

    def _loads2(sy):
        for dst, src in [(ident[:], identin), (maskT[:], maskin), (onestt[:], onesttin),
                         (os32f[:], os32fin), (os32b[:], os32bin), (onecolb[:], onecolbin),
                         (iotacol[:], iotacolin), (eArow[:], eArowin),
                         (eBcol[:], eBcolin), (signA[:], signAin),
                         (cbqd[:], c_bqd), (cbkvd[:], c_bkvd), (cbqu[:], c_bqu),
                         (cbkn[:], c_bkn), (cbkr[:], c_bkr), (cbo[:], c_bo),
                         (cfb[:], c_fb), (cws[:], c_ws), (identf[:], identfin),
                         (gst.t_[0:1, 0:D], r_bvu)]:
            dma(sy, dst, src, 'dW')
        return None
    Y(_loads2)
    DW0 = CN['dW']

    def _setup(v):
        v.wait_ge(s_dW, DW0)
        v.memset(bvu32[:], 0.0)
        v.memset(rows32[:], 0.0)
        v.memset(inv32[:], 0.0)
        v.memset(cpool[:], 0.0)
        v.drain()
        v.memset(cpool[:, CP_EPSLN:CP_EPSLN + 1], EPS_LN)
        v.memset(cpool[:, CP_NHALF:CP_NHALF + 1], -0.5)
        v.memset(cpool[:, CP_SMAX:CP_SMAX + 1], SMAX)
        v.memset(cpool[:, CP_EPSA:CP_EPSA + 1], EPS_A)
        v.memset(cpool[:, CP_MU:CP_MU + 1], 0.7)
        v.tensor_copy(bvu32[0:1, 0:D], gst.t_[0:1, 0:D])
        v.drain()
        v.tensor_copy(itmp[:, 0:1], cpool[:, CP_MU:CP_MU + 1])
        v.drain()
        v.tensor_copy(cpool[:, CP_MU:CP_MU + 1], itmp[:, 0:1])
        v.drain()
        v.tensor_scalar(cpool[:, CP_ROUND:CP_ROUND + 1], cpool[:, CP_MU:CP_MU + 1], -0.5, 0.5, ALU.mult, ALU.add)
        v.tensor_scalar(cpool[:, CP_QUANT:CP_QUANT + 1], cpool[:, CP_MU:CP_MU + 1], -0.5, 128.5, ALU.mult, ALU.add)
        return v.drain()
    V(_setup)
    V_SETUP = CN['V']

    def _invf(sc):
        sc.wait_ge(s_V, V_SETUP)
        sc.activation(rowt[0:1, RW_T1:RW_T1 + 128], eArow[:], AF.Exp, scale=LN1E4)
        sc.activation(invBcol[:], eBcol[:], AF.Exp, scale=LN1E4)
        return sc.drain()
    Sc(_invf)
    sN = CN['S']

    def _inv1(v, sN=sN):
        v.wait_ge(s_S, sN)
        v.tensor_copy(inv32[0:1, :], rowt[0:1, RW_T1:RW_T1 + 128])
        return v.drain()
    V(_inv1)

    st_setup = ctx.enter_context(contextlib.ExitStack())
    psX = pst('psX', [128, 512], F32, st_setup)

    vN = CN['V']
    def _invb1(te, vN=vN):
        te.wait_ge(s_V, vN)
        return te.matmul(psX[:, 0:128], os32f[:], inv32[:], start=True, stop=True)
    T(_invb1)

    tN = CN['T']
    def _inv2(v, tN=tN):
        v.wait_ge(s_T, tN)
        v.tensor_copy(invfb[:], psX[:, 0:128])
        v.tensor_mul(inv32[0:1, :], rowt[0:1, RW_T1:RW_T1 + 128], signA[:])
        return v.drain()
    V(_inv2)

    vN = CN['V']
    def _invb2(te, vN=vN):
        te.wait_ge(s_V, vN)
        return te.matmul(psX[:, 0:128], os32f[:], inv32[:], start=True, stop=True)
    T(_invb2)

    tN = CN['T']
    def _inv3(v, tN=tN):
        v.wait_ge(s_T, tN)
        v.tensor_copy(invfgnb[:], psX[:, 0:128])
        return v.drain()
    V(_inv3)
    V_INV = CN['V']

    def tab_red(v, ang, b):
        p = 1 if b >= 1024 else 0
        u = gst.t_[:, b + 256:b + 384]
        r1 = gst.t_[:, b + 384:b + 512]
        ic = itmp[:, p * 128:(p + 1) * 128]
        v.tensor_scalar(u, ang, INV2PI, cpool[:, CP_ROUND:CP_ROUND + 1], ALU.mult, ALU.add)
        v.drain()
        v.tensor_copy(ic, u)
        v.drain()
        v.tensor_copy(u, ic)
        v.drain()
        v.scalar_tensor_tensor(r1, u, -PI_HI, ang, ALU.mult, ALU.add)
        v.drain()
        v.scalar_tensor_tensor(ang, u, -PI_LO, r1, ALU.mult, ALU.add)
        v.drain()

    return_env = dict(locals())
    return _stages(return_env, debug=debug, stop_after=stop_after)


def _stages(env, debug=(), stop_after=None):
    import contextlib
    g = env
    nc = g['nc']
    T, V, Sc, G, Y = g['T'], g['V'], g['Sc'], g['G'], g['Y']
    sec = g['sec']
    CN, SEM, dma = g['CN'], g['SEM'], g['dma']
    s_dW, s_dG, s_dO = g['s_dW'], g['s_dG'], g['s_dO']
    s_T, s_V, s_S = g['s_T'], g['s_V'], g['s_S']
    sbt, pst = g['sbt'], g['pst']
    gst, itmp, cpool, rowt, rows32 = g['gst'], g['itmp'], g['cpool'], g['rowt'], g['rows32']
    itmp2 = g['itmp2']
    RW_M, RW_RSTD, RW_T1 = g['RW_M'], g['RW_RSTD'], g['RW_T1']
    invfb, invfgnb, invBcol = g['invfb'], g['invfgnb'], g['invBcol']
    identf = g['identf']
    iotacol, ident, maskT, onestt, os32f, os32b, onecolb = (
        g['iotacol'], g['ident'], g['maskT'], g['onestt'], g['os32f'], g['os32b'], g['onecolb'])
    cbqd, cbkvd, cbqu, cbkn, cbkr, cbo, cfb, cws, bvu32, c1col = (
        g['cbqd'], g['cbkvd'], g['cbqu'], g['cbkn'], g['cbkr'], g['cbo'], g['cfb'], g['cws'],
        g['bvu32'], g['c1col'])
    mst = g['mstP']
    mst2 = g['mstP2']
    f16st = g['f16stP']
    tab_red = g['tab_red']
    dbg = g['dbg']
    dbg_list = g['dbg_list']
    ctx = g['ctx']
    CPR, CPQ, CPE, CPH, CPS, CPA, CPM = (g['CP_ROUND'], g['CP_QUANT'], g['CP_EPSLN'],
                                          g['CP_NHALF'], g['CP_SMAX'], g['CP_EPSA'], g['CP_MU'])
    xT, xoT, x2s, kAs, aoas = g['xT'], g['xoT'], g['x2s'], g['kAs'], g['aoas']
    wqT, wkT, wvT, woaT = g['wqT'], g['wkT'], g['wvT'], g['woaT']
    wqdT, wkvdT, wquT, wknT, wkrT, wvuT, wobT, wf8T = (
        g['wqdT'], g['wkvdT'], g['wquT'], g['wknT'], g['wkrT'], g['wvuT'], g['wobT'], g['wf8T'])
    iota32in = g['iota32in']
    tabAcin, tabAsin = g['tabAcin'], g['tabAsin']
    tabBcin, tabBsin = g['tabBcin'], g['tabBsin']
    psX = g['psX']
    st_setup = g['st_setup']
    S_ = S
    V_INV = g['V_INV']

    SEMH = {'T': s_T, 'V': s_V, 'S': s_S, 'dW': s_dW, 'dG': s_dG, 'dO': s_dO}


    def mkap(h, extra_off, free_dims):
        a = h[:, :] if not isinstance(h, bass.AP) else h
        return bass.AP(a.tensor, a.offset + extra_off, [list(a.ap[0])] + [list(d) for d in free_dims])

    def w(eng, *deps):
        for d in deps:
            if CN[d] > 0:
                eng.wait_ge(SEMH[d], CN[d])

    MK = {}
    def mset(name, key):
        MK[name] = (key, CN[key])
    def mwait(eng, *names):
        for nm in names:
            if nm in MK:
                k, c = MK[nm]
                if c > 0:
                    eng.wait_ge(SEMH[k], c)

    wsA2t = sbt('wsA2', [128, NT * 512], BF16)

    stA_scope = ctx.enter_context(contextlib.ExitStack())
    tabAc = sbt('tabAc', [128, 16 * 128], BF16, stA_scope)
    tabAs = sbt('tabAs', [128, 16 * 128], BF16, stA_scope)
    Xa = Mat(sbt('Xa', [128, NT * 512], F32, stA_scope), NT, 512)
    ybf = Mat(sbt('ybf', [128, NT * 512], BF16, stA_scope), NT, 512)
    qfB = Mat(sbt('qfB', [128, 4 * 2048], BF16, stA_scope), 4, 2048)
    qfA = Mat(sbt('qfA', [128, NT * 512], BF16, stA_scope), NT, 512)
    kfB = Mat(sbt('kfB', [128, 4 * 512], BF16, stA_scope), 4, 512)
    kfA = Mat(sbt('kfA', [128, 4 * 512], BF16, stA_scope), 4, 512)
    vaug = Mat(sbt('vaug', [128, 4 * 516], BF16, stA_scope), 4, 516)
    stLa = sbt('stLa', [128, KVH * 129], F32, stA_scope)
    stbf = sbt('stbf', [128, KVH * 129], BF16, stA_scope)
    ws0 = Mat(sbt('ws0A', [128, NT * 512], BF16, stA_scope), NT, 512)
    ws1 = Mat(sbt('ws1A', [128, NT * 512], BF16, stA_scope), NT, 512)
    WS = [ws0, ws1]

    def _tabinit(v):
        w(v, 'V')
        v.memset(stLa[:], 0.0)
        v.memset(stbf[:], 0.0)
        v.memset(vaug.t_[:, :], 0.0)
        v.drain()
        for tt in range(4):
            for h in range(KVH):
                c0 = tt * 516 + h * 129 + 128
                v.memset(vaug.t_[:, c0:c0 + 1], 1.0)
        return v.drain()
    V(_tabinit)

    def _ldtabA(sy):
        dma(sy, tabAc[:, :], tabAcin, 'dW')
        dma(sy, tabAs[:, :], tabAsin, 'dW')
        return None
    Y(_ldtabA)

    if stop_after == 'setup':
        def _fin(sy):
            w(sy, 'S', 'V', 'T')
            for nm, src, dt in (('tabAc', tabAc[:, :], BF16), ('tabAs', tabAs[:, :], BF16), ('cpool', cpool[:], F32)):
                d = dbg(nm, [128, src.shape[1]], dt)
                if d is not None:
                    dma(sy, d, src, 'dO')
            sy.wait_ge(s_dO, CN['dO'])
            return None
        Y(_fin)
        ctx.close()
        return nc, dbg_list

    st_setup.close()
    psP = [pst('psP0', [128, 512], F32), pst('psP1', [128, 512], F32)]
    psS_ = [pst('psS0', [128, 512], F32), pst('psS1', [128, 512], F32)]
    psV_ = [pst('psV0', [128, 512], F32), pst('psV1', [128, 512], F32)]
    psU = pst('psU', [128, 512], F32)
    psT_ = pst('psTt', [128, 1024], BF16)

    def ln_stats(Xsrc, ybf_dst, precopied=False, strip_dw=None):
        if strip_dw is not None:
            for st4 in range(4):
                def _cps(v, st4=st4, dwv=strip_dw[st4]):
                    v.wait_ge(s_dW, dwv)
                    w(v, 'V', 'T')
                    for j in range(st4 * 4, st4 * 4 + 4):
                        v.tensor_copy(ybf_dst.t(j), Xsrc.t(j))
                    return v.drain()
                V(_cps)
                def _mm_sum_s(te, st4=st4):
                    w(te, 'V')
                    last = None
                    for j in range(st4 * 4, st4 * 4 + 4):
                        last = te.matmul(psP[0][0:32, :], onestt[:], ybf_dst.t(j), start=(j == 0), stop=(j == NT - 1))
                    return last
                T(_mm_sum_s)
                for j2 in (st4 * 4, st4 * 4 + 2):
                    def _sq(v, j2=j2):
                        w(v, 'V', 'T')
                        v.tensor_mul(mst[:, 0:512], ybf_dst.t(j2), ybf_dst.t(j2))
                        v.tensor_mul(mst[:, 512:1024], ybf_dst.t(j2 + 1), ybf_dst.t(j2 + 1))
                        return v.drain()
                    V(_sq)
                    def _mm_sq(te, j2=j2):
                        w(te, 'V')
                        te.matmul(psP[1][0:32, :], onestt[:], mst[:, 0:512], start=(j2 == 0), stop=False)
                        return te.matmul(psP[1][0:32, :], onestt[:], mst[:, 512:1024], start=False, stop=(j2 == NT - 2))
                    T(_mm_sq)
        else:
            if not precopied:
                def _cp(v):
                    w(v, 'V', 'T')
                    for j in range(NT):
                        v.tensor_copy(ybf_dst.t(j), Xsrc.t(j))
                    return v.drain()
                V(_cp)
            def _mm_sum(te):
                w(te, 'V')
                last = None
                for j in range(NT):
                    last = te.matmul(psP[0][0:32, :], onestt[:], ybf_dst.t(j), start=(j == 0), stop=(j == NT - 1))
                return last
            T(_mm_sum)
            for j2 in range(0, NT, 2):
                def _sq(v, j2=j2):
                    w(v, 'V', 'T')
                    v.tensor_mul(mst[:, 0:512], ybf_dst.t(j2), ybf_dst.t(j2))
                    v.tensor_mul(mst[:, 512:1024], ybf_dst.t(j2 + 1), ybf_dst.t(j2 + 1))
                    return v.drain()
                V(_sq)
                def _mm_sq(te, j2=j2):
                    w(te, 'V')
                    te.matmul(psP[1][0:32, :], onestt[:], mst[:, 0:512], start=(j2 == 0), stop=False)
                    return te.matmul(psP[1][0:32, :], onestt[:], mst[:, 512:1024], start=False, stop=(j2 == NT - 2))
                T(_mm_sq)
        def _rowm(v):
            w(v, 'T')
            v.tensor_scalar(rowt[0:1, RW_M:RW_M + 512], psP[0][0:1, :], 1.0 / 2048.0, None, ALU.mult)
            v.tensor_scalar(rowt[0:1, RW_T1:RW_T1 + 512], psP[1][0:1, :], 1.0 / 2048.0, None, ALU.mult)
            v.drain()
            v.tensor_mul(rowt[0:1, RW_RSTD:RW_RSTD + 512], rowt[0:1, RW_M:RW_M + 512], rowt[0:1, RW_M:RW_M + 512])
            v.drain()
            v.tensor_sub(rowt[0:1, RW_T1:RW_T1 + 512], rowt[0:1, RW_T1:RW_T1 + 512], rowt[0:1, RW_RSTD:RW_RSTD + 512])
            return v.drain()
        V(_rowm)
        def _rstd(sc):
            w(sc, 'V')
            sc.activation(rowt[0:1, RW_RSTD:RW_RSTD + 512], rowt[0:1, RW_T1:RW_T1 + 512], AF.Ln, bias=cpool[0:1, CPE:CPE + 1])
            sc.drain()
            sc.activation(rowt[0:1, RW_RSTD:RW_RSTD + 512], rowt[0:1, RW_RSTD:RW_RSTD + 512], AF.Exp, scale=cpool[0:1, CPH:CPH + 1])
            return sc.drain()
        Sc(_rstd)

    def ln_apply(Xsrc, out_mat, bb=None):
        if bb is None:
            bb = (psV_[0], psV_[1])
        def _rows(v):
            w(v, 'S', 'T')
            v.scalar_tensor_tensor(rowt[0:1, RW_T1:RW_T1 + 512], rowt[0:1, RW_M:RW_M + 512], -1.0,
                                   rowt[0:1, RW_RSTD:RW_RSTD + 512], ALU.mult, ALU.mult)
            v.drain()
            v.tensor_copy(rows32[0:1, 0:512], rowt[0:1, RW_RSTD:RW_RSTD + 512])
            v.tensor_copy(rows32[0:1, 512:1024], rowt[0:1, RW_T1:RW_T1 + 512])
            return v.drain()
        V(_rows)
        def _b12(te):
            w(te, 'V')
            te.matmul(bb[0][:, :], os32f[:], rows32[:, 0:512], start=True, stop=True)
            return te.matmul(bb[1][:, :], os32f[:], rows32[:, 512:1024], start=True, stop=True)
        T(_b12)
        marks = []
        for j2 in range(0, NT, 2):
            def _apply(v, j2=j2):
                w(v, 'T')
                for j in (j2, j2 + 1):
                    p = (j % 2) * 1024
                    v.tensor_mul(gst.t_[:, p:p + 512], Xsrc.t(j), bb[0][:, :])
                    v.drain()
                    v.tensor_add(out_mat.t(j), gst.t_[:, p:p + 512], bb[1][:, :])
                    v.drain()
                return v.drain()
            V(_apply)
            marks.append(CN['V'])
        return marks

    def stream_weight(sy, src_ap, dst_mat, n_dtiles, cols, col0, sem_key='dG'):
        src = src_ap[0:n_dtiles * 128, col0:col0 + cols]
        src3 = src.rearrange('(n p) c -> p n c', p=128)
        dst3 = dst_mat.t_[:, 0:n_dtiles * cols].rearrange('p (n c) -> p n c', c=cols)
        dma(sy, dst3, src3, sem_key)

    def stream_weight_rt(sy, src_ap, dst_mat, sd0, n, dd0, cols, col0, sem_key='dG'):
        src = src_ap[sd0 * 128:(sd0 + n) * 128, col0:col0 + cols]
        src3 = src.rearrange('(n p) c -> p n c', p=128)
        dst3 = dst_mat.t_[:, dd0 * cols:(dd0 + n) * cols].rearrange('p (n c) -> p n c', c=cols)
        dma(sy, dst3, src3, sem_key)

    def rope_elu(v, pp, dst, b, tt, kind):
        cosb = tabAc[:, (b * 4 + tt) * 128:(b * 4 + tt + 1) * 128]
        sinb = tabAs[:, (b * 4 + tt) * 128:(b * 4 + tt + 1) * 128]
        cos4 = mkap(cosb, 0, [[0, 4], [64, 2], [1, 64]])
        sin4 = mkap(sinb, 0, [[0, 4], [64, 2], [1, 64]])
        swp = mkap(pp, 64, [[128, 4], [-64, 2], [1, 64]])
        pp4 = mkap(pp, 0, [[128, 4], [64, 2], [1, 64]])
        m0 = mkap(mst, 0, [[128, 4], [64, 2], [1, 64]])
        m1 = mkap(mst, 512, [[128, 4], [64, 2], [1, 64]])
        v.tensor_tensor(m0, pp4, cos4, ALU.mult)
        v.tensor_tensor(m1, swp, sin4, ALU.mult)
        v.drain()
        v.tensor_add(mst[:, 0:512], mst[:, 0:512], mst[:, 512:1024])
        v.drain()
        v.tensor_scalar(mst[:, 512:1024], mst[:, 0:512], 0.0, None, ALU.min)
        v.drain()

    def rope_elu_fin(v, dst):
        v.scalar_tensor_tensor(dst, mst[:, 0:512], 0.0, mst[:, 512:1024], ALU.max, ALU.add)
        v.drain()

    def emit_ffn(Xcur, ybuf, ws, first_block, prefused=False):
        BKF = [psS_[0], psS_[1], psP[0], psP[1]]
        dqm = []
        ln_stats(Xcur, ybuf, precopied=prefused)
        if not prefused:
            def _tree(v):
                w(v, 'T', 'S', 'V')
                v.tensor_max(gst.t_[:, 0:512], Xcur.t(0), Xcur.t(1))
                v.tensor_tensor(gst.t_[:, 1024:1536], Xcur.t(0), Xcur.t(1), ALU.min)
                v.drain()
                for j in range(2, NT):
                    v.tensor_max(gst.t_[:, 0:512], gst.t_[:, 0:512], Xcur.t(j))
                    v.tensor_tensor(gst.t_[:, 1024:1536], gst.t_[:, 1024:1536], Xcur.t(j), ALU.min)
                    v.drain()
                return v.drain()
            V(_tree)
        def _trc(te):
            w(te, 'V')
            last = None
            for c4 in range(4):
                last = te.matmul(psS_[0][:, c4 * 128:(c4 + 1) * 128],
                                 gst.t_[:, c4 * 128:(c4 + 1) * 128], identf[:, :],
                                 start=True, stop=True)
                last = te.matmul(psS_[1][:, c4 * 128:(c4 + 1) * 128],
                                 gst.t_[:, c4 * 128 + 1024:(c4 + 1) * 128 + 1024], identf[:, :],
                                 start=True, stop=True)
            return last
        T(_trc)
        def _red(v):
            w(v, 'T')
            for c4 in range(4):
                v.reduce_max(gst.t_[:, 512 + c4:512 + c4 + 1], psS_[0][:, c4 * 128:(c4 + 1) * 128], mybir.AxisListType.X)
                v.tensor_reduce(gst.t_[:, 516 + c4:516 + c4 + 1], psS_[1][:, c4 * 128:(c4 + 1) * 128], mybir.AxisListType.X, ALU.min)
            return v.drain()
        V(_red)
        def _trr(te):
            w(te, 'V')
            last = None
            for c4 in range(4):
                last = te.matmul(psS_[0][0:1, c4 * 128:(c4 + 1) * 128], gst.t_[:, 512 + c4:513 + c4],
                                 identf[:, :], start=True, stop=True)
                last = te.matmul(psS_[1][0:1, c4 * 128:(c4 + 1) * 128], gst.t_[:, 516 + c4:517 + c4],
                                 identf[:, :], start=True, stop=True)
            return last
        T(_trr)
        def _rows_pre(v):
            w(v, 'T', 'S')
            v.tensor_copy(rows32[0:1, 1024:1536], rowt[0:1, RW_M:RW_M + 512])
            v.tensor_copy(rows32[0:1, 512:1024], rowt[0:1, RW_RSTD:RW_RSTD + 512])
            return v.drain()
        V(_rows_pre)
        def _trm(te):
            w(te, 'V')
            last = None
            for c4 in range(4):
                last = te.transpose(psV_[0][:, c4 * 32:(c4 + 1) * 32],
                                    rows32[0:32, 1024 + c4 * 128:1024 + (c4 + 1) * 128], identf[0:32, 0:32])
                last = te.transpose(psV_[1][:, c4 * 32:(c4 + 1) * 32],
                                    rows32[0:32, 512 + c4 * 128:512 + (c4 + 1) * 128], identf[0:32, 0:32])
            return last
        T(_trm)
        def _rows_col(v):
            w(v, 'T')
            mcol = mkap(psV_[0], 0, [[32, 4]])
            rcol = mkap(psV_[1], 0, [[32, 4]])
            tmax = gst.t_[:, 512:516]
            tmin = gst.t_[:, 516:520]
            a1 = gst.t_[:, 520:524]
            a2 = gst.t_[:, 524:528]
            a3 = gst.t_[:, 528:532]
            rqc = gst.t_[:, 532:536]
            inscc = gst.t_[:, 536:540]
            nmc = gst.t_[:, 540:544]
            v.tensor_tensor(a1, tmax, mcol, ALU.subtract)
            v.tensor_tensor(a2, mcol, tmin, ALU.subtract)
            v.drain()
            v.tensor_max(a1, a1, a2)
            v.drain()
            v.reciprocal(a2, a1)
            v.tensor_tensor(a3, a1, rcol, ALU.mult)
            v.drain()
            v.tensor_scalar(rqc, a2, 127.0, None, ALU.mult)
            v.tensor_scalar(inscc, a3, 1e-7, 1.0 / 127.0, ALU.max, ALU.mult)
            v.drain()
            v.tensor_tensor(nmc, rqc, mcol, ALU.mult)
            v.drain()
            v.tensor_scalar(nmc, nmc, -1.0, None, ALU.mult)
            v.drain()
            v.tensor_scalar(nmc, nmc, cpool[:, CPQ:CPQ + 1], None, ALU.add)
            return v.drain()
        V(_rows_col)
        def _trr2(te):
            w(te, 'V')
            last = None
            for c4 in range(4):
                last = te.transpose(psS_[0][0:1, c4 * 128:(c4 + 1) * 128], gst.t_[:, 532 + c4:533 + c4], identf[:, :])
                last = te.transpose(psS_[1][0:1, c4 * 128:(c4 + 1) * 128], gst.t_[:, 540 + c4:541 + c4], identf[:, :])
                last = te.transpose(psV_[0][0:1, c4 * 128:(c4 + 1) * 128], gst.t_[:, 536 + c4:537 + c4], identf[:, :])
            return last
        T(_trr2)
        def _rows_fin(v):
            w(v, 'T')
            v.tensor_copy(rows32[0:1, 0:512], psS_[0][0:1, 0:512])
            v.tensor_copy(rows32[0:1, 512:1024], psS_[1][0:1, 0:512])
            v.tensor_copy(rows32[0:1, 1024:1536], psV_[0][0:1, 0:512])
            return v.drain()
        V(_rows_fin)
        mset('ppqf1', 'V')
        def _bx(te):
            w(te, 'V')
            te.matmul(psP[0][:, :], os32f[:], rows32[:, 0:512], start=True, stop=True)
            te.matmul(psP[1][:, :], os32f[:], rows32[:, 512:1024], start=True, stop=True)
            return te.matmul(psS_[0][:, :], os32f[:], rows32[:, 1024:1536], start=True, stop=True)
        T(_bx)
        def _b3c(v):
            w(v, 'T')
            v.tensor_copy(gst.t_[:, 0:512], psS_[0][:, :])
            return v.drain()
        V(_b3c)
        mset('ppqf0', 'V')
        qbase_s = CN['S']
        vq_marks = []
        for j in range(NT):
            def _qz(v, j=j, qs=qbase_s):
                w(v, 'T')
                if j >= 2:
                    v.wait_ge(SEMH['S'], qs + j - 1)
                p = 1024
                v.tensor_mul(gst.t_[:, p:p + 512], Xcur.t(j), psP[0][:, :])
                v.drain()
                it = itmp if j % 2 == 0 else itmp2
                v.tensor_tensor(it[:, 0:512], gst.t_[:, p:p + 512], psP[1][:, :], ALU.add)
                return v.drain()
            V(_qz)
            vq_marks.append(CN['V'])
        def _qcasts(sc, vq_marks=tuple(vq_marks)):
            for j in range(NT):
                sc.wait_ge(SEMH['V'], vq_marks[j])
                it = itmp if j % 2 == 0 else itmp2
                sc.copy(ybuf.t(j), it[:, 0:512]).then_inc(SEMH['S'], 1)
            return None
        sec('S', _qcasts, None)
        CN['S'] += NT
        mset('q128', 'S')
        mset('ppqf2', 'V')
        mset('ppqf3', 'V')
        for s_ in range(4):
            def _ldf(gp, s_=s_):
                mwait(gp, 'wsb%d' % (s_ % 2))
                stream_weight(gp, wf8T, ws[s_ % 2], NT, 512, s_ * 512)
                return None
            G(_ldf)
            for ot in range(4):
                og = s_ * 4 + ot
                def _mmf(te, s_=s_, ot=ot, og=og, first_block=first_block, qs=qbase_s):
                    w(te, 'dG')
                    mwait(te, 'ppqf%d' % (og % 4))
                    pp = BKF[og % 4]
                    last = None
                    if s_ == 0:
                        for j in range(NT):
                            te.wait_ge(SEMH['S'], qs + j + 1)
                            last = te.matmul(pp[:, :], ws[s_ % 2].tc(j, ot * 128, (ot + 1) * 128),
                                             ybuf.t(j), start=(j == 0), stop=(j == NT - 1))
                    else:
                        mwait(te, 'q128')
                        for j in range(NT):
                            last = te.matmul(pp[:, :], ws[s_ % 2].tc(j, ot * 128, (ot + 1) * 128),
                                             ybuf.t(j), start=(j == 0), stop=(j == NT - 1))
                    if first_block:
                        for j in range(NT):
                            last = te.matmul(psU[:, og:og + 1], ws[s_ % 2].tc(j, ot * 128, (ot + 1) * 128),
                                             onecolb[:], start=(j == 0), stop=(j == NT - 1))
                    return last
                T(_mmf)
                if first_block:
                    def _c1(v, og=og):
                        w(v, 'T')
                        v.tensor_scalar(c1col[:, og:og + 1], psU[:, og:og + 1], -128.0, None, ALU.mult)
                        return v.drain()
                    V(_c1)
                    mset('psU', 'V')
                def _dq(v, og=og):
                    w(v, 'T')
                    pp = BKF[og % 4]
                    v.scalar_tensor_tensor(gst.t_[:, 1024:1536], pp[:, :], c1col[:, og:og + 1], gst.t_[:, 0:512], ALU.add, ALU.mult)
                    v.drain()
                    v.tensor_scalar(f16st[:], gst.t_[:, 1024:1536], cws[:, og:og + 1], cfb[:, og:og + 1], ALU.mult, ALU.add)
                    v.drain()
                    v.tensor_copy(gst.t_[:, 1024:1536], f16st[:])
                    v.drain()
                    v.tensor_add(Xcur.t(og), Xcur.t(og), gst.t_[:, 1024:1536])
                    return v.drain()
                V(_dq)
                dqm.append(CN['V'])
                mset('ppqf%d' % (og % 4), 'V')
                if ot == 3:
                    mset('wsb%d' % (s_ % 2), 'T')
        mset('ppq0', 'V')
        mset('ppq1', 'V')
        return [dqm[st4 * 4 + 3] for st4 in range(4)]


    def emit_stageA_head(b, xdw=None):
        if xdw is None:
            def _ldx(sy, b=b):
                src3 = xT[:, b * BLK:(b + 1) * BLK].rearrange('(n p) c -> p n c', p=128)
                dst3 = Xa.t_[:, :].rearrange('p (n c) -> p n c', c=512)
                dma(sy, dst3, src3, 'dW')
                return None
            Y(_ldx)

            def _wx(v):
                w(v, 'V', 'dW')
                return v.drain()
            V(_wx)

            ln_stats(Xa, ybf)
        else:
            ln_stats(Xa, ybf, strip_dw=xdw)
        aplA = ln_apply(Xa, ybf)
        mset('yA', 'V')

        def eluq(q):
            t = mst if q < 2 else mst2
            return t[:, (q % 2) * 512:(q % 2) * 512 + 512]

        def rope2(v, pp, dst, tt, q):
            cosb = tabAc[:, (b * 4 + tt) * 128:(b * 4 + tt + 1) * 128]
            sinb = tabAs[:, (b * 4 + tt) * 128:(b * 4 + tt + 1) * 128]
            nh4 = dst.shape[1] // 128
            cos4 = mkap(cosb, 0, [[0, nh4], [64, 2], [1, 64]])
            sin4 = mkap(sinb, 0, [[0, nh4], [64, 2], [1, 64]])
            swp = mkap(pp, 64, [[128, nh4], [-64, 2], [1, 64]])
            pp4 = mkap(pp, 0, [[128, nh4], [64, 2], [1, 64]])
            d4 = mkap(dst, 0, [[128, nh4], [64, 2], [1, 64]])
            mh = eluq(q)
            m1 = mkap(mh, 0, [[128, nh4], [64, 2], [1, 64]])
            v.tensor_tensor(d4, pp4, cos4, ALU.mult)
            v.tensor_tensor(m1, swp, sin4, ALU.mult)
            v.drain()
            v.tensor_add(dst, dst, mh)
            v.drain()
            v.tensor_scalar(mh, dst, 0.0, None, ALU.min)
            v.drain()

        for s in range(4):
            def _ldw(gp, s=s):
                mwait(gp, 'wsb%d' % (s % 2))
                stream_weight(gp, wqT, WS[s % 2], NT, 512, s * 512)
                return None
            G(_ldw)
            vmarks = []
            sbase_e = CN['S']
            for tt in range(4):
                def _mmq(te, s=s, tt=tt, se=sbase_e, apl=tuple(aplA)):
                    w(te, 'dG')
                    if s == 0:
                        mwait(te, 'ppq%d' % (tt % 2))
                    else:
                        mwait(te, 'yA', 'ppq%d' % (tt % 2))
                    last = None
                    pp = psP[tt % 2]
                    for j in range(NT):
                        if s == 0:
                            te.wait_ge(SEMH['V'], apl[j // 2])
                        last = te.matmul(pp[:, :], ybf.tc(j, tt * 128, (tt + 1) * 128), WS[s % 2].t(j),
                                         start=(j == 0), stop=(j == NT - 1))
                    return last
                T(_mmq)
                def _rope_q(v, s=s, tt=tt, se=sbase_e):
                    w(v, 'T')
                    rope2(v, psP[tt % 2], qfB.tc(tt, s * 512, (s + 1) * 512), tt, tt)
                    return v.drain()
                V(_rope_q)
                mset('ppq%d' % (tt % 2), 'V')
                vmarks.append(CN['V'])
            def _ees(sc, vmarks=tuple(vmarks)):
                for tt in range(4):
                    sc.wait_ge(SEMH['V'], vmarks[tt])
                    sc.activation(eluq(tt), eluq(tt), AF.Exp).then_inc(SEMH['S'], 1)
                return None
            sec('S', _ees, None)
            CN['S'] += 4
            def _efs(v, s=s, se=sbase_e):
                for tt in range(4):
                    v.wait_ge(SEMH['S'], se + tt + 1)
                    v.scalar_tensor_tensor(qfB.tc(tt, s * 512, (s + 1) * 512),
                                           qfB.tc(tt, s * 512, (s + 1) * 512), 0.0,
                                           eluq(tt), ALU.max, ALU.add)
                    v.drain()
                return v.drain()
            V(_efs)
            mset('wsb%d' % (s % 2), 'T')

        def _ldk(gp):
            mwait(gp, 'wsb0')
            stream_weight(gp, wkT, WS[0], NT, 512, 0)
            return None
        G(_ldk)
        vmarks = []
        sbase_e = CN['S']
        for tt in range(4):
            def _mmk(te, tt=tt, se=sbase_e):
                w(te, 'dG')
                mwait(te, 'yA', 'ppq%d' % (tt % 2))
                last = None
                pp = psP[tt % 2]
                for j in range(NT):
                    last = te.matmul(pp[:, :], ybf.tc(j, tt * 128, (tt + 1) * 128), WS[0].t(j),
                                     start=(j == 0), stop=(j == NT - 1))
                return last
            T(_mmk)
            def _rope_k(v, tt=tt, se=sbase_e):
                w(v, 'T')
                rope2(v, psP[tt % 2], kfB.t(tt), tt, tt)
                return v.drain()
            V(_rope_k)
            mset('ppq%d' % (tt % 2), 'V')
            vmarks.append(CN['V'])
        def _eeks(sc, vmarks=tuple(vmarks)):
            for tt in range(4):
                sc.wait_ge(SEMH['V'], vmarks[tt])
                sc.activation(eluq(tt), eluq(tt), AF.Exp).then_inc(SEMH['S'], 1)
            return None
        sec('S', _eeks, None)
        CN['S'] += 4
        def _efks(v, se=sbase_e):
            for tt in range(4):
                v.wait_ge(SEMH['S'], se + tt + 1)
                v.scalar_tensor_tensor(kfB.t(tt), kfB.t(tt), 0.0,
                                       eluq(tt), ALU.max, ALU.add)
                v.drain()
            return v.drain()
        V(_efks)

        mset('wsb0', 'T')
        def _ldv(gp):
            mwait(gp, 'wsb1')
            stream_weight(gp, wvT, WS[1], NT, 512, 0)
            return None
        G(_ldv)
        for tt in range(4):
            def _mmv(te, tt=tt):
                w(te, 'dG')
                mwait(te, 'yA', 'ppq%d' % (tt % 2))
                last = None
                pp = psP[tt % 2]
                for j in range(NT):
                    last = te.matmul(pp[:, :], ybf.tc(j, tt * 128, (tt + 1) * 128), WS[1].t(j),
                                     start=(j == 0), stop=(j == NT - 1))
                return last
            T(_mmv)
            def _ev_v(v, tt=tt):
                w(v, 'T')
                pp = psP[tt % 2]
                src = pp[:, :].rearrange('p (h c) -> p h c', c=128)
                dstv = mkap(vaug.t_, tt * 516, [[129, 4], [1, 128]])
                v.tensor_copy(dstv, src)
                return v.drain()
            V(_ev_v)
            mset('ppq%d' % (tt % 2), 'V')

        mset('wsb1', 'T')
        for h2 in range(0, NH, 2):
            def _tr_q(te, h2=h2):
                w(te, 'V')
                last = None
                for hh in range(2):
                    for tt in range(4):
                        last = te.transpose(psT_[:, hh * 512 + tt * 128:hh * 512 + (tt + 1) * 128],
                                            qfB.tc(tt, (h2 + hh) * 128, (h2 + hh + 1) * 128), ident[:])
                return last
            T(_tr_q)
            def _ev_q(v, h2=h2):
                w(v, 'T')
                v.tensor_copy(qfA.t(h2), psT_[:, 0:512])
                v.tensor_copy(qfA.t(h2 + 1), psT_[:, 512:1024])
                return v.drain()
            V(_ev_q)
        for h2 in range(0, KVH, 2):
            def _tr_k(te, h2=h2):
                w(te, 'V')
                last = None
                for hh in range(2):
                    for tt in range(4):
                        last = te.transpose(psT_[:, hh * 512 + tt * 128:hh * 512 + (tt + 1) * 128],
                                            kfB.tc(tt, (h2 + hh) * 128, (h2 + hh + 1) * 128), ident[:])
                return last
            T(_tr_k)
            def _ev_k(v, h2=h2):
                w(v, 'T')
                v.tensor_copy(kfA.t(h2), psT_[:, 0:512])
                v.tensor_copy(kfA.t(h2 + 1), psT_[:, 512:1024])
                return v.drain()
            V(_ev_k)
        mset('qkvA', 'V')



    scdiv = [0, 0, 0, 0]
    scS_pred = [0, 0]
    def emit_scan(bglob):
        AVB = [psV_[0], psV_[1], psP[0], psP[1]]
        for cc in range(4):
            sdbase = CN['S']
            tbase = CN['T']
            vbase = CN['V']
            vupd = CN['V']
            sdw = [(sdbase + hg - 3) if hg >= 4 else scdiv[hg % 4] for hg in range(8)]
            scw = list(scS_pred)
            def _scanT(te, cc=cc, tb=tbase, vb=vbase, sdw=tuple(sdw), scw=tuple(scw)):
                mwait(te, 'qkvA')
                for hg in range(8):
                    h0 = hg * 2
                    kv0, kv1 = h0 // 4, (h0 + 1) // 4
                    sw = (vb + 2 * (hg - 2) + 1) if hg >= 2 else scw[hg % 2]
                    if sw > 0:
                        te.wait_ge(SEMH['V'], sw)
                    pp = psS_[hg % 2]
                    te.matmul(pp[:, 0:128], kfA.tc(kv0, cc * 128, (cc + 1) * 128),
                              qfA.tc(h0, cc * 128, (cc + 1) * 128), start=True, stop=True)
                    te.matmul(pp[:, 128:256], kfA.tc(kv1, cc * 128, (cc + 1) * 128),
                              qfA.tc(h0 + 1, cc * 128, (cc + 1) * 128), start=True,
                              stop=True).then_inc(SEMH['T'], 1)
                    te.wait_ge(SEMH['V'], vb + 2 * hg + 1)
                    if sdw[hg] > 0:
                        te.wait_ge(SEMH['S'], sdw[hg])
                    ppa = AVB[hg % 4]
                    ms0 = mst[:, (hg % 2) * 512:(hg % 2) * 512 + 128]
                    ms1 = mst[:, (hg % 2) * 512 + 128:(hg % 2) * 512 + 256]
                    te.matmul(ppa[:, 0:129], ms0,
                              vaug.t_[:, cc * 516 + kv0 * 129: cc * 516 + kv0 * 129 + 129],
                              start=True, stop=False)
                    te.matmul(ppa[:, 0:129], qfA.tc(h0, cc * 128, (cc + 1) * 128),
                              stbf[:, kv0 * 129:(kv0 + 1) * 129], start=False, stop=True)
                    te.matmul(ppa[:, 256:385], ms1,
                              vaug.t_[:, cc * 516 + kv1 * 129: cc * 516 + kv1 * 129 + 129],
                              start=True, stop=False)
                    te.matmul(ppa[:, 256:385], qfA.tc(h0 + 1, cc * 128, (cc + 1) * 128),
                              stbf[:, kv1 * 129:(kv1 + 1) * 129], start=False,
                              stop=True).then_inc(SEMH['T'], 1)
                return None
            sec('T', _scanT, None)
            CN['T'] += 16
            for hg in range(8):
                def _mask(v, hg=hg, tb=tbase):
                    v.wait_ge(SEMH['T'], tb + 2 * hg + 1)
                    pp = psS_[hg % 2]
                    src = mkap(pp, 0, [[128, 2], [1, 128]])
                    mk = mkap(maskT, 0, [[0, 2], [1, 128]])
                    dst = mkap(mst, (hg % 2) * 512, [[128, 2], [1, 128]])
                    v.tensor_tensor(dst, src, mk, ALU.mult)
                    return v.drain()
                V(_mask)
                def _recip(v, cc=cc, hg=hg, tb=tbase):
                    v.wait_ge(SEMH['T'], tb + 2 * hg + 2)
                    pp = AVB[hg % 4]
                    dn = mkap(pp, 128, [[256, 2], [1, 1]])
                    v.tensor_scalar(gst.t_[:, 1536 + hg * 4:1538 + hg * 4], dn, EPS_A, None, ALU.add)
                    v.drain()
                    v.reciprocal(gst.t_[:, 1538 + hg * 4:1540 + hg * 4], gst.t_[:, 1536 + hg * 4:1538 + hg * 4])
                    return v.drain()
                V(_recip)
            for hg in range(8):
                scS_pred[hg % 2] = vbase + 2 * hg + 1
            def _sdivs(sc, cc=cc, vb=vbase):
                for hg in range(8):
                    h0 = hg * 2
                    sc.wait_ge(SEMH['V'], vb + 2 * hg + 2)
                    pp = AVB[hg % 4]
                    for i in range(2):
                        ins = sc.activation(qfB.tc(cc, (h0 + i) * 128, (h0 + i + 1) * 128),
                                            pp[:, i * 256:i * 256 + 128], AF.Identity,
                                            scale=gst.t_[:, 1538 + hg * 4 + i:1539 + hg * 4 + i])
                        if i == 1:
                            ins.then_inc(SEMH['S'], 1)
                return None
            sec('S', _sdivs, None)
            CN['S'] += 8
            for hg in range(8):
                scdiv[hg % 4] = sdbase + hg + 1
            for rnd in range(2):
                def _mmu(te, cc=cc, rnd=rnd):
                    mwait(te, 'psU')
                    kv0, kv1 = rnd * 2, rnd * 2 + 1
                    te.matmul(psU[:, 0:129], kfB.tc(cc, kv0 * 128, (kv0 + 1) * 128),
                              vaug.t_[:, cc * 516 + kv0 * 129: cc * 516 + kv0 * 129 + 129],
                              start=True, stop=True)
                    return te.matmul(psU[:, 256:385], kfB.tc(cc, kv1 * 128, (kv1 + 1) * 128),
                                     vaug.t_[:, cc * 516 + kv1 * 129: cc * 516 + kv1 * 129 + 129],
                                     start=True, stop=True)
                T(_mmu)
                def _upd(v, rnd=rnd):
                    w(v, 'T')
                    kv0, kv1 = rnd * 2, rnd * 2 + 1
                    v.tensor_add(stLa[:, kv0 * 129:(kv0 + 1) * 129], stLa[:, kv0 * 129:(kv0 + 1) * 129], psU[:, 0:129])
                    v.tensor_add(stLa[:, kv1 * 129:(kv1 + 1) * 129], stLa[:, kv1 * 129:(kv1 + 1) * 129], psU[:, 256:385])
                    v.drain()
                    v.tensor_copy(stbf[:, kv0 * 129:(kv0 + 1) * 129], stLa[:, kv0 * 129:(kv0 + 1) * 129])
                    v.tensor_copy(stbf[:, kv1 * 129:(kv1 + 1) * 129], stLa[:, kv1 * 129:(kv1 + 1) * 129])
                    return v.drain()
                V(_upd)
                mset('psU', 'V')

    def emit_out_wo(bglob):
        for h2 in range(0, NH, 2):
            def _tr_o(te, h2=h2):
                w(te, 'V', 'S')
                last = None
                for hh in range(2):
                    for tt in range(4):
                        last = te.transpose(psT_[:, hh * 512 + tt * 128:hh * 512 + (tt + 1) * 128],
                                            qfB.tc(tt, (h2 + hh) * 128, (h2 + hh + 1) * 128), ident[:])
                return last
            T(_tr_o)
            def _ev_o(v, h2=h2):
                w(v, 'T')
                v.tensor_copy(qfA.t(h2), psT_[:, 0:512])
                v.tensor_copy(qfA.t(h2 + 1), psT_[:, 512:1024])
                return v.drain()
            V(_ev_o)
        mset('outA', 'V')
        for s_ in range(4):
            def _ldwo(gp, s_=s_):
                mwait(gp, 'wsb%d' % (s_ % 2))
                stream_weight(gp, woaT, WS[s_ % 2], NT, 512, s_ * 512)
                return None
            G(_ldwo)
            for dt_ in range(4):
                dmt = s_ * 4 + dt_
                def _mmo(te, s_=s_, dt_=dt_, dmt=dmt):
                    w(te, 'dG')
                    mwait(te, 'outA', 'ppq%d' % (dmt % 2))
                    pp = psP[dmt % 2]
                    last = None
                    for j in range(NT):
                        last = te.matmul(pp[:, :], WS[s_ % 2].tc(j, dt_ * 128, (dt_ + 1) * 128),
                                         qfA.t(j), start=(j == 0), stop=(j == NT - 1))
                    return last
                T(_mmo)
                def _res(v, dmt=dmt):
                    w(v, 'T')
                    v.tensor_add(Xa.t(dmt), Xa.t(dmt), psP[dmt % 2][:, :])
                    v.drain()
                    v.tensor_copy(ybf.t(dmt), Xa.t(dmt))
                    if dmt == 0:
                        v.tensor_copy(gst.t_[:, 0:512], Xa.t(0))
                        v.tensor_copy(gst.t_[:, 1024:1536], Xa.t(0))
                    else:
                        v.tensor_max(gst.t_[:, 0:512], gst.t_[:, 0:512], Xa.t(dmt))
                        v.tensor_tensor(gst.t_[:, 1024:1536], gst.t_[:, 1024:1536], Xa.t(dmt), ALU.min)
                    return v.drain()
                V(_res)
                mset('ppq%d' % (dmt % 2), 'V')
            mset('wsb%d' % (s_ % 2), 'T')


    def emit_stageA(milestone):
        xdw = None
        for b in range(NB):
            emit_stageA_head(b, xdw)
            if milestone == 'proj' and b == 0:
                return 'stop'
            emit_scan(b)
            mset('ppq0', 'S')
            mset('ppq1', 'S')
            emit_out_wo(b)
            if milestone == 'attnA' and b == 0:
                return 'stop'
            dqm = emit_ffn(Xa, ybf, WS, first_block=(b == 0), prefused=True)
            if milestone == 'ffn1' and b == 0:
                return 'stop'
            xdw = [] if b + 1 < NB else None
            for st4 in range(4):
                def _stx(sy, b=b, st4=st4, vw=dqm[st4]):
                    sy.wait_ge(s_V, vw)
                    src3 = Xa.t_[:, st4 * 2048:(st4 + 1) * 2048].rearrange('p (n c) -> p n c', c=512)
                    dst3 = x2s[st4 * 512:(st4 + 1) * 512, b * BLK:(b + 1) * BLK].rearrange('(n p) c -> p n c', p=128)
                    dma(sy, dst3, src3, 'dO')
                    return None
                Y(_stx)
                if b + 1 < NB:
                    def _ldxs(sy, b=b, st4=st4):
                        src3 = xT[st4 * 512:(st4 + 1) * 512, (b + 1) * BLK:(b + 2) * BLK].rearrange('(n p) c -> p n c', p=128)
                        dst3 = Xa.t_[:, st4 * 2048:(st4 + 1) * 2048].rearrange('p (n c) -> p n c', c=512)
                        dma(sy, dst3, src3, 'dW')
                        return None
                    Y(_ldxs)
                    xdw.append(CN['dW'])
        return None

    r = emit_stageA(stop_after)
    if r == 'stop':
        if stop_after == 'proj':
            def _fin(sy):
                w(sy, 'S', 'V', 'T')
                for nm, src, sh, dt in (('qfA', qfA.t_[:, :], [128, NT * 512], BF16),
                                    ('kfA', kfA.t_[:, :], [128, 4 * 512], BF16),
                                    ('vaug', vaug.t_[:, :], [128, 4 * 516], BF16),
                                    ('ybf', ybf.t_[:, :], [128, NT * 512], BF16),
                                    ('m', rowt[0:1, 0:512], [1, 512], F32),
                                    ('rstd', rowt[0:1, 512:1024], [1, 512], F32)):
                    d = dbg(nm, sh, dt)
                    if d is not None:
                        dma(sy, d, src, 'dO')
                sy.wait_ge(s_dO, CN['dO'])
                return None
            Y(_fin)
            ctx.close()
            return nc, dbg_list


        if stop_after == 'attnA':
            def _fin(sy):
                w(sy, 'S', 'V', 'T')
                d = dbg('X1', [128, NT * 512], F32)
                if d is not None:
                    dma(sy, d, Xa.t_[:, :], 'dO')
                sy.wait_ge(s_dO, CN['dO'])
                return None
            Y(_fin)
            ctx.close()
            return nc, dbg_list


        if stop_after == 'ffn1':
            def _fin(sy):
                w(sy, 'S', 'V', 'T')
                d = dbg('X2', [128, NT * 512], F32)
                if d is not None:
                    dma(sy, d, Xa.t_[:, :], 'dO')
                sy.wait_ge(s_dO, CN['dO'])
                return None
            Y(_fin)
            ctx.close()
            return nc, dbg_list


    if stop_after == 'stageA':
        def _fin(sy):
            w(sy, 'S', 'V', 'T')
            sy.wait_ge(SEMH['dO'], CN['dO'])
            d = dbg('X2full', [D, S_], F32)
            if d is not None:
                dma(sy, d, x2s[:, :], 'dO')
            sy.wait_ge(s_dO, CN['dO'])
            return None
        Y(_fin)
        ctx.close()
        return nc, dbg_list

    stA_scope.close()
    stAB = ctx.enter_context(contextlib.ExitStack())
    latq = Mat(sbt('latq', [128, 4 * 2048], BF16, stAB), 4, 2048)
    vaugB = Mat(sbt('vaugB', [128, 16 * 2064], BF16, stAB), 16, 2064)

    stA2 = ctx.enter_context(contextlib.ExitStack())
    Xa2 = Mat(sbt('Xa2', [128, NT * 512], F32, stA2), NT, 512)
    y3 = Mat(sbt('y3', [128, NT * 512], BF16, stA2), NT, 512)
    kvlat = Mat(sbt('kvlat', [128, 4 * 512], BF16, stA2), 4, 512)
    tabBc = sbt('tabBc', [128, 4 * 512], BF16, stA2)
    tabBs = sbt('tabBs', [128, 4 * 512], BF16, stA2)
    wsA2 = Mat(wsA2t, NT, 512)
    ws_kn8 = Mat(wsA2t, 8, 1024)
    ws_vu = Mat(wsA2t, 4, 2048)
    BK4 = [psP[0], psP[1], psS_[0], psS_[1]]

    def _a2init(v):
        w(v, 'V', 'T', 'S', 'dO', 'dW', 'dG')
        v.memset(mkap(vaugB.t_, 128, [[2064, 16], [129, 16], [1, 1]]), 1.0)
        return v.drain()
    V(_a2init)
    for b in range(NB):
        def _ldtb(sy, b=b):
            if b > 0:
                mwait(sy, 'ekrdone')
            dma(sy, tabBc[:, :], tabBcin[:, b * 2048:(b + 1) * 2048], 'dW')
            dma(sy, tabBs[:, :], tabBsin[:, b * 2048:(b + 1) * 2048], 'dW')
            return None
        Y(_ldtb)
        if b == 0:
            def _ldx2(sy, b=b):
                w(sy, 'V', 'dO')
                src3 = x2s[:, b * BLK:(b + 1) * BLK].rearrange('(n p) c -> p n c', p=128)
                dst3 = Xa2.t_[:, :].rearrange('p (n c) -> p n c', c=512)
                dma(sy, dst3, src3, 'dW')
                return None
            Y(_ldx2)
            def _wx2(v):
                w(v, 'V', 'dW')
                return v.drain()
            V(_wx2)
            ln_stats(Xa2, y3)
            vmst_a2 = CN['V']
            aplA2 = ln_apply(Xa2, y3)
            mset('yA', 'V')
            def _ldx2n(sy):
                mwait(sy, 'yA')
                src3 = x2s[:, 1 * BLK:2 * BLK].rearrange('(n p) c -> p n c', p=128)
                dst3 = Xa2.t_[:, :].rearrange('p (n c) -> p n c', c=512)
                dma(sy, dst3, src3, 'dW')
                return None
            Y(_ldx2n)
        for wsrc, bias, dst in ((wqdT, cbqd, 'q'), (wkvdT, cbkvd, 'kv')):
            def _ldd0(gp, wsrc=wsrc):
                mwait(gp, 'a2h0')
                stream_weight_rt(gp, wsrc, wsA2, 0, 8, 0, 512, 0)
                return None
            G(_ldd0)
            dg0 = CN['dG']
            def _ldd1(gp, wsrc=wsrc):
                mwait(gp, 'a2h1')
                stream_weight_rt(gp, wsrc, wsA2, 8, 8, 8, 512, 0)
                return None
            G(_ldd1)
            dg1 = CN['dG']
            for lt in range(4):
                def _mmd0(te, lt=lt, dg0=dg0, first=(dst == 'q'), apl=tuple(aplA2)):
                    te.wait_ge(SEMH['dG'], dg0)
                    if first:
                        mwait(te, 'ppq%d' % lt)
                    else:
                        mwait(te, 'yA', 'ppq%d' % lt)
                    pp = BK4[lt]
                    last = None
                    for j in range(8):
                        if first:
                            te.wait_ge(SEMH['V'], apl[j // 2])
                        last = te.matmul(pp[:, :], wsA2.tc(j, lt * 128, (lt + 1) * 128), y3.t(j),
                                         start=(j == 0), stop=False)
                    return last
                T(_mmd0)
            mset('a2h0', 'T')
            for lt in range(4):
                def _mmd1(te, lt=lt, dg1=dg1, first=(dst == 'q'), apl=tuple(aplA2)):
                    te.wait_ge(SEMH['dG'], dg1)
                    pp = BK4[lt]
                    last = None
                    for j in range(8, NT):
                        if first:
                            te.wait_ge(SEMH['V'], apl[j // 2])
                        last = te.matmul(pp[:, :], wsA2.tc(j, lt * 128, (lt + 1) * 128), y3.t(j),
                                         start=False, stop=(j == NT - 1))
                    return last
                T(_mmd1)
                def _evd(v, lt=lt, bias=bias, dst=dst, b=b):
                    w(v, 'T')
                    pp = BK4[lt]
                    if dst == 'q':
                        v.tensor_scalar(latq.tc(lt, b * 512, (b + 1) * 512), pp[:, :], bias[:, lt:lt + 1], None, ALU.add)
                    else:
                        v.tensor_scalar(kvlat.t(lt), pp[:, :], bias[:, lt:lt + 1], None, ALU.add)
                    return v.drain()
                V(_evd)
                mset('ppq%d' % lt, 'V')
            mset('a2h1', 'T')
        mset('kvlat', 'V')
        def _ldkn(gp):
            mwait(gp, 'a2h0')
            stream_weight_rt(gp, wknT, ws_kn8, 0, 4, 0, 1024, 0)
            return None
        G(_ldkn)
        dgm_kn = CN['dG']
        sbase_kn = CN['S']
        tpred_kn = []
        for ot in range(8):
            def _mkn(te, ot=ot, dgm=dgm_kn, sb=sbase_kn):
                te.wait_ge(SEMH['dG'], dgm)
                if ot >= 4:
                    te.wait_ge(SEMH['S'], sb + ot - 3)
                    mwait(te, 'kvlat')
                else:
                    mwait(te, 'kvlat', 'ppq%d' % (ot % 4))
                pp = BK4[ot % 4]
                last = None
                for lt in range(4):
                    last = te.matmul(pp[:, :], ws_kn8.tc(lt, ot * 128, (ot + 1) * 128), kvlat.t(lt),
                                     start=(lt == 0), stop=(lt == 3))
                return last
            T(_mkn)
            tpred_kn.append(CN['T'])
            if ot == 7:
                mset('a2h0', 'T')
        dO0_kn = CN['dO']
        def _ekns(sc, tpred=tuple(tpred_kn), sb=sbase_kn, dO0=dO0_kn, vm=vmst_a2):
            sc.wait_ge(SEMH['V'], vm)
            sc.wait_ge(s_dO, dO0)
            for ot in range(8):
                sc.wait_ge(SEMH['T'], tpred[ot])
                if ot >= 2:
                    sc.wait_ge(s_dO, dO0 + (ot - 1) * 16)
                sc.activation(mst[:, (ot % 2) * 512:(ot % 2) * 512 + 512], BK4[ot % 4][:, :], AF.Identity,
                              bias=cbkn[:, ot:ot + 1]).then_inc(SEMH['S'], 1)
            return None
        sec('S', _ekns, None)
        CN['S'] += 8
        for ot in range(4, 8):
            MK['ppq%d' % (ot % 4)] = ('S', sbase_kn + ot + 1)
        for ot in range(8):
            def _skn(sy, ot=ot, b=b, sb=sbase_kn):
                sy.wait_ge(SEMH['S'], sb + ot + 1)
                dma(sy, kAs[ot * 128:(ot + 1) * 128, b * BLK:(b + 1) * BLK], mst[:, (ot % 2) * 512:(ot % 2) * 512 + 512], 'dO')
                return None
            Y(_skn)
        def _ldkr(gp):
            mwait(gp, 'a2h1')
            stream_weight_rt(gp, wkrT, ws_kn8, 0, 4, 4, 1024, 0)
            return None
        G(_ldkr)
        dgm_kr = CN['dG']
        for jt in range(4):
            def _mkr(te, jt=jt, dgm=dgm_kr):
                te.wait_ge(SEMH['dG'], dgm)
                mwait(te, 'kvlat', 'ppq0', 'ppq1')
                last = None
                for lt in range(4):
                    last = te.matmul(psP[0][:, :], ws_kn8.tc(4 + lt, jt * 128, (jt + 1) * 128), kvlat.t(lt),
                                     start=(lt == 0), stop=(lt == 3))
                for lt in range(4):
                    last = te.matmul(psP[1][:, :], ws_kn8.tc(4 + lt, (jt + 4) * 128, (jt + 5) * 128), kvlat.t(lt),
                                     start=(lt == 0), stop=(lt == 3))
                return last
            T(_mkr)
            if jt == 3:
                mset('a2h1', 'T')
            def _ekr(v, jt=jt):
                w(v, 'T', 'dO', 'S', 'dW')
                v.tensor_scalar(gst.t_[:, 0:512], psP[0][:, :], cbkr[:, jt:jt + 1], None, ALU.add)
                v.tensor_scalar(gst.t_[:, 512:1024], psP[1][:, :], cbkr[:, jt + 4:jt + 5], None, ALU.add)
                v.drain()
                v.tensor_mul(gst.t_[:, 1024:1536], gst.t_[:, 0:512], tabBc[:, jt * 512:(jt + 1) * 512])
                v.tensor_mul(gst.t_[:, 1536:2048], gst.t_[:, 512:1024], tabBs[:, jt * 512:(jt + 1) * 512])
                v.drain()
                v.tensor_sub(mst[:, 0:512], gst.t_[:, 1024:1536], gst.t_[:, 1536:2048])
                v.drain()
                v.tensor_mul(gst.t_[:, 1024:1536], gst.t_[:, 512:1024], tabBc[:, jt * 512:(jt + 1) * 512])
                v.tensor_mul(gst.t_[:, 1536:2048], gst.t_[:, 0:512], tabBs[:, jt * 512:(jt + 1) * 512])
                v.drain()
                v.tensor_add(mst[:, 512:1024], gst.t_[:, 1024:1536], gst.t_[:, 1536:2048])
                return v.drain()
            V(_ekr)
            mset('ppq0', 'V')
            mset('ppq1', 'V')
            def _skr(sy, jt=jt, b=b):
                w(sy, 'V')
                dma(sy, kAs[(8 + jt) * 128:(9 + jt) * 128, b * BLK:(b + 1) * BLK], mst[:, 0:512], 'dO')
                dma(sy, kAs[(12 + jt) * 128:(13 + jt) * 128, b * BLK:(b + 1) * BLK], mst[:, 512:1024], 'dO')
                return None
            Y(_skr)
            if jt == 3:
                mset('ekrdone', 'V')
        if b + 1 < NB:
            def _wko(v):
                w(v, 'dO', 'dW')
                return v.drain()
            V(_wko)
            ln_stats(Xa2, y3)
            vmst_a2 = CN['V']
            mset('ppq0', 'V')
            mset('ppq1', 'V')
            aplA2 = ln_apply(Xa2, y3)
            mset('yA', 'V')
            if b + 2 < NB:
                def _ldx2n2(sy, b=b):
                    mwait(sy, 'yA')
                    src3 = x2s[:, (b + 2) * BLK:(b + 3) * BLK].rearrange('(n p) c -> p n c', p=128)
                    dst3 = Xa2.t_[:, :].rearrange('p (n c) -> p n c', c=512)
                    dma(sy, dst3, src3, 'dW')
                    return None
                Y(_ldx2n2)

        def _ldvu0(gp):
            mwait(gp, 'a2h0')
            stream_weight_rt(gp, wvuT, ws_vu, 0, 2, 0, 2048, 0)
            return None
        G(_ldvu0)
        dgv0 = CN['dG']
        def _ldvu1(gp):
            mwait(gp, 'a2h1')
            stream_weight_rt(gp, wvuT, ws_vu, 2, 2, 2, 2048, 0)
            return None
        G(_ldvu1)
        dgv1 = CN['dG']
        sbase_vu = CN['S']
        tpred_vu = []
        for tt in range(4):
            for sv in range(4):
                bank = (tt * 4 + sv) % 4
                idx = tt * 4 + sv
                def _mvu0(te, tt=tt, sv=sv, bank=bank, idx=idx, dgv0=dgv0, sb=sbase_vu):
                    te.wait_ge(SEMH['dG'], dgv0)
                    if idx >= 4:
                        te.wait_ge(SEMH['S'], sb + idx - 3)
                        mwait(te, 'kvlat')
                    else:
                        mwait(te, 'kvlat', 'ppq%d' % bank)
                    pp = BK4[bank]
                    last = None
                    for lt in range(2):
                        last = te.matmul(pp[:, :], kvlat.tc(lt, tt * 128, (tt + 1) * 128),
                                         ws_vu.tc(lt, sv * 512, (sv + 1) * 512), start=(lt == 0), stop=False)
                    return last
                T(_mvu0)
                if tt == 3 and sv == 3:
                    mset('a2h0', 'T')
                def _mvu1(te, tt=tt, sv=sv, bank=bank, dgv1=dgv1):
                    te.wait_ge(SEMH['dG'], dgv1)
                    pp = BK4[bank]
                    for lt in range(2, 4):
                        te.matmul(pp[:, :], kvlat.tc(lt, tt * 128, (tt + 1) * 128),
                                  ws_vu.tc(lt, sv * 512, (sv + 1) * 512), start=False, stop=False)
                    return te.matmul(pp[:, :], os32b[:], bvu32[:, sv * 512:(sv + 1) * 512], start=False, stop=True)
                T(_mvu1)
                tpred_vu.append(CN['T'])
        mset('a2h1', 'T')
        def _evus(sc, tpred=tuple(tpred_vu), sb=sbase_vu, b=b, vm=vmst_a2):
            sc.wait_ge(SEMH['V'], vm)
            for tt in range(4):
                for sv in range(4):
                    idx = tt * 4 + sv
                    bank = idx % 4
                    sc.wait_ge(SEMH['T'], tpred[idx])
                    pp = BK4[bank]
                    src = pp[:, :].rearrange('p (h c) -> p h c', c=128)
                    dstv = mkap(vaugB.t_, (b * 4 + tt) * 2064 + 4 * sv * 129, [[129, 4], [1, 128]])
                    sc.copy(dstv, src).then_inc(SEMH['S'], 1)
            return None
        sec('S', _evus, None)
        CN['S'] += 16
        for idx in range(12, 16):
            MK['ppq%d' % (idx % 4)] = ('S', sbase_vu + idx + 1)

    stA2.close()

    if stop_after == 'a2':
        def _fin(sy):
            w(sy, 'S', 'V', 'T')
            sy.wait_ge(SEMH['dO'], CN['dO'])
            for nm, src, sh, dt in (('kAfull', kAs[:, :], [D, S_], BF16),
                                     ('latq', latq.t_[:, :], [128, 4 * 2048], BF16),
                                     ('vaugB', vaugB.t_[:, :], [128, 16 * 2064], BF16)):
                d = dbg(nm, sh, dt)
                if d is not None:
                    dma(sy, d, src, 'dO')
            sy.wait_ge(s_dO, CN['dO'])
            return None
        Y(_fin)
        ctx.close()
        return nc, dbg_list

    stB = ctx.enter_context(contextlib.ExitStack())
    wqut = sbt('wqut', [128, 4 * 2048], BF16, stB)
    wqu = Mat(wqut, 4, 2048)
    kAh = sbt('kAh', [128, 2048], BF16, stB)
    kAh2 = sbt('kAh2', [128, 2048], BF16, stB)
    KAHB = [kAh, kAh2]
    qAh = sbt('qAh', [128, 2048], BF16, stB)
    expb = sbt('expb', [128, 16 * 512], BF16, stB)
    aostg = sbt('aostg', [128, 512], BF16, stB)

    def _ldqu(gp):
        w(gp, 'T', 'V', 'S', 'dO')
        stream_weight(gp, wquT, wqu, 4, 2048, 0)
        return None
    G(_ldqu)

    SCB = [psP[0], psP[1], psS_[0], psS_[1]]
    scbmark = [0, 0, 0, 0]
    dw_kh = [0, 0]

    def emit_ldkh(h):
        def _ldkh(sy, h=h):
            mwait(sy, 'kAh%d' % (h % 2))
            w(sy, 'T', 'V', 'S', 'dO') if h == 0 else None
            dma(sy, KAHB[h % 2][:, :], kAs[h * 128:(h + 1) * 128, :], 'dW')
            return None
        Y(_ldkh)
        dw_kh[h % 2] = CN['dW']

    emit_ldkh(0)
    for h in range(NH):
        kbuf = KAHB[h % 2]
        dwm_kh = dw_kh[h % 2]
        for qb in range(4):
            def _mqu(te, h=h, qb=qb):
                w(te, 'dG')
                mwait(te, 'psVav', 'psVq%d' % (qb % 2))
                if h == 0 and qb == 0:
                    w(te, 'V', 'S')
                pp = psV_[qb % 2]
                last = None
                for lt in range(4):
                    last = te.matmul(pp[:, :], wqu.tc(lt, h * 128, (h + 1) * 128),
                                     latq.tc(lt, qb * 512, (qb + 1) * 512), start=(lt == 0), stop=(lt == 3))
                return last
            T(_mqu)
            def _equ(v, h=h, qb=qb):
                w(v, 'T')
                v.tensor_scalar(qAh[:, qb * 512:(qb + 1) * 512], psV_[qb % 2][:, :], cbqu[:, h:h + 1], None, ALU.add)
                return v.drain()
            V(_equ)
            mset('psVq%d' % (qb % 2), 'V')
        mset('qAh', 'V')
        if h + 1 < NH:
            emit_ldkh(h + 1)
        for qb in range(4):
            sbase = CN['S']
            tbase = CN['T']
            smw = [(sbase + kt - 3) if kt >= 4 else scbmark[kt % 4] for kt in range(16)]
            def _mscs(te, h=h, qb=qb, smw=tuple(smw), kbuf=kbuf, dwm_kh=dwm_kh):
                te.wait_ge(SEMH['dW'], dwm_kh)
                mwait(te, 'qAh')
                for kt in range(16):
                    if smw[kt] > 0:
                        te.wait_ge(SEMH['S'], smw[kt])
                    te.matmul(SCB[kt % 4][:, :], kbuf[:, kt * 128:(kt + 1) * 128],
                              qAh[:, qb * 512:(qb + 1) * 512], start=True,
                              stop=True).then_inc(SEMH['T'], 1)
                return None
            sec('T', _mscs, None)
            CN['T'] += 16
            def _exps(sc, h=h, qb=qb, tbase=tbase):
                for kt in range(16):
                    sc.wait_ge(SEMH['T'], tbase + kt + 1)
                    sc.activation(expb[:, kt * 512:(kt + 1) * 512], SCB[kt % 4][:, :], AF.Exp,
                                  scale=cpool[:, CPS:CPS + 1]).then_inc(SEMH['S'], 1)
                return None
            sec('S', _exps, None)
            CN['S'] += 16
            for kt in range(16):
                scbmark[kt % 4] = sbase + kt + 1
            if qb == 3:
                mset('kAh%d' % (h % 2), 'T')
            savb = CN['S']
            def _mavs(te, h=h, qb=qb, savb=savb):
                mwait(te, 'psVav', 'qAh')
                for qt in range(4):
                    pp = psV_[qt // 2]
                    c0 = (qt % 2) * 256
                    for kt in range(16):
                        if qt == 0 or kt == 15:
                            te.wait_ge(SEMH['S'], savb - 16 + kt + 1)
                        last = te.matmul(pp[:, c0:c0 + 129], expb[:, kt * 512 + qt * 128: kt * 512 + (qt + 1) * 128],
                                         vaugB.t_[:, kt * 2064 + h * 129: kt * 2064 + h * 129 + 129],
                                         start=(kt == 0), stop=(kt == 15))
                    last.then_inc(SEMH['T'], 1)
                return None
            sec('T', _mavs, None)
            CN['T'] += 4
            def _divb(v, qb=qb):
                w(v, 'T')
                for bank in range(2):
                    pp = psV_[bank]
                    dn = mkap(pp, 128, [[256, 2], [1, 1]])
                    v.reciprocal(gst.t_[:, 1536 + bank * 2:1538 + bank * 2], dn)
                    v.drain()
                    src = mkap(pp, 0, [[256, 2], [1, 128]])
                    rc = mkap(gst.t_, 1536 + bank * 2, [[1, 2], [0, 128]])
                    dst = mkap(aostg, bank * 256, [[128, 2], [1, 128]])
                    v.tensor_tensor(dst, src, rc, ALU.mult)
                    v.drain()
                return v.drain()
            V(_divb)
            mset('psVav', 'V')
            def _trb(te, qb=qb):
                w(te, 'V')
                last = None
                for qt in range(4):
                    last = te.transpose(psT_[:, qt * 128:(qt + 1) * 128], aostg[:, qt * 128:(qt + 1) * 128], ident[:])
                return last
            T(_trb)
            def _evb(v, qb=qb):
                w(v, 'T', 'dO')
                v.tensor_copy(mst[:, 0:512], psT_[:, 0:512])
                return v.drain()
            V(_evb)
            def _stb(sy, h=h, qb=qb):
                w(sy, 'V')
                dma(sy, aoas[h * 128:(h + 1) * 128, qb * 512:(qb + 1) * 512], mst[:, 0:512], 'dO')
                return None
            Y(_stb)

    stB.close()
    stAB.close()

    if stop_after == 'b':
        def _fin(sy):
            w(sy, 'S', 'V', 'T')
            sy.wait_ge(SEMH['dO'], CN['dO'])
            d = dbg('aofull', [D, S_], BF16)
            if d is not None:
                dma(sy, d, aoas[:, :], 'dO')
            sy.wait_ge(s_dO, CN['dO'])
            return None
        Y(_fin)
        ctx.close()
        return nc, dbg_list

    stC = ctx.enter_context(contextlib.ExitStack())
    Xc = Mat(sbt('Xc', [128, NT * 512], F32, stC), NT, 512)
    Xc2 = Mat(sbt('Xc2', [128, NT * 512], F32, stC), NT, 512)
    ybfC = Mat(sbt('ybfC', [128, NT * 512], BF16, stC), NT, 512)
    aoAc = Mat(sbt('aoAc', [128, NT * 512], BF16, stC), NT, 512)
    aoAc2 = Mat(sbt('aoAc2', [128, NT * 512], BF16, stC), NT, 512)
    ws0C = Mat(sbt('ws0C', [128, NT * 512], BF16, stC), NT, 512)
    ws1C = Mat(sbt('ws1C', [128, NT * 512], BF16, stC), NT, 512)
    WSC = [ws0C, ws1C]
    XCB = [Xc, Xc2]
    AOB = [aoAc, aoAc2]
    dw_xc = [0, 0]

    def emit_ldxc(b):
        def _ldxc(sy, b=b):
            if b == 0:
                w(sy, 'V', 'T', 'S', 'dG', 'dO')
            else:
                mwait(sy, 'xcst%d' % (b % 2), 'aofree%d' % (b % 2))
            src3 = x2s[:, b * BLK:(b + 1) * BLK].rearrange('(n p) c -> p n c', p=128)
            dst3 = XCB[b % 2].t_[:, :].rearrange('p (n c) -> p n c', c=512)
            dma(sy, dst3, src3, 'dW')
            src4 = aoas[:, b * BLK:(b + 1) * BLK].rearrange('(n p) c -> p n c', p=128)
            dst4 = AOB[b % 2].t_[:, :].rearrange('p (n c) -> p n c', c=512)
            dma(sy, dst4, src4, 'dW')
            return None
        Y(_ldxc)
        dw_xc[b % 2] = CN['dW']

    mset('ppq0', 'S')
    mset('ppq1', 'S')
    emit_ldxc(0)
    for b in range(NB):
        Xcur = XCB[b % 2]
        aoc = AOB[b % 2]
        dwm_xc = dw_xc[b % 2]
        for s_ in range(4):
            def _ldwoc(gp, s_=s_, first=(b == 0 and s_ < 2)):
                if first:
                    w(gp, 'T', 'V', 'S', 'dO')
                mwait(gp, 'wsb%d' % (s_ % 2))
                stream_weight(gp, wobT, WSC[s_ % 2], NT, 512, s_ * 512)
                return None
            G(_ldwoc)
            for dt_ in range(4):
                dmt = s_ * 4 + dt_
                def _mmoc(te, s_=s_, dt_=dt_, dmt=dmt, aoc=aoc, dwm_xc=dwm_xc):
                    w(te, 'dG')
                    te.wait_ge(SEMH['dW'], dwm_xc)
                    mwait(te, 'ppq%d' % (dmt % 2))
                    pp = psP[dmt % 2]
                    last = None
                    for j in range(NT):
                        last = te.matmul(pp[:, :], WSC[s_ % 2].tc(j, dt_ * 128, (dt_ + 1) * 128),
                                         aoc.t(j), start=(j == 0), stop=(j == NT - 1))
                    return last
                T(_mmoc)
                def _resc(v, dmt=dmt, Xcur=Xcur):
                    w(v, 'T')
                    v.scalar_tensor_tensor(Xcur.t(dmt), psP[dmt % 2][:, :], cbo[:, dmt:dmt + 1], Xcur.t(dmt), ALU.add, ALU.add)
                    v.drain()
                    v.tensor_copy(ybfC.t(dmt), Xcur.t(dmt))
                    if dmt == 0:
                        v.tensor_copy(gst.t_[:, 0:512], Xcur.t(0))
                        v.tensor_copy(gst.t_[:, 1024:1536], Xcur.t(0))
                    else:
                        v.tensor_max(gst.t_[:, 0:512], gst.t_[:, 0:512], Xcur.t(dmt))
                        v.tensor_tensor(gst.t_[:, 1024:1536], gst.t_[:, 1024:1536], Xcur.t(dmt), ALU.min)
                    return v.drain()
                V(_resc)
                mset('ppq%d' % (dmt % 2), 'V')
            mset('wsb%d' % (s_ % 2), 'T')
        mset('aofree%d' % (b % 2), 'T')
        if b + 1 < NB:
            emit_ldxc(b + 1)
        emit_ffn(Xcur, ybfC, WSC, first_block=False, prefused=True)
        def _stout(sy, b=b, Xcur=Xcur):
            w(sy, 'V', 'T')
            src3 = Xcur.t_[:, :].rearrange('p (n c) -> p n c', c=512)
            dst3 = xoT[:, b * BLK:(b + 1) * BLK].rearrange('(n p) c -> p n c', p=128)
            dma(sy, dst3, src3, 'dO')
            return None
        Y(_stout)
        mset('xcst%d' % (b % 2), 'dO')

    def _final(sy):
        w(sy, 'S', 'V', 'T')
        sy.wait_ge(s_dO, CN['dO'])
        return None
    Y(_final)
    ctx.close()
    return nc, dbg_list


import ml_dtypes


def _host_prep(inputs, core):
    d = {}
    d['xT'] = np.ascontiguousarray(np.asarray(inputs['L0_attn_a_x'])[core].T)
    d['wqT'] = np.ascontiguousarray(np.asarray(inputs['L0_attn_a_Wq']).T)
    d['wkT'] = np.ascontiguousarray(np.asarray(inputs['L0_attn_a_Wk']).T)
    d['wvT'] = np.ascontiguousarray(np.asarray(inputs['L0_attn_a_Wv']).T)
    d['woaT'] = np.ascontiguousarray(np.asarray(inputs['L0_attn_a_Wo']).T)
    d['wqdT'] = np.ascontiguousarray(np.asarray(inputs['L0_attn_b_Wq_down']).T)
    d['wkvdT'] = np.ascontiguousarray(np.asarray(inputs['L0_attn_b_Wkv_down']).T)
    d['wquT'] = np.ascontiguousarray(np.asarray(inputs['L0_attn_b_Wq_up']).T)
    d['wknT'] = np.ascontiguousarray(np.asarray(inputs['L0_attn_b_Wk_nope']).T)
    d['wkrT'] = np.ascontiguousarray(np.asarray(inputs['L0_attn_b_Wk_rope']).T)
    d['wvuT'] = np.ascontiguousarray(np.asarray(inputs['L0_attn_b_Wv_up']).T)
    d['wobT'] = np.ascontiguousarray(np.asarray(inputs['L0_attn_b_Wo']).T)
    d['wf8T'] = np.ascontiguousarray(np.asarray(inputs['L0_ffn_weight']).T)

    def cols(vec, n):
        return np.ascontiguousarray(np.asarray(vec, np.float32).reshape(n, 128).T)
    d['c_bqd'] = cols(inputs['L0_attn_b_bq_down'], 4)
    d['c_bkvd'] = cols(inputs['L0_attn_b_bkv_down'], 4)
    d['c_bqu'] = cols(inputs['L0_attn_b_bq_up'], 16)
    d['c_bkn'] = cols(inputs['L0_attn_b_bk_nope'], 8)
    d['c_bkr'] = cols(inputs['L0_attn_b_bk_rope'], 8)
    d['c_bo'] = cols(inputs['L0_attn_b_bo'], 16)
    d['c_fb'] = cols(inputs['L0_ffn_bias'], 16)
    d['c_ws'] = cols(np.asarray(inputs['L0_ffn_w_scale'])[:, 0], 16)
    d['r_bvu'] = np.asarray(inputs['L0_attn_b_bv_up'], np.float32).reshape(1, -1)

    bf = ml_dtypes.bfloat16
    d['identin'] = np.eye(128, dtype=bf)
    kk = np.arange(128)
    d['maskin'] = (kk[:, None] <= kk[None, :]).astype(bf)
    d['onesttin'] = np.ones((128, 32), dtype=bf)
    o32 = np.zeros((32, 128), np.float32); o32[0, :] = 1.0
    d['os32fin'] = o32
    d['os32bin'] = o32.astype(bf)
    d['onecolbin'] = np.ones((128, 1), dtype=bf)
    d['iotacolin'] = np.arange(128, dtype=np.float32).reshape(128, 1)
    ii = np.arange(128)
    d['eArowin'] = (-2.0 * (ii % 64) / 128.0).astype(np.float32).reshape(1, 128)
    jj = np.arange(512)
    d['eBcolin'] = np.ascontiguousarray((-2.0 * jj / 1024.0).astype(np.float32).reshape(4, 128).T)
    d['signAin'] = np.where(ii % 128 < 64, -1.0, 1.0).astype(np.float32).reshape(1, 128)
    d['identfin'] = np.eye(128, dtype=np.float32)
    io = np.zeros((32, 2048), np.float32); io[0, :] = np.arange(2048)
    d['iota32in'] = io
    t64 = np.arange(2048, dtype=np.float64)
    dA = np.arange(128)
    invA = np.float64(np.float32(10000.0) ** np.float32(-2.0 * (dA % 64) / 128.0))
    angA = np.float64(np.float32(t64[:, None] * invA[None, :]))
    cosA = np.cos(angA)
    sinA = np.sin(angA) * np.where(dA < 64, -1.0, 1.0)[None, :]
    d['tabAcin'] = cosA.reshape(16, 128, 128).transpose(1, 0, 2).reshape(128, 2048).astype(ml_dtypes.bfloat16)
    d['tabAsin'] = sinA.reshape(16, 128, 128).transpose(1, 0, 2).reshape(128, 2048).astype(ml_dtypes.bfloat16)
    jB = np.arange(512)
    invB = np.float64(np.float32(10000.0) ** np.float32(-2.0 * jB / 1024.0))
    angB = np.float64(np.float32(t64[:, None] * invB[None, :]))
    cosB, sinB = np.cos(angB), np.sin(angB)
    d['tabBcin'] = cosB.reshape(4, 512, 4, 128).transpose(3, 0, 2, 1).reshape(128, 8192).astype(ml_dtypes.bfloat16)
    d['tabBsin'] = sinB.reshape(4, 512, 4, 128).transpose(3, 0, 2, 1).reshape(128, 8192).astype(ml_dtypes.bfloat16)
    return d


def kernel(**inputs):
    from concourse.bass_utils import run_bass_kernel_spmd
    nc, _ = build()
    in_maps = [_host_prep(inputs, core) for core in range(8)]
    res = run_bass_kernel_spmd(nc, in_maps, core_ids=list(range(8)))
    out = np.empty((8, 2048, 2048), dtype=np.float32)
    for core in range(8):
        out[core] = np.asarray(res.results[core]['xoT']).T
    return out


# version 30
# speedup vs baseline: 1.0165x; 1.0066x over previous
import numpy as np
import concourse.bass as bass
import concourse.mybir as mybir

F32 = mybir.dt.float32
F16 = mybir.dt.float16
BF16 = mybir.dt.bfloat16
I32 = mybir.dt.int32
I8 = mybir.dt.int8
AF = mybir.ActivationFunctionType
ALU = mybir.AluOpType

D = 2048
S = 2048
NB = 4
BLK = 512
NT = 16
NH = 16
KVH = 4
HD = 128
LAT = 512
EPS_LN = 1e-5
EPS_A = 1e-6
SMAX = float(1.0 / np.sqrt(np.float32(128.0)))
PI_HI = 6.28125
PI_LO = float(2.0 * np.pi - 6.28125)
INV2PI = float(1.0 / (2.0 * np.pi))
HALFPI = float(np.pi / 2.0)
LN1E4 = float(np.log(10000.0))


class Mat:
    def __init__(self, t, nt, C):
        self.t_, self.nt, self.C = t, nt, C

    def t(self, j):
        return self.t_[:, j * self.C:(j + 1) * self.C]

    def tc(self, j, c0, c1):
        return self.t_[:, j * self.C + c0: j * self.C + c1]


def build(debug=(), stop_after=None):
    import contextlib
    nc = bass.Bass('TRN2', target_bir_lowering=False, debug=False)

    xT = nc.dram_tensor('xT', [D, S], F32, kind='ExternalInput').ap()
    wqT = nc.dram_tensor('wqT', [D, D], F32, kind='ExternalInput').ap()
    wkT = nc.dram_tensor('wkT', [D, 512], F32, kind='ExternalInput').ap()
    wvT = nc.dram_tensor('wvT', [D, 512], F32, kind='ExternalInput').ap()
    woaT = nc.dram_tensor('woaT', [D, D], F32, kind='ExternalInput').ap()
    wqdT = nc.dram_tensor('wqdT', [D, LAT], F32, kind='ExternalInput').ap()
    wkvdT = nc.dram_tensor('wkvdT', [D, LAT], F32, kind='ExternalInput').ap()
    wquT = nc.dram_tensor('wquT', [LAT, D], F32, kind='ExternalInput').ap()
    wknT = nc.dram_tensor('wknT', [LAT, 1024], F32, kind='ExternalInput').ap()
    wkrT = nc.dram_tensor('wkrT', [LAT, 1024], F32, kind='ExternalInput').ap()
    wvuT = nc.dram_tensor('wvuT', [LAT, D], F32, kind='ExternalInput').ap()
    wobT = nc.dram_tensor('wobT', [D, D], F32, kind='ExternalInput').ap()
    wf8T = nc.dram_tensor('wf8T', [D, D], I8, kind='ExternalInput').ap()
    c_bqd = nc.dram_tensor('c_bqd', [128, 4], F32, kind='ExternalInput').ap()
    c_bkvd = nc.dram_tensor('c_bkvd', [128, 4], F32, kind='ExternalInput').ap()
    c_bqu = nc.dram_tensor('c_bqu', [128, 16], F32, kind='ExternalInput').ap()
    c_bkn = nc.dram_tensor('c_bkn', [128, 8], F32, kind='ExternalInput').ap()
    c_bkr = nc.dram_tensor('c_bkr', [128, 8], F32, kind='ExternalInput').ap()
    c_bo = nc.dram_tensor('c_bo', [128, 16], F32, kind='ExternalInput').ap()
    c_fb = nc.dram_tensor('c_fb', [128, 16], F32, kind='ExternalInput').ap()
    c_ws = nc.dram_tensor('c_ws', [128, 16], F32, kind='ExternalInput').ap()
    r_bvu = nc.dram_tensor('r_bvu', [1, D], F32, kind='ExternalInput').ap()
    identin = nc.dram_tensor('identin', [128, 128], BF16, kind='ExternalInput').ap()
    maskin = nc.dram_tensor('maskin', [128, 128], BF16, kind='ExternalInput').ap()
    onesttin = nc.dram_tensor('onesttin', [128, 32], BF16, kind='ExternalInput').ap()
    os32fin = nc.dram_tensor('os32fin', [32, 128], F32, kind='ExternalInput').ap()
    os32bin = nc.dram_tensor('os32bin', [32, 128], BF16, kind='ExternalInput').ap()
    onecolbin = nc.dram_tensor('onecolbin', [128, 1], BF16, kind='ExternalInput').ap()
    iotacolin = nc.dram_tensor('iotacolin', [128, 1], F32, kind='ExternalInput').ap()
    eArowin = nc.dram_tensor('eArowin', [1, 128], F32, kind='ExternalInput').ap()
    eBcolin = nc.dram_tensor('eBcolin', [128, 4], F32, kind='ExternalInput').ap()
    signAin = nc.dram_tensor('signAin', [1, 128], F32, kind='ExternalInput').ap()
    identfin = nc.dram_tensor('identfin', [128, 128], F32, kind='ExternalInput').ap()
    iota32in = nc.dram_tensor('iota32in', [32, S], F32, kind='ExternalInput').ap()
    tabAcin = nc.dram_tensor('tabAcin', [128, 2048], BF16, kind='ExternalInput').ap()
    tabAsin = nc.dram_tensor('tabAsin', [128, 2048], BF16, kind='ExternalInput').ap()
    tabBcin = nc.dram_tensor('tabBcin', [128, 8192], BF16, kind='ExternalInput').ap()
    tabBsin = nc.dram_tensor('tabBsin', [128, 8192], BF16, kind='ExternalInput').ap()

    xoT = nc.dram_tensor('xoT', [D, S], F32, kind='ExternalOutput').ap()
    x2s = nc.dram_tensor('x2s', [D, S], F32, kind='Internal').ap()
    kAs = nc.dram_tensor('kAs', [D, S], BF16, kind='Internal').ap()
    aoas = nc.dram_tensor('aoas', [D, S], BF16, kind='Internal').ap()

    dbg_list = []
    dbg_t = {}
    def dbg(name, shape, dtype=F32):
        if name in debug:
            dbg_t[name] = nc.dram_tensor('dbg_' + name, shape, dtype, kind='ExternalOutput').ap()
            dbg_list.append('dbg_' + name)
            return dbg_t[name]
        return None

    ctx = contextlib.ExitStack()

    def sbt(name, shape, dt, st=None):
        return (st or ctx).enter_context(nc.sbuf_tensor(name, shape, dt))

    def pst(name, shape, dt, st=None):
        return (st or ctx).enter_context(nc.psum_tensor(name, shape, dt))

    ident = sbt('ident', [128, 128], BF16)
    maskT = sbt('maskT', [128, 128], BF16)
    onestt = sbt('onestt', [128, 32], BF16)
    os32f = sbt('os32f', [32, 128], F32)
    os32b = sbt('os32b', [32, 128], BF16)
    onecolb = sbt('onecolb', [128, 1], BF16)
    iotacol = sbt('iotacol', [128, 1], F32)
    eArow = sbt('eArow', [1, 128], F32)
    eBcol = sbt('eBcol', [128, 4], F32)
    signA = sbt('signA', [1, 128], F32)
    identf = sbt('identf', [128, 128], F32)
    cbqd = sbt('cbqd', [128, 4], F32)
    cbkvd = sbt('cbkvd', [128, 4], F32)
    cbqu = sbt('cbqu', [128, 16], F32)
    cbkn = sbt('cbkn', [128, 8], F32)
    cbkr = sbt('cbkr', [128, 8], F32)
    cbo = sbt('cbo', [128, 16], F32)
    cfb = sbt('cfb', [128, 16], F32)
    cws = sbt('cws', [128, 16], F32)
    bvu32 = sbt('bvu32', [32, D], BF16)
    cpool = sbt('cpool', [128, 8], F32)
    CP_ROUND, CP_QUANT, CP_EPSLN, CP_NHALF, CP_SMAX, CP_EPSA, CP_MU = 0, 1, 2, 3, 4, 5, 6
    c1col = sbt('c1col', [128, 16], F32)
    itmp = sbt('itmp', [128, 512], I32)
    itmp2 = sbt('itmp2', [128, 512], I32)
    rows32 = sbt('rows32', [32, 1536], F32)
    rowt = sbt('rowt', [1, 1536], F32)
    RW_M, RW_RSTD, RW_T1 = 0, 512, 1024
    inv32 = sbt('inv32', [32, 128], F32)
    invfgnb = sbt('invfgnb', [128, 128], F32)
    invfb = sbt('invfb', [128, 128], F32)
    invBcol = sbt('invBcol', [128, 4], F32)
    gst = Mat(sbt('gst', [128, 2048], F32), 2, 1024)
    mstP = sbt('mstP', [128, 2 * 512], BF16)
    mstP2 = sbt('mstP2', [128, 2 * 512], BF16)
    f16stP = sbt('f16stP', [128, 512], F16)

    s_dW = ctx.enter_context(nc.semaphore('s_dW'))
    s_dG = ctx.enter_context(nc.semaphore('s_dG'))
    s_dO = ctx.enter_context(nc.semaphore('s_dO'))
    s_T = ctx.enter_context(nc.semaphore('s_T'))
    s_V = ctx.enter_context(nc.semaphore('s_V'))
    s_S = ctx.enter_context(nc.semaphore('s_S'))

    blk = ctx.enter_context(nc.Block())

    CN = {'T': 0, 'V': 0, 'S': 0, 'dW': 0, 'dG': 0, 'dO': 0}
    SEM = {'T': s_T, 'V': s_V, 'S': s_S, 'dW': s_dW, 'dG': s_dG, 'dO': s_dO}

    def sec(engine_name, f, inc=None):
        dec = {'T': blk.tensor, 'V': blk.vector, 'S': blk.scalar,
               'G': blk.gpsimd, 'Y': blk.sync}[engine_name]
        def wrapper(eng):
            last = f(eng)
            if inc is not None:
                assert last is not None, f'section for {engine_name} returned no instr'
                CN[inc] += 1
                last.then_inc(SEM[inc], 1)
        dec(wrapper)

    def T(f):
        sec('T', f, 'T')
    def V(f):
        sec('V', f, 'V')
    def Sc(f):
        sec('S', f, 'S')
    def G(f):
        sec('G', f, None)
    def Y(f):
        sec('Y', f, None)

    def dma(eng, out, in_, sem_key):
        eng.dma_start(out=out, in_=in_).then_inc(SEM[sem_key], 16)
        CN[sem_key] += 16

    def _loads(sy):
        for dst, src in [(ident[:], identin), (maskT[:], maskin), (onestt[:], onesttin),
                         (os32f[:], os32fin), (os32b[:], os32bin), (onecolb[:], onecolbin),
                         (iotacol[:], iotacolin), (eArow[:], eArowin),
                         (eBcol[:], eBcolin), (signA[:], signAin),
                         (cbqd[:], c_bqd), (cbkvd[:], c_bkvd), (cbqu[:], c_bqu),
                         (cbkn[:], c_bkn), (cbkr[:], c_bkr), (cbo[:], c_bo),
                         (cfb[:], c_fb), (cws[:], c_ws), (rowt[0:1, 0:D] if D <= 1536 else None, r_bvu)]:
            pass
        return None

    def _loads2(sy):
        for dst, src in [(ident[:], identin), (maskT[:], maskin), (onestt[:], onesttin),
                         (os32f[:], os32fin), (os32b[:], os32bin), (onecolb[:], onecolbin),
                         (iotacol[:], iotacolin), (eArow[:], eArowin),
                         (eBcol[:], eBcolin), (signA[:], signAin),
                         (cbqd[:], c_bqd), (cbkvd[:], c_bkvd), (cbqu[:], c_bqu),
                         (cbkn[:], c_bkn), (cbkr[:], c_bkr), (cbo[:], c_bo),
                         (cfb[:], c_fb), (cws[:], c_ws), (identf[:], identfin),
                         (gst.t_[0:1, 0:D], r_bvu)]:
            dma(sy, dst, src, 'dW')
        return None
    Y(_loads2)
    DW0 = CN['dW']

    def _setup(v):
        v.wait_ge(s_dW, DW0)
        v.memset(bvu32[:], 0.0)
        v.memset(rows32[:], 0.0)
        v.memset(inv32[:], 0.0)
        v.memset(cpool[:], 0.0)
        v.drain()
        v.memset(cpool[:, CP_EPSLN:CP_EPSLN + 1], EPS_LN)
        v.memset(cpool[:, CP_NHALF:CP_NHALF + 1], -0.5)
        v.memset(cpool[:, CP_SMAX:CP_SMAX + 1], SMAX)
        v.memset(cpool[:, CP_EPSA:CP_EPSA + 1], EPS_A)
        v.memset(cpool[:, CP_MU:CP_MU + 1], 0.7)
        v.tensor_copy(bvu32[0:1, 0:D], gst.t_[0:1, 0:D])
        v.drain()
        v.tensor_copy(itmp[:, 0:1], cpool[:, CP_MU:CP_MU + 1])
        v.drain()
        v.tensor_copy(cpool[:, CP_MU:CP_MU + 1], itmp[:, 0:1])
        v.drain()
        v.tensor_scalar(cpool[:, CP_ROUND:CP_ROUND + 1], cpool[:, CP_MU:CP_MU + 1], -0.5, 0.5, ALU.mult, ALU.add)
        v.tensor_scalar(cpool[:, CP_QUANT:CP_QUANT + 1], cpool[:, CP_MU:CP_MU + 1], -0.5, 128.5, ALU.mult, ALU.add)
        return v.drain()
    V(_setup)
    V_SETUP = CN['V']

    def _invf(sc):
        sc.wait_ge(s_V, V_SETUP)
        sc.activation(rowt[0:1, RW_T1:RW_T1 + 128], eArow[:], AF.Exp, scale=LN1E4)
        sc.activation(invBcol[:], eBcol[:], AF.Exp, scale=LN1E4)
        return sc.drain()
    Sc(_invf)
    sN = CN['S']

    def _inv1(v, sN=sN):
        v.wait_ge(s_S, sN)
        v.tensor_copy(inv32[0:1, :], rowt[0:1, RW_T1:RW_T1 + 128])
        return v.drain()
    V(_inv1)

    st_setup = ctx.enter_context(contextlib.ExitStack())
    psX = pst('psX', [128, 512], F32, st_setup)

    vN = CN['V']
    def _invb1(te, vN=vN):
        te.wait_ge(s_V, vN)
        return te.matmul(psX[:, 0:128], os32f[:], inv32[:], start=True, stop=True)
    T(_invb1)

    tN = CN['T']
    def _inv2(v, tN=tN):
        v.wait_ge(s_T, tN)
        v.tensor_copy(invfb[:], psX[:, 0:128])
        v.tensor_mul(inv32[0:1, :], rowt[0:1, RW_T1:RW_T1 + 128], signA[:])
        return v.drain()
    V(_inv2)

    vN = CN['V']
    def _invb2(te, vN=vN):
        te.wait_ge(s_V, vN)
        return te.matmul(psX[:, 0:128], os32f[:], inv32[:], start=True, stop=True)
    T(_invb2)

    tN = CN['T']
    def _inv3(v, tN=tN):
        v.wait_ge(s_T, tN)
        v.tensor_copy(invfgnb[:], psX[:, 0:128])
        return v.drain()
    V(_inv3)
    V_INV = CN['V']

    def tab_red(v, ang, b):
        p = 1 if b >= 1024 else 0
        u = gst.t_[:, b + 256:b + 384]
        r1 = gst.t_[:, b + 384:b + 512]
        ic = itmp[:, p * 128:(p + 1) * 128]
        v.tensor_scalar(u, ang, INV2PI, cpool[:, CP_ROUND:CP_ROUND + 1], ALU.mult, ALU.add)
        v.drain()
        v.tensor_copy(ic, u)
        v.drain()
        v.tensor_copy(u, ic)
        v.drain()
        v.scalar_tensor_tensor(r1, u, -PI_HI, ang, ALU.mult, ALU.add)
        v.drain()
        v.scalar_tensor_tensor(ang, u, -PI_LO, r1, ALU.mult, ALU.add)
        v.drain()

    return_env = dict(locals())
    return _stages(return_env, debug=debug, stop_after=stop_after)


def _stages(env, debug=(), stop_after=None):
    import contextlib
    g = env
    nc = g['nc']
    T, V, Sc, G, Y = g['T'], g['V'], g['Sc'], g['G'], g['Y']
    sec = g['sec']
    CN, SEM, dma = g['CN'], g['SEM'], g['dma']
    s_dW, s_dG, s_dO = g['s_dW'], g['s_dG'], g['s_dO']
    s_T, s_V, s_S = g['s_T'], g['s_V'], g['s_S']
    sbt, pst = g['sbt'], g['pst']
    gst, itmp, cpool, rowt, rows32 = g['gst'], g['itmp'], g['cpool'], g['rowt'], g['rows32']
    itmp2 = g['itmp2']
    RW_M, RW_RSTD, RW_T1 = g['RW_M'], g['RW_RSTD'], g['RW_T1']
    invfb, invfgnb, invBcol = g['invfb'], g['invfgnb'], g['invBcol']
    identf = g['identf']
    iotacol, ident, maskT, onestt, os32f, os32b, onecolb = (
        g['iotacol'], g['ident'], g['maskT'], g['onestt'], g['os32f'], g['os32b'], g['onecolb'])
    cbqd, cbkvd, cbqu, cbkn, cbkr, cbo, cfb, cws, bvu32, c1col = (
        g['cbqd'], g['cbkvd'], g['cbqu'], g['cbkn'], g['cbkr'], g['cbo'], g['cfb'], g['cws'],
        g['bvu32'], g['c1col'])
    mst = g['mstP']
    mst2 = g['mstP2']
    f16st = g['f16stP']
    tab_red = g['tab_red']
    dbg = g['dbg']
    dbg_list = g['dbg_list']
    ctx = g['ctx']
    CPR, CPQ, CPE, CPH, CPS, CPA, CPM = (g['CP_ROUND'], g['CP_QUANT'], g['CP_EPSLN'],
                                          g['CP_NHALF'], g['CP_SMAX'], g['CP_EPSA'], g['CP_MU'])
    xT, xoT, x2s, kAs, aoas = g['xT'], g['xoT'], g['x2s'], g['kAs'], g['aoas']
    wqT, wkT, wvT, woaT = g['wqT'], g['wkT'], g['wvT'], g['woaT']
    wqdT, wkvdT, wquT, wknT, wkrT, wvuT, wobT, wf8T = (
        g['wqdT'], g['wkvdT'], g['wquT'], g['wknT'], g['wkrT'], g['wvuT'], g['wobT'], g['wf8T'])
    iota32in = g['iota32in']
    tabAcin, tabAsin = g['tabAcin'], g['tabAsin']
    tabBcin, tabBsin = g['tabBcin'], g['tabBsin']
    psX = g['psX']
    st_setup = g['st_setup']
    S_ = S
    V_INV = g['V_INV']

    SEMH = {'T': s_T, 'V': s_V, 'S': s_S, 'dW': s_dW, 'dG': s_dG, 'dO': s_dO}


    def mkap(h, extra_off, free_dims):
        a = h[:, :] if not isinstance(h, bass.AP) else h
        return bass.AP(a.tensor, a.offset + extra_off, [list(a.ap[0])] + [list(d) for d in free_dims])

    def w(eng, *deps):
        for d in deps:
            if CN[d] > 0:
                eng.wait_ge(SEMH[d], CN[d])

    MK = {}
    def mset(name, key):
        MK[name] = (key, CN[key])
    def mwait(eng, *names):
        for nm in names:
            if nm in MK:
                k, c = MK[nm]
                if c > 0:
                    eng.wait_ge(SEMH[k], c)

    wsA2t = sbt('wsA2', [128, NT * 512], BF16)

    stA_scope = ctx.enter_context(contextlib.ExitStack())
    tabAc = sbt('tabAc', [128, 16 * 128], BF16, stA_scope)
    tabAs = sbt('tabAs', [128, 16 * 128], BF16, stA_scope)
    Xa = Mat(sbt('Xa', [128, NT * 512], F32, stA_scope), NT, 512)
    ybf = Mat(sbt('ybf', [128, NT * 512], BF16, stA_scope), NT, 512)
    qfB = Mat(sbt('qfB', [128, 4 * 2048], BF16, stA_scope), 4, 2048)
    qfA = Mat(sbt('qfA', [128, NT * 512], BF16, stA_scope), NT, 512)
    kfB = Mat(sbt('kfB', [128, 4 * 512], BF16, stA_scope), 4, 512)
    kfA = Mat(sbt('kfA', [128, 4 * 512], BF16, stA_scope), 4, 512)
    vaug = Mat(sbt('vaug', [128, 4 * 516], BF16, stA_scope), 4, 516)
    stLa = sbt('stLa', [128, KVH * 129], F32, stA_scope)
    stbf = sbt('stbf', [128, KVH * 129], BF16, stA_scope)
    ws0 = Mat(sbt('ws0A', [128, NT * 512], BF16, stA_scope), NT, 512)
    ws1 = Mat(sbt('ws1A', [128, NT * 512], BF16, stA_scope), NT, 512)
    WS = [ws0, ws1]

    def _tabinit(v):
        w(v, 'V')
        v.memset(stLa[:], 0.0)
        v.memset(stbf[:], 0.0)
        v.memset(vaug.t_[:, :], 0.0)
        v.drain()
        for tt in range(4):
            for h in range(KVH):
                c0 = tt * 516 + h * 129 + 128
                v.memset(vaug.t_[:, c0:c0 + 1], 1.0)
        return v.drain()
    V(_tabinit)

    def _ldtabA(sy):
        dma(sy, tabAc[:, :], tabAcin, 'dW')
        dma(sy, tabAs[:, :], tabAsin, 'dW')
        return None
    Y(_ldtabA)

    if stop_after == 'setup':
        def _fin(sy):
            w(sy, 'S', 'V', 'T')
            for nm, src, dt in (('tabAc', tabAc[:, :], BF16), ('tabAs', tabAs[:, :], BF16), ('cpool', cpool[:], F32)):
                d = dbg(nm, [128, src.shape[1]], dt)
                if d is not None:
                    dma(sy, d, src, 'dO')
            sy.wait_ge(s_dO, CN['dO'])
            return None
        Y(_fin)
        ctx.close()
        return nc, dbg_list

    st_setup.close()
    psP = [pst('psP0', [128, 512], F32), pst('psP1', [128, 512], F32)]
    psS_ = [pst('psS0', [128, 512], F32), pst('psS1', [128, 512], F32)]
    psV_ = [pst('psV0', [128, 512], F32), pst('psV1', [128, 512], F32)]
    psU = pst('psU', [128, 512], F32)
    psT_ = pst('psTt', [128, 1024], BF16)

    def ln_stats(Xsrc, ybf_dst, precopied=False, strip_dw=None):
        if strip_dw is not None:
            for st4 in range(4):
                def _cps(v, st4=st4, dwv=strip_dw[st4]):
                    v.wait_ge(s_dW, dwv)
                    w(v, 'V', 'T')
                    for j in range(st4 * 4, st4 * 4 + 4):
                        v.tensor_copy(ybf_dst.t(j), Xsrc.t(j))
                    return v.drain()
                V(_cps)
                def _mm_sum_s(te, st4=st4):
                    w(te, 'V')
                    last = None
                    for j in range(st4 * 4, st4 * 4 + 4):
                        last = te.matmul(psP[0][0:32, :], onestt[:], ybf_dst.t(j), start=(j == 0), stop=(j == NT - 1))
                    return last
                T(_mm_sum_s)
                for j2 in (st4 * 4, st4 * 4 + 2):
                    def _sq(v, j2=j2):
                        w(v, 'V', 'T')
                        v.tensor_mul(mst[:, 0:512], ybf_dst.t(j2), ybf_dst.t(j2))
                        v.tensor_mul(mst[:, 512:1024], ybf_dst.t(j2 + 1), ybf_dst.t(j2 + 1))
                        return v.drain()
                    V(_sq)
                    def _mm_sq(te, j2=j2):
                        w(te, 'V')
                        te.matmul(psP[1][0:32, :], onestt[:], mst[:, 0:512], start=(j2 == 0), stop=False)
                        return te.matmul(psP[1][0:32, :], onestt[:], mst[:, 512:1024], start=False, stop=(j2 == NT - 2))
                    T(_mm_sq)
        else:
            if not precopied:
                def _cp(v):
                    w(v, 'V', 'T')
                    for j in range(NT):
                        v.tensor_copy(ybf_dst.t(j), Xsrc.t(j))
                    return v.drain()
                V(_cp)
            def _mm_sum(te):
                w(te, 'V')
                last = None
                for j in range(NT):
                    last = te.matmul(psP[0][0:32, :], onestt[:], ybf_dst.t(j), start=(j == 0), stop=(j == NT - 1))
                return last
            T(_mm_sum)
            for j2 in range(0, NT, 2):
                def _sq(v, j2=j2):
                    w(v, 'V', 'T')
                    v.tensor_mul(mst[:, 0:512], ybf_dst.t(j2), ybf_dst.t(j2))
                    v.tensor_mul(mst[:, 512:1024], ybf_dst.t(j2 + 1), ybf_dst.t(j2 + 1))
                    return v.drain()
                V(_sq)
                def _mm_sq(te, j2=j2):
                    w(te, 'V')
                    te.matmul(psP[1][0:32, :], onestt[:], mst[:, 0:512], start=(j2 == 0), stop=False)
                    return te.matmul(psP[1][0:32, :], onestt[:], mst[:, 512:1024], start=False, stop=(j2 == NT - 2))
                T(_mm_sq)
        def _rowm(v):
            w(v, 'T')
            v.tensor_scalar(rowt[0:1, RW_M:RW_M + 512], psP[0][0:1, :], 1.0 / 2048.0, None, ALU.mult)
            v.tensor_scalar(rowt[0:1, RW_T1:RW_T1 + 512], psP[1][0:1, :], 1.0 / 2048.0, None, ALU.mult)
            v.drain()
            v.tensor_mul(rowt[0:1, RW_RSTD:RW_RSTD + 512], rowt[0:1, RW_M:RW_M + 512], rowt[0:1, RW_M:RW_M + 512])
            v.drain()
            v.tensor_sub(rowt[0:1, RW_T1:RW_T1 + 512], rowt[0:1, RW_T1:RW_T1 + 512], rowt[0:1, RW_RSTD:RW_RSTD + 512])
            return v.drain()
        V(_rowm)
        def _rstd(sc):
            w(sc, 'V')
            sc.activation(rowt[0:1, RW_RSTD:RW_RSTD + 512], rowt[0:1, RW_T1:RW_T1 + 512], AF.Ln, bias=cpool[0:1, CPE:CPE + 1])
            sc.drain()
            sc.activation(rowt[0:1, RW_RSTD:RW_RSTD + 512], rowt[0:1, RW_RSTD:RW_RSTD + 512], AF.Exp, scale=cpool[0:1, CPH:CPH + 1])
            return sc.drain()
        Sc(_rstd)

    def ln_apply(Xsrc, out_mat, bb=None):
        if bb is None:
            bb = (psV_[0], psV_[1])
        def _rows(v):
            w(v, 'S', 'T')
            v.scalar_tensor_tensor(rowt[0:1, RW_T1:RW_T1 + 512], rowt[0:1, RW_M:RW_M + 512], -1.0,
                                   rowt[0:1, RW_RSTD:RW_RSTD + 512], ALU.mult, ALU.mult)
            v.drain()
            v.tensor_copy(rows32[0:1, 0:512], rowt[0:1, RW_RSTD:RW_RSTD + 512])
            v.tensor_copy(rows32[0:1, 512:1024], rowt[0:1, RW_T1:RW_T1 + 512])
            return v.drain()
        V(_rows)
        def _b12(te):
            w(te, 'V')
            te.matmul(bb[0][:, :], os32f[:], rows32[:, 0:512], start=True, stop=True)
            return te.matmul(bb[1][:, :], os32f[:], rows32[:, 512:1024], start=True, stop=True)
        T(_b12)
        marks = []
        for j2 in range(0, NT, 2):
            def _apply(v, j2=j2):
                w(v, 'T')
                for j in (j2, j2 + 1):
                    p = (j % 2) * 1024
                    v.tensor_mul(gst.t_[:, p:p + 512], Xsrc.t(j), bb[0][:, :])
                    v.drain()
                    v.tensor_add(out_mat.t(j), gst.t_[:, p:p + 512], bb[1][:, :])
                    v.drain()
                return v.drain()
            V(_apply)
            marks.append(CN['V'])
        return marks

    def stream_weight(sy, src_ap, dst_mat, n_dtiles, cols, col0, sem_key='dG'):
        src = src_ap[0:n_dtiles * 128, col0:col0 + cols]
        src3 = src.rearrange('(n p) c -> p n c', p=128)
        dst3 = dst_mat.t_[:, 0:n_dtiles * cols].rearrange('p (n c) -> p n c', c=cols)
        dma(sy, dst3, src3, sem_key)

    def stream_weight_rt(sy, src_ap, dst_mat, sd0, n, dd0, cols, col0, sem_key='dG'):
        src = src_ap[sd0 * 128:(sd0 + n) * 128, col0:col0 + cols]
        src3 = src.rearrange('(n p) c -> p n c', p=128)
        dst3 = dst_mat.t_[:, dd0 * cols:(dd0 + n) * cols].rearrange('p (n c) -> p n c', c=cols)
        dma(sy, dst3, src3, sem_key)

    def rope_elu(v, pp, dst, b, tt, kind):
        cosb = tabAc[:, (b * 4 + tt) * 128:(b * 4 + tt + 1) * 128]
        sinb = tabAs[:, (b * 4 + tt) * 128:(b * 4 + tt + 1) * 128]
        cos4 = mkap(cosb, 0, [[0, 4], [64, 2], [1, 64]])
        sin4 = mkap(sinb, 0, [[0, 4], [64, 2], [1, 64]])
        swp = mkap(pp, 64, [[128, 4], [-64, 2], [1, 64]])
        pp4 = mkap(pp, 0, [[128, 4], [64, 2], [1, 64]])
        m0 = mkap(mst, 0, [[128, 4], [64, 2], [1, 64]])
        m1 = mkap(mst, 512, [[128, 4], [64, 2], [1, 64]])
        v.tensor_tensor(m0, pp4, cos4, ALU.mult)
        v.tensor_tensor(m1, swp, sin4, ALU.mult)
        v.drain()
        v.tensor_add(mst[:, 0:512], mst[:, 0:512], mst[:, 512:1024])
        v.drain()
        v.tensor_scalar(mst[:, 512:1024], mst[:, 0:512], 0.0, None, ALU.min)
        v.drain()

    def rope_elu_fin(v, dst):
        v.scalar_tensor_tensor(dst, mst[:, 0:512], 0.0, mst[:, 512:1024], ALU.max, ALU.add)
        v.drain()

    def emit_ffn(Xcur, ybuf, ws, first_block, prefused=False):
        BKF = [psS_[0], psS_[1], psP[0], psP[1]]
        dqm = []
        ln_stats(Xcur, ybuf, precopied=prefused)
        if not prefused:
            def _tree(v):
                w(v, 'T', 'S', 'V')
                v.tensor_max(gst.t_[:, 0:512], Xcur.t(0), Xcur.t(1))
                v.tensor_tensor(gst.t_[:, 1024:1536], Xcur.t(0), Xcur.t(1), ALU.min)
                v.drain()
                for j in range(2, NT):
                    v.tensor_max(gst.t_[:, 0:512], gst.t_[:, 0:512], Xcur.t(j))
                    v.tensor_tensor(gst.t_[:, 1024:1536], gst.t_[:, 1024:1536], Xcur.t(j), ALU.min)
                    v.drain()
                return v.drain()
            V(_tree)
        def _trc(te):
            w(te, 'V')
            last = None
            for c4 in range(4):
                last = te.matmul(psS_[0][:, c4 * 128:(c4 + 1) * 128],
                                 gst.t_[:, c4 * 128:(c4 + 1) * 128], identf[:, :],
                                 start=True, stop=True)
                last = te.matmul(psS_[1][:, c4 * 128:(c4 + 1) * 128],
                                 gst.t_[:, c4 * 128 + 1024:(c4 + 1) * 128 + 1024], identf[:, :],
                                 start=True, stop=True)
            return last
        T(_trc)
        def _red(v):
            w(v, 'T')
            for c4 in range(4):
                v.reduce_max(gst.t_[:, 512 + c4:512 + c4 + 1], psS_[0][:, c4 * 128:(c4 + 1) * 128], mybir.AxisListType.X)
                v.tensor_reduce(gst.t_[:, 516 + c4:516 + c4 + 1], psS_[1][:, c4 * 128:(c4 + 1) * 128], mybir.AxisListType.X, ALU.min)
            return v.drain()
        V(_red)
        def _trr(te):
            w(te, 'V')
            last = None
            for c4 in range(4):
                last = te.matmul(psS_[0][0:1, c4 * 128:(c4 + 1) * 128], gst.t_[:, 512 + c4:513 + c4],
                                 identf[:, :], start=True, stop=True)
                last = te.matmul(psS_[1][0:1, c4 * 128:(c4 + 1) * 128], gst.t_[:, 516 + c4:517 + c4],
                                 identf[:, :], start=True, stop=True)
            return last
        T(_trr)
        def _rows_pre(v):
            w(v, 'T', 'S')
            v.tensor_copy(rows32[0:1, 1024:1536], rowt[0:1, RW_M:RW_M + 512])
            v.tensor_copy(rows32[0:1, 512:1024], rowt[0:1, RW_RSTD:RW_RSTD + 512])
            return v.drain()
        V(_rows_pre)
        def _trm(te):
            w(te, 'V')
            last = None
            for c4 in range(4):
                last = te.transpose(psV_[0][:, c4 * 32:(c4 + 1) * 32],
                                    rows32[0:32, 1024 + c4 * 128:1024 + (c4 + 1) * 128], identf[0:32, 0:32])
                last = te.transpose(psV_[1][:, c4 * 32:(c4 + 1) * 32],
                                    rows32[0:32, 512 + c4 * 128:512 + (c4 + 1) * 128], identf[0:32, 0:32])
            return last
        T(_trm)
        def _rows_col(v):
            w(v, 'T')
            mcol = mkap(psV_[0], 0, [[32, 4]])
            rcol = mkap(psV_[1], 0, [[32, 4]])
            tmax = gst.t_[:, 512:516]
            tmin = gst.t_[:, 516:520]
            a1 = gst.t_[:, 520:524]
            a2 = gst.t_[:, 524:528]
            a3 = gst.t_[:, 528:532]
            rqc = gst.t_[:, 532:536]
            inscc = gst.t_[:, 536:540]
            nmc = gst.t_[:, 540:544]
            v.tensor_tensor(a1, tmax, mcol, ALU.subtract)
            v.tensor_tensor(a2, mcol, tmin, ALU.subtract)
            v.drain()
            v.tensor_max(a1, a1, a2)
            v.drain()
            v.reciprocal(a2, a1)
            v.tensor_tensor(a3, a1, rcol, ALU.mult)
            v.drain()
            v.tensor_scalar(rqc, a2, 127.0, None, ALU.mult)
            v.tensor_scalar(inscc, a3, 1e-7, 1.0 / 127.0, ALU.max, ALU.mult)
            v.drain()
            v.tensor_tensor(nmc, rqc, mcol, ALU.mult)
            v.drain()
            v.tensor_scalar(nmc, nmc, -1.0, None, ALU.mult)
            v.drain()
            v.tensor_scalar(nmc, nmc, cpool[:, CPQ:CPQ + 1], None, ALU.add)
            return v.drain()
        V(_rows_col)
        def _trr2(te):
            w(te, 'V')
            last = None
            for c4 in range(4):
                last = te.transpose(psS_[0][0:1, c4 * 128:(c4 + 1) * 128], gst.t_[:, 532 + c4:533 + c4], identf[:, :])
                last = te.transpose(psS_[1][0:1, c4 * 128:(c4 + 1) * 128], gst.t_[:, 540 + c4:541 + c4], identf[:, :])
                last = te.transpose(psV_[0][0:1, c4 * 128:(c4 + 1) * 128], gst.t_[:, 536 + c4:537 + c4], identf[:, :])
            return last
        T(_trr2)
        def _rows_fin(v):
            w(v, 'T')
            v.tensor_copy(rows32[0:1, 0:512], psS_[0][0:1, 0:512])
            v.tensor_copy(rows32[0:1, 512:1024], psS_[1][0:1, 0:512])
            v.tensor_copy(rows32[0:1, 1024:1536], psV_[0][0:1, 0:512])
            return v.drain()
        V(_rows_fin)
        mset('ppqf1', 'V')
        def _bx(te):
            w(te, 'V')
            te.matmul(psP[0][:, :], os32f[:], rows32[:, 0:512], start=True, stop=True)
            te.matmul(psP[1][:, :], os32f[:], rows32[:, 512:1024], start=True, stop=True)
            return te.matmul(psS_[0][:, :], os32f[:], rows32[:, 1024:1536], start=True, stop=True)
        T(_bx)
        def _b3c(v):
            w(v, 'T')
            v.tensor_copy(gst.t_[:, 0:512], psS_[0][:, :])
            return v.drain()
        V(_b3c)
        mset('ppqf0', 'V')
        qbase_s = CN['S']
        vq_marks = []
        for j in range(NT):
            def _qz(v, j=j, qs=qbase_s):
                w(v, 'T')
                if j >= 2:
                    v.wait_ge(SEMH['S'], qs + j - 1)
                p = 1024
                v.tensor_mul(gst.t_[:, p:p + 512], Xcur.t(j), psP[0][:, :])
                v.drain()
                it = itmp if j % 2 == 0 else itmp2
                v.tensor_tensor(it[:, 0:512], gst.t_[:, p:p + 512], psP[1][:, :], ALU.add)
                return v.drain()
            V(_qz)
            vq_marks.append(CN['V'])
        def _qcasts(sc, vq_marks=tuple(vq_marks)):
            for j in range(NT):
                sc.wait_ge(SEMH['V'], vq_marks[j])
                it = itmp if j % 2 == 0 else itmp2
                sc.copy(ybuf.t(j), it[:, 0:512]).then_inc(SEMH['S'], 1)
            return None
        sec('S', _qcasts, None)
        CN['S'] += NT
        mset('q128', 'S')
        mset('ppqf2', 'V')
        mset('ppqf3', 'V')
        for s_ in range(4):
            def _ldf(gp, s_=s_):
                mwait(gp, 'wsb%d' % (s_ % 2))
                stream_weight(gp, wf8T, ws[s_ % 2], NT, 512, s_ * 512)
                return None
            G(_ldf)
            for ot in range(4):
                og = s_ * 4 + ot
                def _mmf(te, s_=s_, ot=ot, og=og, first_block=first_block, qs=qbase_s):
                    w(te, 'dG')
                    mwait(te, 'ppqf%d' % (og % 4))
                    pp = BKF[og % 4]
                    last = None
                    if s_ == 0:
                        for j in range(NT):
                            te.wait_ge(SEMH['S'], qs + j + 1)
                            last = te.matmul(pp[:, :], ws[s_ % 2].tc(j, ot * 128, (ot + 1) * 128),
                                             ybuf.t(j), start=(j == 0), stop=(j == NT - 1))
                    else:
                        mwait(te, 'q128')
                        for j in range(NT):
                            last = te.matmul(pp[:, :], ws[s_ % 2].tc(j, ot * 128, (ot + 1) * 128),
                                             ybuf.t(j), start=(j == 0), stop=(j == NT - 1))
                    if first_block:
                        for j in range(NT):
                            last = te.matmul(psU[:, og:og + 1], ws[s_ % 2].tc(j, ot * 128, (ot + 1) * 128),
                                             onecolb[:], start=(j == 0), stop=(j == NT - 1))
                    return last
                T(_mmf)
                if first_block:
                    def _c1(v, og=og):
                        w(v, 'T')
                        v.tensor_scalar(c1col[:, og:og + 1], psU[:, og:og + 1], -128.0, None, ALU.mult)
                        return v.drain()
                    V(_c1)
                    mset('psU', 'V')
                def _dq(v, og=og):
                    w(v, 'T')
                    pp = BKF[og % 4]
                    v.scalar_tensor_tensor(gst.t_[:, 1024:1536], pp[:, :], c1col[:, og:og + 1], gst.t_[:, 0:512], ALU.add, ALU.mult)
                    v.drain()
                    v.tensor_scalar(f16st[:], gst.t_[:, 1024:1536], cws[:, og:og + 1], cfb[:, og:og + 1], ALU.mult, ALU.add)
                    v.drain()
                    v.tensor_copy(gst.t_[:, 1024:1536], f16st[:])
                    v.drain()
                    v.tensor_add(Xcur.t(og), Xcur.t(og), gst.t_[:, 1024:1536])
                    return v.drain()
                V(_dq)
                dqm.append(CN['V'])
                mset('ppqf%d' % (og % 4), 'V')
                if ot == 3:
                    mset('wsb%d' % (s_ % 2), 'T')
        mset('ppq0', 'V')
        mset('ppq1', 'V')
        return [dqm[st4 * 4 + 3] for st4 in range(4)]


    def emit_stageA_head(b, xdw=None):
        if xdw is None:
            def _ldx(sy, b=b):
                src3 = xT[:, b * BLK:(b + 1) * BLK].rearrange('(n p) c -> p n c', p=128)
                dst3 = Xa.t_[:, :].rearrange('p (n c) -> p n c', c=512)
                dma(sy, dst3, src3, 'dW')
                return None
            Y(_ldx)

            def _wx(v):
                w(v, 'V', 'dW')
                return v.drain()
            V(_wx)

            ln_stats(Xa, ybf)
        else:
            ln_stats(Xa, ybf, strip_dw=xdw)
        aplA = ln_apply(Xa, ybf)
        mset('yA', 'V')

        def eluq(q):
            t = mst if q < 2 else mst2
            return t[:, (q % 2) * 512:(q % 2) * 512 + 512]

        def rope2(v, pp, dst, tt, q):
            cosb = tabAc[:, (b * 4 + tt) * 128:(b * 4 + tt + 1) * 128]
            sinb = tabAs[:, (b * 4 + tt) * 128:(b * 4 + tt + 1) * 128]
            nh4 = dst.shape[1] // 128
            cos4 = mkap(cosb, 0, [[0, nh4], [64, 2], [1, 64]])
            sin4 = mkap(sinb, 0, [[0, nh4], [64, 2], [1, 64]])
            swp = mkap(pp, 64, [[128, nh4], [-64, 2], [1, 64]])
            pp4 = mkap(pp, 0, [[128, nh4], [64, 2], [1, 64]])
            d4 = mkap(dst, 0, [[128, nh4], [64, 2], [1, 64]])
            mh = eluq(q)
            m1 = mkap(mh, 0, [[128, nh4], [64, 2], [1, 64]])
            v.tensor_tensor(d4, pp4, cos4, ALU.mult)
            v.tensor_tensor(m1, swp, sin4, ALU.mult)
            v.drain()
            v.tensor_add(dst, dst, mh)
            v.drain()
            v.tensor_scalar(mh, dst, 0.0, None, ALU.min)
            v.drain()

        for s in range(4):
            def _ldw(gp, s=s):
                mwait(gp, 'wsb%d' % (s % 2))
                stream_weight(gp, wqT, WS[s % 2], NT, 512, s * 512)
                return None
            G(_ldw)
            vmarks = []
            sbase_e = CN['S']
            for tt in range(4):
                def _mmq(te, s=s, tt=tt, se=sbase_e, apl=tuple(aplA)):
                    w(te, 'dG')
                    if s == 0:
                        mwait(te, 'ppq%d' % (tt % 2))
                    else:
                        mwait(te, 'yA', 'ppq%d' % (tt % 2))
                    last = None
                    pp = psP[tt % 2]
                    for j in range(NT):
                        if s == 0:
                            te.wait_ge(SEMH['V'], apl[j // 2])
                        last = te.matmul(pp[:, :], ybf.tc(j, tt * 128, (tt + 1) * 128), WS[s % 2].t(j),
                                         start=(j == 0), stop=(j == NT - 1))
                    return last
                T(_mmq)
                def _rope_q(v, s=s, tt=tt, se=sbase_e):
                    w(v, 'T')
                    rope2(v, psP[tt % 2], qfB.tc(tt, s * 512, (s + 1) * 512), tt, tt)
                    return v.drain()
                V(_rope_q)
                mset('ppq%d' % (tt % 2), 'V')
                vmarks.append(CN['V'])
            def _ees(sc, vmarks=tuple(vmarks)):
                for tt in range(4):
                    sc.wait_ge(SEMH['V'], vmarks[tt])
                    sc.activation(eluq(tt), eluq(tt), AF.Exp).then_inc(SEMH['S'], 1)
                return None
            sec('S', _ees, None)
            CN['S'] += 4
            def _efs(v, s=s, se=sbase_e):
                for tt in range(4):
                    v.wait_ge(SEMH['S'], se + tt + 1)
                    v.scalar_tensor_tensor(qfB.tc(tt, s * 512, (s + 1) * 512),
                                           qfB.tc(tt, s * 512, (s + 1) * 512), 0.0,
                                           eluq(tt), ALU.max, ALU.add)
                    v.drain()
                return v.drain()
            V(_efs)
            mset('wsb%d' % (s % 2), 'T')

        def _ldk(gp):
            mwait(gp, 'wsb0')
            stream_weight(gp, wkT, WS[0], NT, 512, 0)
            return None
        G(_ldk)
        vmarks = []
        sbase_e = CN['S']
        for tt in range(4):
            def _mmk(te, tt=tt, se=sbase_e):
                w(te, 'dG')
                mwait(te, 'yA', 'ppq%d' % (tt % 2))
                last = None
                pp = psP[tt % 2]
                for j in range(NT):
                    last = te.matmul(pp[:, :], ybf.tc(j, tt * 128, (tt + 1) * 128), WS[0].t(j),
                                     start=(j == 0), stop=(j == NT - 1))
                return last
            T(_mmk)
            def _rope_k(v, tt=tt, se=sbase_e):
                w(v, 'T')
                rope2(v, psP[tt % 2], kfB.t(tt), tt, tt)
                return v.drain()
            V(_rope_k)
            mset('ppq%d' % (tt % 2), 'V')
            vmarks.append(CN['V'])
        def _eeks(sc, vmarks=tuple(vmarks)):
            for tt in range(4):
                sc.wait_ge(SEMH['V'], vmarks[tt])
                sc.activation(eluq(tt), eluq(tt), AF.Exp).then_inc(SEMH['S'], 1)
            return None
        sec('S', _eeks, None)
        CN['S'] += 4
        def _efks(v, se=sbase_e):
            for tt in range(4):
                v.wait_ge(SEMH['S'], se + tt + 1)
                v.scalar_tensor_tensor(kfB.t(tt), kfB.t(tt), 0.0,
                                       eluq(tt), ALU.max, ALU.add)
                v.drain()
            return v.drain()
        V(_efks)

        mset('wsb0', 'T')
        def _ldv(gp):
            mwait(gp, 'wsb1')
            stream_weight(gp, wvT, WS[1], NT, 512, 0)
            return None
        G(_ldv)
        for tt in range(4):
            def _mmv(te, tt=tt):
                w(te, 'dG')
                mwait(te, 'yA', 'ppq%d' % (tt % 2))
                last = None
                pp = psP[tt % 2]
                for j in range(NT):
                    last = te.matmul(pp[:, :], ybf.tc(j, tt * 128, (tt + 1) * 128), WS[1].t(j),
                                     start=(j == 0), stop=(j == NT - 1))
                return last
            T(_mmv)
            def _ev_v(v, tt=tt):
                w(v, 'T')
                pp = psP[tt % 2]
                src = pp[:, :].rearrange('p (h c) -> p h c', c=128)
                dstv = mkap(vaug.t_, tt * 516, [[129, 4], [1, 128]])
                v.tensor_copy(dstv, src)
                return v.drain()
            V(_ev_v)
            mset('ppq%d' % (tt % 2), 'V')

        mset('wsb1', 'T')
        for h2 in range(0, NH, 2):
            def _tr_q(te, h2=h2):
                w(te, 'V')
                last = None
                for hh in range(2):
                    for tt in range(4):
                        last = te.transpose(psT_[:, hh * 512 + tt * 128:hh * 512 + (tt + 1) * 128],
                                            qfB.tc(tt, (h2 + hh) * 128, (h2 + hh + 1) * 128), ident[:])
                return last
            T(_tr_q)
            def _ev_q(v, h2=h2):
                w(v, 'T')
                v.tensor_copy(qfA.t(h2), psT_[:, 0:512])
                v.tensor_copy(qfA.t(h2 + 1), psT_[:, 512:1024])
                return v.drain()
            V(_ev_q)
        for h2 in range(0, KVH, 2):
            def _tr_k(te, h2=h2):
                w(te, 'V')
                last = None
                for hh in range(2):
                    for tt in range(4):
                        last = te.transpose(psT_[:, hh * 512 + tt * 128:hh * 512 + (tt + 1) * 128],
                                            kfB.tc(tt, (h2 + hh) * 128, (h2 + hh + 1) * 128), ident[:])
                return last
            T(_tr_k)
            def _ev_k(v, h2=h2):
                w(v, 'T')
                v.tensor_copy(kfA.t(h2), psT_[:, 0:512])
                v.tensor_copy(kfA.t(h2 + 1), psT_[:, 512:1024])
                return v.drain()
            V(_ev_k)
        mset('qkvA', 'V')



    scdiv = [0, 0, 0, 0]
    scS_pred = [0, 0]
    def emit_scan(bglob):
        AVB = [psV_[0], psV_[1], psP[0], psP[1]]
        for cc in range(4):
            sdbase = CN['S']
            tbase = CN['T']
            vbase = CN['V']
            vupd = CN['V']
            sdw = [(sdbase + hg - 3) if hg >= 4 else scdiv[hg % 4] for hg in range(8)]
            scw = list(scS_pred)
            def _scanT(te, cc=cc, tb=tbase, vb=vbase, sdw=tuple(sdw), scw=tuple(scw)):
                mwait(te, 'qkvA')
                for hg in range(8):
                    h0 = hg * 2
                    kv0, kv1 = h0 // 4, (h0 + 1) // 4
                    sw = (vb + 2 * (hg - 2) + 1) if hg >= 2 else scw[hg % 2]
                    if sw > 0:
                        te.wait_ge(SEMH['V'], sw)
                    pp = psS_[hg % 2]
                    te.matmul(pp[:, 0:128], kfA.tc(kv0, cc * 128, (cc + 1) * 128),
                              qfA.tc(h0, cc * 128, (cc + 1) * 128), start=True, stop=True)
                    te.matmul(pp[:, 128:256], kfA.tc(kv1, cc * 128, (cc + 1) * 128),
                              qfA.tc(h0 + 1, cc * 128, (cc + 1) * 128), start=True,
                              stop=True).then_inc(SEMH['T'], 1)
                    te.wait_ge(SEMH['V'], vb + 2 * hg + 1)
                    if sdw[hg] > 0:
                        te.wait_ge(SEMH['S'], sdw[hg])
                    ppa = AVB[hg % 4]
                    ms0 = mst[:, (hg % 2) * 512:(hg % 2) * 512 + 128]
                    ms1 = mst[:, (hg % 2) * 512 + 128:(hg % 2) * 512 + 256]
                    te.matmul(ppa[:, 0:129], ms0,
                              vaug.t_[:, cc * 516 + kv0 * 129: cc * 516 + kv0 * 129 + 129],
                              start=True, stop=False)
                    te.matmul(ppa[:, 0:129], qfA.tc(h0, cc * 128, (cc + 1) * 128),
                              stbf[:, kv0 * 129:(kv0 + 1) * 129], start=False, stop=True)
                    te.matmul(ppa[:, 256:385], ms1,
                              vaug.t_[:, cc * 516 + kv1 * 129: cc * 516 + kv1 * 129 + 129],
                              start=True, stop=False)
                    te.matmul(ppa[:, 256:385], qfA.tc(h0 + 1, cc * 128, (cc + 1) * 128),
                              stbf[:, kv1 * 129:(kv1 + 1) * 129], start=False,
                              stop=True).then_inc(SEMH['T'], 1)
                return None
            sec('T', _scanT, None)
            CN['T'] += 16
            for hg in range(8):
                def _mask(v, hg=hg, tb=tbase):
                    v.wait_ge(SEMH['T'], tb + 2 * hg + 1)
                    pp = psS_[hg % 2]
                    src = mkap(pp, 0, [[128, 2], [1, 128]])
                    mk = mkap(maskT, 0, [[0, 2], [1, 128]])
                    dst = mkap(mst, (hg % 2) * 512, [[128, 2], [1, 128]])
                    v.tensor_tensor(dst, src, mk, ALU.mult)
                    return v.drain()
                V(_mask)
                def _recip(v, cc=cc, hg=hg, tb=tbase):
                    v.wait_ge(SEMH['T'], tb + 2 * hg + 2)
                    pp = AVB[hg % 4]
                    dn = mkap(pp, 128, [[256, 2], [1, 1]])
                    v.tensor_scalar(gst.t_[:, 1536 + hg * 4:1538 + hg * 4], dn, EPS_A, None, ALU.add)
                    v.drain()
                    v.reciprocal(gst.t_[:, 1538 + hg * 4:1540 + hg * 4], gst.t_[:, 1536 + hg * 4:1538 + hg * 4])
                    return v.drain()
                V(_recip)
            for hg in range(8):
                scS_pred[hg % 2] = vbase + 2 * hg + 1
            def _sdivs(sc, cc=cc, vb=vbase):
                for hg in range(8):
                    h0 = hg * 2
                    sc.wait_ge(SEMH['V'], vb + 2 * hg + 2)
                    pp = AVB[hg % 4]
                    for i in range(2):
                        ins = sc.activation(qfB.tc(cc, (h0 + i) * 128, (h0 + i + 1) * 128),
                                            pp[:, i * 256:i * 256 + 128], AF.Identity,
                                            scale=gst.t_[:, 1538 + hg * 4 + i:1539 + hg * 4 + i])
                        if i == 1:
                            ins.then_inc(SEMH['S'], 1)
                return None
            sec('S', _sdivs, None)
            CN['S'] += 8
            for hg in range(8):
                scdiv[hg % 4] = sdbase + hg + 1
            for rnd in range(2):
                def _mmu(te, cc=cc, rnd=rnd):
                    mwait(te, 'psU')
                    kv0, kv1 = rnd * 2, rnd * 2 + 1
                    te.matmul(psU[:, 0:129], kfB.tc(cc, kv0 * 128, (kv0 + 1) * 128),
                              vaug.t_[:, cc * 516 + kv0 * 129: cc * 516 + kv0 * 129 + 129],
                              start=True, stop=True)
                    return te.matmul(psU[:, 256:385], kfB.tc(cc, kv1 * 128, (kv1 + 1) * 128),
                                     vaug.t_[:, cc * 516 + kv1 * 129: cc * 516 + kv1 * 129 + 129],
                                     start=True, stop=True)
                T(_mmu)
                def _upd(v, rnd=rnd):
                    w(v, 'T')
                    kv0, kv1 = rnd * 2, rnd * 2 + 1
                    v.tensor_add(stLa[:, kv0 * 129:(kv0 + 1) * 129], stLa[:, kv0 * 129:(kv0 + 1) * 129], psU[:, 0:129])
                    v.tensor_add(stLa[:, kv1 * 129:(kv1 + 1) * 129], stLa[:, kv1 * 129:(kv1 + 1) * 129], psU[:, 256:385])
                    v.drain()
                    v.tensor_copy(stbf[:, kv0 * 129:(kv0 + 1) * 129], stLa[:, kv0 * 129:(kv0 + 1) * 129])
                    v.tensor_copy(stbf[:, kv1 * 129:(kv1 + 1) * 129], stLa[:, kv1 * 129:(kv1 + 1) * 129])
                    return v.drain()
                V(_upd)
                mset('psU', 'V')

    def emit_out_wo(bglob):
        for h2 in range(0, NH, 2):
            def _tr_o(te, h2=h2):
                w(te, 'V', 'S')
                last = None
                for hh in range(2):
                    for tt in range(4):
                        last = te.transpose(psT_[:, hh * 512 + tt * 128:hh * 512 + (tt + 1) * 128],
                                            qfB.tc(tt, (h2 + hh) * 128, (h2 + hh + 1) * 128), ident[:])
                return last
            T(_tr_o)
            def _ev_o(v, h2=h2):
                w(v, 'T')
                v.tensor_copy(qfA.t(h2), psT_[:, 0:512])
                v.tensor_copy(qfA.t(h2 + 1), psT_[:, 512:1024])
                return v.drain()
            V(_ev_o)
        mset('outA', 'V')
        for s_ in range(4):
            def _ldwo(gp, s_=s_):
                mwait(gp, 'wsb%d' % (s_ % 2))
                stream_weight(gp, woaT, WS[s_ % 2], NT, 512, s_ * 512)
                return None
            G(_ldwo)
            for dt_ in range(4):
                dmt = s_ * 4 + dt_
                def _mmo(te, s_=s_, dt_=dt_, dmt=dmt):
                    w(te, 'dG')
                    mwait(te, 'outA', 'ppq%d' % (dmt % 2))
                    pp = psP[dmt % 2]
                    last = None
                    for j in range(NT):
                        last = te.matmul(pp[:, :], WS[s_ % 2].tc(j, dt_ * 128, (dt_ + 1) * 128),
                                         qfA.t(j), start=(j == 0), stop=(j == NT - 1))
                    return last
                T(_mmo)
                def _res(v, dmt=dmt):
                    w(v, 'T')
                    v.tensor_add(Xa.t(dmt), Xa.t(dmt), psP[dmt % 2][:, :])
                    v.drain()
                    v.tensor_copy(ybf.t(dmt), Xa.t(dmt))
                    if dmt == 0:
                        v.tensor_copy(gst.t_[:, 0:512], Xa.t(0))
                        v.tensor_copy(gst.t_[:, 1024:1536], Xa.t(0))
                    else:
                        v.tensor_max(gst.t_[:, 0:512], gst.t_[:, 0:512], Xa.t(dmt))
                        v.tensor_tensor(gst.t_[:, 1024:1536], gst.t_[:, 1024:1536], Xa.t(dmt), ALU.min)
                    return v.drain()
                V(_res)
                mset('ppq%d' % (dmt % 2), 'V')
            mset('wsb%d' % (s_ % 2), 'T')


    def emit_stageA(milestone):
        xdw = None
        for b in range(NB):
            emit_stageA_head(b, xdw)
            if milestone == 'proj' and b == 0:
                return 'stop'
            emit_scan(b)
            mset('ppq0', 'S')
            mset('ppq1', 'S')
            emit_out_wo(b)
            if milestone == 'attnA' and b == 0:
                return 'stop'
            dqm = emit_ffn(Xa, ybf, WS, first_block=(b == 0), prefused=True)
            if milestone == 'ffn1' and b == 0:
                return 'stop'
            xdw = [] if b + 1 < NB else None
            for st4 in range(4):
                def _stx(sy, b=b, st4=st4, vw=dqm[st4]):
                    sy.wait_ge(s_V, vw)
                    src3 = Xa.t_[:, st4 * 2048:(st4 + 1) * 2048].rearrange('p (n c) -> p n c', c=512)
                    dst3 = x2s[st4 * 512:(st4 + 1) * 512, b * BLK:(b + 1) * BLK].rearrange('(n p) c -> p n c', p=128)
                    dma(sy, dst3, src3, 'dO')
                    return None
                Y(_stx)
                if b + 1 < NB:
                    def _ldxs(sy, b=b, st4=st4):
                        src3 = xT[st4 * 512:(st4 + 1) * 512, (b + 1) * BLK:(b + 2) * BLK].rearrange('(n p) c -> p n c', p=128)
                        dst3 = Xa.t_[:, st4 * 2048:(st4 + 1) * 2048].rearrange('p (n c) -> p n c', c=512)
                        dma(sy, dst3, src3, 'dW')
                        return None
                    Y(_ldxs)
                    xdw.append(CN['dW'])
        return None

    r = emit_stageA(stop_after)
    if r == 'stop':
        if stop_after == 'proj':
            def _fin(sy):
                w(sy, 'S', 'V', 'T')
                for nm, src, sh, dt in (('qfA', qfA.t_[:, :], [128, NT * 512], BF16),
                                    ('kfA', kfA.t_[:, :], [128, 4 * 512], BF16),
                                    ('vaug', vaug.t_[:, :], [128, 4 * 516], BF16),
                                    ('ybf', ybf.t_[:, :], [128, NT * 512], BF16),
                                    ('m', rowt[0:1, 0:512], [1, 512], F32),
                                    ('rstd', rowt[0:1, 512:1024], [1, 512], F32)):
                    d = dbg(nm, sh, dt)
                    if d is not None:
                        dma(sy, d, src, 'dO')
                sy.wait_ge(s_dO, CN['dO'])
                return None
            Y(_fin)
            ctx.close()
            return nc, dbg_list


        if stop_after == 'attnA':
            def _fin(sy):
                w(sy, 'S', 'V', 'T')
                d = dbg('X1', [128, NT * 512], F32)
                if d is not None:
                    dma(sy, d, Xa.t_[:, :], 'dO')
                sy.wait_ge(s_dO, CN['dO'])
                return None
            Y(_fin)
            ctx.close()
            return nc, dbg_list


        if stop_after == 'ffn1':
            def _fin(sy):
                w(sy, 'S', 'V', 'T')
                d = dbg('X2', [128, NT * 512], F32)
                if d is not None:
                    dma(sy, d, Xa.t_[:, :], 'dO')
                sy.wait_ge(s_dO, CN['dO'])
                return None
            Y(_fin)
            ctx.close()
            return nc, dbg_list


    if stop_after == 'stageA':
        def _fin(sy):
            w(sy, 'S', 'V', 'T')
            sy.wait_ge(SEMH['dO'], CN['dO'])
            d = dbg('X2full', [D, S_], F32)
            if d is not None:
                dma(sy, d, x2s[:, :], 'dO')
            sy.wait_ge(s_dO, CN['dO'])
            return None
        Y(_fin)
        ctx.close()
        return nc, dbg_list

    stA_scope.close()
    stAB = ctx.enter_context(contextlib.ExitStack())
    latq = Mat(sbt('latq', [128, 4 * 2048], BF16, stAB), 4, 2048)
    vaugB = Mat(sbt('vaugB', [128, 16 * 2064], BF16, stAB), 16, 2064)

    stA2 = ctx.enter_context(contextlib.ExitStack())
    Xa2 = Mat(sbt('Xa2', [128, NT * 512], F32, stA2), NT, 512)
    y3 = Mat(sbt('y3', [128, NT * 512], BF16, stA2), NT, 512)
    kvlat = Mat(sbt('kvlat', [128, 4 * 512], BF16, stA2), 4, 512)
    tabBc = sbt('tabBc', [128, 4 * 512], BF16, stA2)
    tabBs = sbt('tabBs', [128, 4 * 512], BF16, stA2)
    wsA2 = Mat(wsA2t, NT, 512)
    ws_kn8 = Mat(wsA2t, 8, 1024)
    ws_vu = Mat(wsA2t, 4, 2048)
    BK4 = [psP[0], psP[1], psS_[0], psS_[1]]

    def _a2init(v):
        w(v, 'V', 'T', 'S', 'dO', 'dW', 'dG')
        v.memset(mkap(vaugB.t_, 128, [[2064, 16], [129, 16], [1, 1]]), 1.0)
        return v.drain()
    V(_a2init)
    for b in range(NB):
        def _ldtb(sy, b=b):
            if b > 0:
                mwait(sy, 'ekrdone')
            dma(sy, tabBc[:, :], tabBcin[:, b * 2048:(b + 1) * 2048], 'dW')
            dma(sy, tabBs[:, :], tabBsin[:, b * 2048:(b + 1) * 2048], 'dW')
            return None
        Y(_ldtb)
        if b == 0:
            def _ldx2(sy, b=b):
                w(sy, 'V', 'dO')
                src3 = x2s[:, b * BLK:(b + 1) * BLK].rearrange('(n p) c -> p n c', p=128)
                dst3 = Xa2.t_[:, :].rearrange('p (n c) -> p n c', c=512)
                dma(sy, dst3, src3, 'dW')
                return None
            Y(_ldx2)
            def _wx2(v):
                w(v, 'V', 'dW')
                return v.drain()
            V(_wx2)
            ln_stats(Xa2, y3)
            vmst_a2 = CN['V']
            aplA2 = ln_apply(Xa2, y3)
            mset('yA', 'V')
            def _ldx2n(sy):
                mwait(sy, 'yA')
                src3 = x2s[:, 1 * BLK:2 * BLK].rearrange('(n p) c -> p n c', p=128)
                dst3 = Xa2.t_[:, :].rearrange('p (n c) -> p n c', c=512)
                dma(sy, dst3, src3, 'dW')
                return None
            Y(_ldx2n)
        for wsrc, bias, dst in ((wqdT, cbqd, 'q'), (wkvdT, cbkvd, 'kv')):
            def _ldd0(gp, wsrc=wsrc):
                mwait(gp, 'a2h0')
                stream_weight_rt(gp, wsrc, wsA2, 0, 8, 0, 512, 0)
                return None
            G(_ldd0)
            dg0 = CN['dG']
            def _ldd1(gp, wsrc=wsrc):
                mwait(gp, 'a2h1')
                stream_weight_rt(gp, wsrc, wsA2, 8, 8, 8, 512, 0)
                return None
            G(_ldd1)
            dg1 = CN['dG']
            for lt in range(4):
                def _mmd0(te, lt=lt, dg0=dg0, first=(dst == 'q'), apl=tuple(aplA2)):
                    te.wait_ge(SEMH['dG'], dg0)
                    if first:
                        mwait(te, 'ppq%d' % lt)
                    else:
                        mwait(te, 'yA', 'ppq%d' % lt)
                    pp = BK4[lt]
                    last = None
                    for j in range(8):
                        if first:
                            te.wait_ge(SEMH['V'], apl[j // 2])
                        last = te.matmul(pp[:, :], wsA2.tc(j, lt * 128, (lt + 1) * 128), y3.t(j),
                                         start=(j == 0), stop=False)
                    return last
                T(_mmd0)
            mset('a2h0', 'T')
            for lt in range(4):
                def _mmd1(te, lt=lt, dg1=dg1, first=(dst == 'q'), apl=tuple(aplA2)):
                    te.wait_ge(SEMH['dG'], dg1)
                    pp = BK4[lt]
                    last = None
                    for j in range(8, NT):
                        if first:
                            te.wait_ge(SEMH['V'], apl[j // 2])
                        last = te.matmul(pp[:, :], wsA2.tc(j, lt * 128, (lt + 1) * 128), y3.t(j),
                                         start=False, stop=(j == NT - 1))
                    return last
                T(_mmd1)
                def _evd(v, lt=lt, bias=bias, dst=dst, b=b):
                    w(v, 'T')
                    pp = BK4[lt]
                    if dst == 'q':
                        v.tensor_scalar(latq.tc(lt, b * 512, (b + 1) * 512), pp[:, :], bias[:, lt:lt + 1], None, ALU.add)
                    else:
                        v.tensor_scalar(kvlat.t(lt), pp[:, :], bias[:, lt:lt + 1], None, ALU.add)
                    return v.drain()
                V(_evd)
                mset('ppq%d' % lt, 'V')
            mset('a2h1', 'T')
        mset('kvlat', 'V')
        def _ldkn(gp):
            mwait(gp, 'a2h0')
            stream_weight_rt(gp, wknT, ws_kn8, 0, 4, 0, 1024, 0)
            return None
        G(_ldkn)
        dgm_kn = CN['dG']
        sbase_kn = CN['S']
        tpred_kn = []
        for ot in range(8):
            def _mkn(te, ot=ot, dgm=dgm_kn, sb=sbase_kn):
                te.wait_ge(SEMH['dG'], dgm)
                if ot >= 4:
                    te.wait_ge(SEMH['S'], sb + ot - 3)
                    mwait(te, 'kvlat')
                else:
                    mwait(te, 'kvlat', 'ppq%d' % (ot % 4))
                pp = BK4[ot % 4]
                last = None
                for lt in range(4):
                    last = te.matmul(pp[:, :], ws_kn8.tc(lt, ot * 128, (ot + 1) * 128), kvlat.t(lt),
                                     start=(lt == 0), stop=(lt == 3))
                return last
            T(_mkn)
            tpred_kn.append(CN['T'])
            if ot == 7:
                mset('a2h0', 'T')
        dO0_kn = CN['dO']
        def _ekns(sc, tpred=tuple(tpred_kn), sb=sbase_kn, dO0=dO0_kn, vm=vmst_a2):
            sc.wait_ge(SEMH['V'], vm)
            sc.wait_ge(s_dO, dO0)
            for ot in range(8):
                sc.wait_ge(SEMH['T'], tpred[ot])
                if ot >= 2:
                    sc.wait_ge(s_dO, dO0 + (ot - 1) * 16)
                sc.activation(mst[:, (ot % 2) * 512:(ot % 2) * 512 + 512], BK4[ot % 4][:, :], AF.Identity,
                              bias=cbkn[:, ot:ot + 1]).then_inc(SEMH['S'], 1)
            return None
        sec('S', _ekns, None)
        CN['S'] += 8
        for ot in range(4, 8):
            MK['ppq%d' % (ot % 4)] = ('S', sbase_kn + ot + 1)
        for ot in range(8):
            def _skn(sy, ot=ot, b=b, sb=sbase_kn):
                sy.wait_ge(SEMH['S'], sb + ot + 1)
                dma(sy, kAs[ot * 128:(ot + 1) * 128, b * BLK:(b + 1) * BLK], mst[:, (ot % 2) * 512:(ot % 2) * 512 + 512], 'dO')
                return None
            Y(_skn)
        def _ldkr(gp):
            mwait(gp, 'a2h1')
            stream_weight_rt(gp, wkrT, ws_kn8, 0, 4, 4, 1024, 0)
            return None
        G(_ldkr)
        dgm_kr = CN['dG']
        for jt in range(4):
            def _mkr(te, jt=jt, dgm=dgm_kr):
                te.wait_ge(SEMH['dG'], dgm)
                mwait(te, 'kvlat', 'ppq0', 'ppq1')
                last = None
                for lt in range(4):
                    last = te.matmul(psP[0][:, :], ws_kn8.tc(4 + lt, jt * 128, (jt + 1) * 128), kvlat.t(lt),
                                     start=(lt == 0), stop=(lt == 3))
                for lt in range(4):
                    last = te.matmul(psP[1][:, :], ws_kn8.tc(4 + lt, (jt + 4) * 128, (jt + 5) * 128), kvlat.t(lt),
                                     start=(lt == 0), stop=(lt == 3))
                return last
            T(_mkr)
            if jt == 3:
                mset('a2h1', 'T')
            def _ekr(v, jt=jt):
                w(v, 'T', 'dO', 'S', 'dW')
                v.tensor_scalar(gst.t_[:, 0:512], psP[0][:, :], cbkr[:, jt:jt + 1], None, ALU.add)
                v.tensor_scalar(gst.t_[:, 512:1024], psP[1][:, :], cbkr[:, jt + 4:jt + 5], None, ALU.add)
                v.drain()
                v.tensor_mul(gst.t_[:, 1024:1536], gst.t_[:, 0:512], tabBc[:, jt * 512:(jt + 1) * 512])
                v.tensor_mul(gst.t_[:, 1536:2048], gst.t_[:, 512:1024], tabBs[:, jt * 512:(jt + 1) * 512])
                v.drain()
                v.tensor_sub(mst[:, 0:512], gst.t_[:, 1024:1536], gst.t_[:, 1536:2048])
                v.drain()
                v.tensor_mul(gst.t_[:, 1024:1536], gst.t_[:, 512:1024], tabBc[:, jt * 512:(jt + 1) * 512])
                v.tensor_mul(gst.t_[:, 1536:2048], gst.t_[:, 0:512], tabBs[:, jt * 512:(jt + 1) * 512])
                v.drain()
                v.tensor_add(mst[:, 512:1024], gst.t_[:, 1024:1536], gst.t_[:, 1536:2048])
                return v.drain()
            V(_ekr)
            mset('ppq0', 'V')
            mset('ppq1', 'V')
            def _skr(sy, jt=jt, b=b):
                w(sy, 'V')
                dma(sy, kAs[(8 + jt) * 128:(9 + jt) * 128, b * BLK:(b + 1) * BLK], mst[:, 0:512], 'dO')
                dma(sy, kAs[(12 + jt) * 128:(13 + jt) * 128, b * BLK:(b + 1) * BLK], mst[:, 512:1024], 'dO')
                return None
            Y(_skr)
            if jt == 3:
                mset('ekrdone', 'V')
        if b + 1 < NB:
            def _wko(v):
                w(v, 'dO', 'dW')
                return v.drain()
            V(_wko)
            ln_stats(Xa2, y3)
            vmst_a2 = CN['V']
            mset('ppq0', 'V')
            mset('ppq1', 'V')
            aplA2 = ln_apply(Xa2, y3)
            mset('yA', 'V')
            if b + 2 < NB:
                def _ldx2n2(sy, b=b):
                    mwait(sy, 'yA')
                    src3 = x2s[:, (b + 2) * BLK:(b + 3) * BLK].rearrange('(n p) c -> p n c', p=128)
                    dst3 = Xa2.t_[:, :].rearrange('p (n c) -> p n c', c=512)
                    dma(sy, dst3, src3, 'dW')
                    return None
                Y(_ldx2n2)

        def _ldvu0(gp):
            mwait(gp, 'a2h0')
            stream_weight_rt(gp, wvuT, ws_vu, 0, 2, 0, 2048, 0)
            return None
        G(_ldvu0)
        dgv0 = CN['dG']
        def _ldvu1(gp):
            mwait(gp, 'a2h1')
            stream_weight_rt(gp, wvuT, ws_vu, 2, 2, 2, 2048, 0)
            return None
        G(_ldvu1)
        dgv1 = CN['dG']
        sbase_vu = CN['S']
        tpred_vu = []
        for tt in range(4):
            for sv in range(4):
                bank = (tt * 4 + sv) % 4
                idx = tt * 4 + sv
                def _mvu0(te, tt=tt, sv=sv, bank=bank, idx=idx, dgv0=dgv0, sb=sbase_vu):
                    te.wait_ge(SEMH['dG'], dgv0)
                    if idx >= 4:
                        te.wait_ge(SEMH['S'], sb + idx - 3)
                        mwait(te, 'kvlat')
                    else:
                        mwait(te, 'kvlat', 'ppq%d' % bank)
                    pp = BK4[bank]
                    last = None
                    for lt in range(2):
                        last = te.matmul(pp[:, :], kvlat.tc(lt, tt * 128, (tt + 1) * 128),
                                         ws_vu.tc(lt, sv * 512, (sv + 1) * 512), start=(lt == 0), stop=False)
                    return last
                T(_mvu0)
                if tt == 3 and sv == 3:
                    mset('a2h0', 'T')
                def _mvu1(te, tt=tt, sv=sv, bank=bank, dgv1=dgv1):
                    te.wait_ge(SEMH['dG'], dgv1)
                    pp = BK4[bank]
                    for lt in range(2, 4):
                        te.matmul(pp[:, :], kvlat.tc(lt, tt * 128, (tt + 1) * 128),
                                  ws_vu.tc(lt, sv * 512, (sv + 1) * 512), start=False, stop=False)
                    return te.matmul(pp[:, :], os32b[:], bvu32[:, sv * 512:(sv + 1) * 512], start=False, stop=True)
                T(_mvu1)
                tpred_vu.append(CN['T'])
        mset('a2h1', 'T')
        def _evus(sc, tpred=tuple(tpred_vu), sb=sbase_vu, b=b, vm=vmst_a2):
            sc.wait_ge(SEMH['V'], vm)
            for tt in range(4):
                for sv in range(4):
                    idx = tt * 4 + sv
                    bank = idx % 4
                    sc.wait_ge(SEMH['T'], tpred[idx])
                    pp = BK4[bank]
                    src = pp[:, :].rearrange('p (h c) -> p h c', c=128)
                    dstv = mkap(vaugB.t_, (b * 4 + tt) * 2064 + 4 * sv * 129, [[129, 4], [1, 128]])
                    sc.copy(dstv, src).then_inc(SEMH['S'], 1)
            return None
        sec('S', _evus, None)
        CN['S'] += 16
        for idx in range(12, 16):
            MK['ppq%d' % (idx % 4)] = ('S', sbase_vu + idx + 1)

    stA2.close()

    if stop_after == 'a2':
        def _fin(sy):
            w(sy, 'S', 'V', 'T')
            sy.wait_ge(SEMH['dO'], CN['dO'])
            for nm, src, sh, dt in (('kAfull', kAs[:, :], [D, S_], BF16),
                                     ('latq', latq.t_[:, :], [128, 4 * 2048], BF16),
                                     ('vaugB', vaugB.t_[:, :], [128, 16 * 2064], BF16)):
                d = dbg(nm, sh, dt)
                if d is not None:
                    dma(sy, d, src, 'dO')
            sy.wait_ge(s_dO, CN['dO'])
            return None
        Y(_fin)
        ctx.close()
        return nc, dbg_list

    stB = ctx.enter_context(contextlib.ExitStack())
    wqut = sbt('wqut', [128, 4 * 2048], BF16, stB)
    wqu = Mat(wqut, 4, 2048)
    kAh = sbt('kAh', [128, 2048], BF16, stB)
    kAh2 = sbt('kAh2', [128, 2048], BF16, stB)
    KAHB = [kAh, kAh2]
    qAh = sbt('qAh', [128, 2048], BF16, stB)
    expb = sbt('expb', [128, 16 * 512], BF16, stB)
    aostg = sbt('aostg', [128, 512], BF16, stB)

    def _ldqu(gp):
        w(gp, 'T', 'V', 'S', 'dO')
        stream_weight(gp, wquT, wqu, 4, 2048, 0)
        return None
    G(_ldqu)

    SCB = [psP[0], psP[1], psS_[0], psS_[1]]
    scbmark = [0, 0, 0, 0]
    dw_kh = [0, 0]

    def emit_ldkh(h):
        def _ldkh(sy, h=h):
            mwait(sy, 'kAh%d' % (h % 2))
            w(sy, 'T', 'V', 'S', 'dO') if h == 0 else None
            dma(sy, KAHB[h % 2][:, :], kAs[h * 128:(h + 1) * 128, :], 'dW')
            return None
        Y(_ldkh)
        dw_kh[h % 2] = CN['dW']

    emit_ldkh(0)
    for h in range(NH):
        kbuf = KAHB[h % 2]
        dwm_kh = dw_kh[h % 2]
        for qb in range(4):
            def _mqu(te, h=h, qb=qb):
                w(te, 'dG')
                mwait(te, 'psVav', 'psVq%d' % (qb % 2))
                if h == 0 and qb == 0:
                    w(te, 'V', 'S')
                pp = psV_[qb % 2]
                last = None
                for lt in range(4):
                    last = te.matmul(pp[:, :], wqu.tc(lt, h * 128, (h + 1) * 128),
                                     latq.tc(lt, qb * 512, (qb + 1) * 512), start=(lt == 0), stop=(lt == 3))
                return last
            T(_mqu)
            def _equ(v, h=h, qb=qb):
                w(v, 'T')
                v.tensor_scalar(qAh[:, qb * 512:(qb + 1) * 512], psV_[qb % 2][:, :], cbqu[:, h:h + 1], None, ALU.add)
                return v.drain()
            V(_equ)
            mset('psVq%d' % (qb % 2), 'V')
        mset('qAh', 'V')
        if h + 1 < NH:
            emit_ldkh(h + 1)
        for qb in range(4):
            sbase = CN['S']
            tbase = CN['T']
            smw = [(sbase + kt - 3) if kt >= 4 else scbmark[kt % 4] for kt in range(16)]
            def _mscs(te, h=h, qb=qb, smw=tuple(smw), kbuf=kbuf, dwm_kh=dwm_kh):
                te.wait_ge(SEMH['dW'], dwm_kh)
                mwait(te, 'qAh')
                for kt in range(16):
                    if smw[kt] > 0:
                        te.wait_ge(SEMH['S'], smw[kt])
                    te.matmul(SCB[kt % 4][:, :], kbuf[:, kt * 128:(kt + 1) * 128],
                              qAh[:, qb * 512:(qb + 1) * 512], start=True,
                              stop=True).then_inc(SEMH['T'], 1)
                return None
            sec('T', _mscs, None)
            CN['T'] += 16
            def _exps(sc, h=h, qb=qb, tbase=tbase):
                for kt in range(16):
                    sc.wait_ge(SEMH['T'], tbase + kt + 1)
                    sc.activation(expb[:, kt * 512:(kt + 1) * 512], SCB[kt % 4][:, :], AF.Exp,
                                  scale=cpool[:, CPS:CPS + 1]).then_inc(SEMH['S'], 1)
                return None
            sec('S', _exps, None)
            CN['S'] += 16
            for kt in range(16):
                scbmark[kt % 4] = sbase + kt + 1
            if qb == 3:
                mset('kAh%d' % (h % 2), 'T')
            savb = CN['S']
            def _mavs(te, h=h, qb=qb, savb=savb):
                mwait(te, 'psVav', 'qAh')
                for qt in range(4):
                    pp = psV_[qt // 2]
                    c0 = (qt % 2) * 256
                    for kt in range(16):
                        if qt == 0 or kt == 15:
                            te.wait_ge(SEMH['S'], savb - 16 + kt + 1)
                        last = te.matmul(pp[:, c0:c0 + 129], expb[:, kt * 512 + qt * 128: kt * 512 + (qt + 1) * 128],
                                         vaugB.t_[:, kt * 2064 + h * 129: kt * 2064 + h * 129 + 129],
                                         start=(kt == 0), stop=(kt == 15))
                    last.then_inc(SEMH['T'], 1)
                return None
            sec('T', _mavs, None)
            CN['T'] += 4
            def _divb(v, qb=qb):
                w(v, 'T')
                for bank in range(2):
                    pp = psV_[bank]
                    dn = mkap(pp, 128, [[256, 2], [1, 1]])
                    v.reciprocal(gst.t_[:, 1536 + bank * 2:1538 + bank * 2], dn)
                    v.drain()
                    src = mkap(pp, 0, [[256, 2], [1, 128]])
                    rc = mkap(gst.t_, 1536 + bank * 2, [[1, 2], [0, 128]])
                    dst = mkap(aostg, bank * 256, [[128, 2], [1, 128]])
                    v.tensor_tensor(dst, src, rc, ALU.mult)
                    v.drain()
                return v.drain()
            V(_divb)
            mset('psVav', 'V')
            def _trb(te, qb=qb):
                w(te, 'V')
                last = None
                for qt in range(4):
                    last = te.transpose(psT_[:, qt * 128:(qt + 1) * 128], aostg[:, qt * 128:(qt + 1) * 128], ident[:])
                return last
            T(_trb)
            def _evb(v, qb=qb):
                w(v, 'T', 'dO')
                v.tensor_copy(mst[:, 0:512], psT_[:, 0:512])
                return v.drain()
            V(_evb)
            def _stb(sy, h=h, qb=qb):
                w(sy, 'V')
                dma(sy, aoas[h * 128:(h + 1) * 128, qb * 512:(qb + 1) * 512], mst[:, 0:512], 'dO')
                return None
            Y(_stb)

    stB.close()
    stAB.close()

    if stop_after == 'b':
        def _fin(sy):
            w(sy, 'S', 'V', 'T')
            sy.wait_ge(SEMH['dO'], CN['dO'])
            d = dbg('aofull', [D, S_], BF16)
            if d is not None:
                dma(sy, d, aoas[:, :], 'dO')
            sy.wait_ge(s_dO, CN['dO'])
            return None
        Y(_fin)
        ctx.close()
        return nc, dbg_list

    stC = ctx.enter_context(contextlib.ExitStack())
    Xc = Mat(sbt('Xc', [128, NT * 512], F32, stC), NT, 512)
    Xc2 = Mat(sbt('Xc2', [128, NT * 512], F32, stC), NT, 512)
    ybfC = Mat(sbt('ybfC', [128, NT * 512], BF16, stC), NT, 512)
    aoAc = Mat(sbt('aoAc', [128, NT * 512], BF16, stC), NT, 512)
    aoAc2 = Mat(sbt('aoAc2', [128, NT * 512], BF16, stC), NT, 512)
    ws0C = Mat(sbt('ws0C', [128, NT * 512], BF16, stC), NT, 512)
    ws1C = Mat(sbt('ws1C', [128, NT * 512], BF16, stC), NT, 512)
    WSC = [ws0C, ws1C]
    XCB = [Xc, Xc2]
    AOB = [aoAc, aoAc2]
    dw_xc = [0, 0]

    def emit_ldxc(b):
        def _ldxc(sy, b=b):
            if b == 0:
                w(sy, 'V', 'T', 'S', 'dG', 'dO')
            else:
                mwait(sy, 'xcst%d' % (b % 2), 'aofree%d' % (b % 2))
            src3 = x2s[:, b * BLK:(b + 1) * BLK].rearrange('(n p) c -> p n c', p=128)
            dst3 = XCB[b % 2].t_[:, :].rearrange('p (n c) -> p n c', c=512)
            dma(sy, dst3, src3, 'dW')
            src4 = aoas[:, b * BLK:(b + 1) * BLK].rearrange('(n p) c -> p n c', p=128)
            dst4 = AOB[b % 2].t_[:, :].rearrange('p (n c) -> p n c', c=512)
            dma(sy, dst4, src4, 'dW')
            return None
        Y(_ldxc)
        dw_xc[b % 2] = CN['dW']

    mset('ppq0', 'S')
    mset('ppq1', 'S')
    BKOC = [psP[0], psP[1], psS_[0], psS_[1]]
    emit_ldxc(0)
    for b in range(NB):
        Xcur = XCB[b % 2]
        aoc = AOB[b % 2]
        dwm_xc = dw_xc[b % 2]
        for s_ in range(4):
            def _ldwoc(gp, s_=s_, first=(b == 0 and s_ < 2)):
                if first:
                    w(gp, 'T', 'V', 'S', 'dO')
                mwait(gp, 'wsb%d' % (s_ % 2))
                stream_weight(gp, wobT, WSC[s_ % 2], NT, 512, s_ * 512)
                return None
            G(_ldwoc)
            for dt_ in range(4):
                dmt = s_ * 4 + dt_
                def _mmoc(te, s_=s_, dt_=dt_, dmt=dmt, aoc=aoc, dwm_xc=dwm_xc, b=b):
                    w(te, 'dG')
                    te.wait_ge(SEMH['dW'], dwm_xc)
                    if dmt < 4:
                        if b > 0:
                            mwait(te, 'ppqf%d' % ([2, 3, 0, 1][dmt]))
                        else:
                            mwait(te, 'ppq%d' % (dmt % 2))
                    else:
                        mwait(te, 'ppqo%d' % (dmt % 4))
                    pp = BKOC[dmt % 4]
                    last = None
                    for j in range(NT):
                        last = te.matmul(pp[:, :], WSC[s_ % 2].tc(j, dt_ * 128, (dt_ + 1) * 128),
                                         aoc.t(j), start=(j == 0), stop=(j == NT - 1))
                    return last
                T(_mmoc)
                def _resc(v, dmt=dmt, Xcur=Xcur):
                    w(v, 'T')
                    v.scalar_tensor_tensor(Xcur.t(dmt), BKOC[dmt % 4][:, :], cbo[:, dmt:dmt + 1], Xcur.t(dmt), ALU.add, ALU.add)
                    v.drain()
                    v.tensor_copy(ybfC.t(dmt), Xcur.t(dmt))
                    if dmt == 0:
                        v.tensor_copy(gst.t_[:, 0:512], Xcur.t(0))
                        v.tensor_copy(gst.t_[:, 1024:1536], Xcur.t(0))
                    else:
                        v.tensor_max(gst.t_[:, 0:512], gst.t_[:, 0:512], Xcur.t(dmt))
                        v.tensor_tensor(gst.t_[:, 1024:1536], gst.t_[:, 1024:1536], Xcur.t(dmt), ALU.min)
                    return v.drain()
                V(_resc)
                mset('ppqo%d' % (dmt % 4), 'V')
            mset('wsb%d' % (s_ % 2), 'T')
        mset('aofree%d' % (b % 2), 'T')
        if b + 1 < NB:
            emit_ldxc(b + 1)
        emit_ffn(Xcur, ybfC, WSC, first_block=False, prefused=True)
        def _stout(sy, b=b, Xcur=Xcur):
            w(sy, 'V', 'T')
            src3 = Xcur.t_[:, :].rearrange('p (n c) -> p n c', c=512)
            dst3 = xoT[:, b * BLK:(b + 1) * BLK].rearrange('(n p) c -> p n c', p=128)
            dma(sy, dst3, src3, 'dO')
            return None
        Y(_stout)
        mset('xcst%d' % (b % 2), 'dO')

    def _final(sy):
        w(sy, 'S', 'V', 'T')
        sy.wait_ge(s_dO, CN['dO'])
        return None
    Y(_final)
    ctx.close()
    return nc, dbg_list


import ml_dtypes


def _host_prep(inputs, core):
    d = {}
    d['xT'] = np.ascontiguousarray(np.asarray(inputs['L0_attn_a_x'])[core].T)
    d['wqT'] = np.ascontiguousarray(np.asarray(inputs['L0_attn_a_Wq']).T)
    d['wkT'] = np.ascontiguousarray(np.asarray(inputs['L0_attn_a_Wk']).T)
    d['wvT'] = np.ascontiguousarray(np.asarray(inputs['L0_attn_a_Wv']).T)
    d['woaT'] = np.ascontiguousarray(np.asarray(inputs['L0_attn_a_Wo']).T)
    d['wqdT'] = np.ascontiguousarray(np.asarray(inputs['L0_attn_b_Wq_down']).T)
    d['wkvdT'] = np.ascontiguousarray(np.asarray(inputs['L0_attn_b_Wkv_down']).T)
    d['wquT'] = np.ascontiguousarray(np.asarray(inputs['L0_attn_b_Wq_up']).T)
    d['wknT'] = np.ascontiguousarray(np.asarray(inputs['L0_attn_b_Wk_nope']).T)
    d['wkrT'] = np.ascontiguousarray(np.asarray(inputs['L0_attn_b_Wk_rope']).T)
    d['wvuT'] = np.ascontiguousarray(np.asarray(inputs['L0_attn_b_Wv_up']).T)
    d['wobT'] = np.ascontiguousarray(np.asarray(inputs['L0_attn_b_Wo']).T)
    d['wf8T'] = np.ascontiguousarray(np.asarray(inputs['L0_ffn_weight']).T)

    def cols(vec, n):
        return np.ascontiguousarray(np.asarray(vec, np.float32).reshape(n, 128).T)
    d['c_bqd'] = cols(inputs['L0_attn_b_bq_down'], 4)
    d['c_bkvd'] = cols(inputs['L0_attn_b_bkv_down'], 4)
    d['c_bqu'] = cols(inputs['L0_attn_b_bq_up'], 16)
    d['c_bkn'] = cols(inputs['L0_attn_b_bk_nope'], 8)
    d['c_bkr'] = cols(inputs['L0_attn_b_bk_rope'], 8)
    d['c_bo'] = cols(inputs['L0_attn_b_bo'], 16)
    d['c_fb'] = cols(inputs['L0_ffn_bias'], 16)
    d['c_ws'] = cols(np.asarray(inputs['L0_ffn_w_scale'])[:, 0], 16)
    d['r_bvu'] = np.asarray(inputs['L0_attn_b_bv_up'], np.float32).reshape(1, -1)

    bf = ml_dtypes.bfloat16
    d['identin'] = np.eye(128, dtype=bf)
    kk = np.arange(128)
    d['maskin'] = (kk[:, None] <= kk[None, :]).astype(bf)
    d['onesttin'] = np.ones((128, 32), dtype=bf)
    o32 = np.zeros((32, 128), np.float32); o32[0, :] = 1.0
    d['os32fin'] = o32
    d['os32bin'] = o32.astype(bf)
    d['onecolbin'] = np.ones((128, 1), dtype=bf)
    d['iotacolin'] = np.arange(128, dtype=np.float32).reshape(128, 1)
    ii = np.arange(128)
    d['eArowin'] = (-2.0 * (ii % 64) / 128.0).astype(np.float32).reshape(1, 128)
    jj = np.arange(512)
    d['eBcolin'] = np.ascontiguousarray((-2.0 * jj / 1024.0).astype(np.float32).reshape(4, 128).T)
    d['signAin'] = np.where(ii % 128 < 64, -1.0, 1.0).astype(np.float32).reshape(1, 128)
    d['identfin'] = np.eye(128, dtype=np.float32)
    io = np.zeros((32, 2048), np.float32); io[0, :] = np.arange(2048)
    d['iota32in'] = io
    t64 = np.arange(2048, dtype=np.float64)
    dA = np.arange(128)
    invA = np.float64(np.float32(10000.0) ** np.float32(-2.0 * (dA % 64) / 128.0))
    angA = np.float64(np.float32(t64[:, None] * invA[None, :]))
    cosA = np.cos(angA)
    sinA = np.sin(angA) * np.where(dA < 64, -1.0, 1.0)[None, :]
    d['tabAcin'] = cosA.reshape(16, 128, 128).transpose(1, 0, 2).reshape(128, 2048).astype(ml_dtypes.bfloat16)
    d['tabAsin'] = sinA.reshape(16, 128, 128).transpose(1, 0, 2).reshape(128, 2048).astype(ml_dtypes.bfloat16)
    jB = np.arange(512)
    invB = np.float64(np.float32(10000.0) ** np.float32(-2.0 * jB / 1024.0))
    angB = np.float64(np.float32(t64[:, None] * invB[None, :]))
    cosB, sinB = np.cos(angB), np.sin(angB)
    d['tabBcin'] = cosB.reshape(4, 512, 4, 128).transpose(3, 0, 2, 1).reshape(128, 8192).astype(ml_dtypes.bfloat16)
    d['tabBsin'] = sinB.reshape(4, 512, 4, 128).transpose(3, 0, 2, 1).reshape(128, 8192).astype(ml_dtypes.bfloat16)
    return d


def kernel(**inputs):
    from concourse.bass_utils import run_bass_kernel_spmd
    nc, _ = build()
    in_maps = [_host_prep(inputs, core) for core in range(8)]
    res = run_bass_kernel_spmd(nc, in_maps, core_ids=list(range(8)))
    out = np.empty((8, 2048, 2048), dtype=np.float32)
    for core in range(8):
        out[core] = np.asarray(res.results[core]['xoT']).T
    return out


# version 31 (final)
# speedup vs baseline: 1.0216x; 1.0050x over previous
import numpy as np
import concourse.bass as bass
import concourse.mybir as mybir

F32 = mybir.dt.float32
F16 = mybir.dt.float16
BF16 = mybir.dt.bfloat16
I32 = mybir.dt.int32
I8 = mybir.dt.int8
AF = mybir.ActivationFunctionType
ALU = mybir.AluOpType

D = 2048
S = 2048
NB = 4
BLK = 512
NT = 16
NH = 16
KVH = 4
HD = 128
LAT = 512
EPS_LN = 1e-5
EPS_A = 1e-6
SMAX = float(1.0 / np.sqrt(np.float32(128.0)))
PI_HI = 6.28125
PI_LO = float(2.0 * np.pi - 6.28125)
INV2PI = float(1.0 / (2.0 * np.pi))
HALFPI = float(np.pi / 2.0)
LN1E4 = float(np.log(10000.0))


class Mat:
    def __init__(self, t, nt, C):
        self.t_, self.nt, self.C = t, nt, C

    def t(self, j):
        return self.t_[:, j * self.C:(j + 1) * self.C]

    def tc(self, j, c0, c1):
        return self.t_[:, j * self.C + c0: j * self.C + c1]


def build(debug=(), stop_after=None):
    import contextlib
    nc = bass.Bass('TRN2', target_bir_lowering=False, debug=False)

    xT = nc.dram_tensor('xT', [D, S], F32, kind='ExternalInput').ap()
    wqT = nc.dram_tensor('wqT', [D, D], F32, kind='ExternalInput').ap()
    wkT = nc.dram_tensor('wkT', [D, 512], F32, kind='ExternalInput').ap()
    wvT = nc.dram_tensor('wvT', [D, 512], F32, kind='ExternalInput').ap()
    woaT = nc.dram_tensor('woaT', [D, D], F32, kind='ExternalInput').ap()
    wqdT = nc.dram_tensor('wqdT', [D, LAT], F32, kind='ExternalInput').ap()
    wkvdT = nc.dram_tensor('wkvdT', [D, LAT], F32, kind='ExternalInput').ap()
    wquT = nc.dram_tensor('wquT', [LAT, D], F32, kind='ExternalInput').ap()
    wknT = nc.dram_tensor('wknT', [LAT, 1024], F32, kind='ExternalInput').ap()
    wkrT = nc.dram_tensor('wkrT', [LAT, 1024], F32, kind='ExternalInput').ap()
    wvuT = nc.dram_tensor('wvuT', [LAT, D], F32, kind='ExternalInput').ap()
    wobT = nc.dram_tensor('wobT', [D, D], F32, kind='ExternalInput').ap()
    wf8T = nc.dram_tensor('wf8T', [D, D], I8, kind='ExternalInput').ap()
    c_bqd = nc.dram_tensor('c_bqd', [128, 4], F32, kind='ExternalInput').ap()
    c_bkvd = nc.dram_tensor('c_bkvd', [128, 4], F32, kind='ExternalInput').ap()
    c_bqu = nc.dram_tensor('c_bqu', [128, 16], F32, kind='ExternalInput').ap()
    c_bkn = nc.dram_tensor('c_bkn', [128, 8], F32, kind='ExternalInput').ap()
    c_bkr = nc.dram_tensor('c_bkr', [128, 8], F32, kind='ExternalInput').ap()
    c_bo = nc.dram_tensor('c_bo', [128, 16], F32, kind='ExternalInput').ap()
    c_fb = nc.dram_tensor('c_fb', [128, 16], F32, kind='ExternalInput').ap()
    c_ws = nc.dram_tensor('c_ws', [128, 16], F32, kind='ExternalInput').ap()
    r_bvu = nc.dram_tensor('r_bvu', [1, D], F32, kind='ExternalInput').ap()
    identin = nc.dram_tensor('identin', [128, 128], BF16, kind='ExternalInput').ap()
    maskin = nc.dram_tensor('maskin', [128, 128], BF16, kind='ExternalInput').ap()
    onesttin = nc.dram_tensor('onesttin', [128, 32], BF16, kind='ExternalInput').ap()
    os32fin = nc.dram_tensor('os32fin', [32, 128], F32, kind='ExternalInput').ap()
    os32bin = nc.dram_tensor('os32bin', [32, 128], BF16, kind='ExternalInput').ap()
    onecolbin = nc.dram_tensor('onecolbin', [128, 1], BF16, kind='ExternalInput').ap()
    iotacolin = nc.dram_tensor('iotacolin', [128, 1], F32, kind='ExternalInput').ap()
    eArowin = nc.dram_tensor('eArowin', [1, 128], F32, kind='ExternalInput').ap()
    eBcolin = nc.dram_tensor('eBcolin', [128, 4], F32, kind='ExternalInput').ap()
    signAin = nc.dram_tensor('signAin', [1, 128], F32, kind='ExternalInput').ap()
    identfin = nc.dram_tensor('identfin', [128, 128], F32, kind='ExternalInput').ap()
    iota32in = nc.dram_tensor('iota32in', [32, S], F32, kind='ExternalInput').ap()
    tabAcin = nc.dram_tensor('tabAcin', [128, 2048], BF16, kind='ExternalInput').ap()
    tabAsin = nc.dram_tensor('tabAsin', [128, 2048], BF16, kind='ExternalInput').ap()
    tabBcin = nc.dram_tensor('tabBcin', [128, 8192], BF16, kind='ExternalInput').ap()
    tabBsin = nc.dram_tensor('tabBsin', [128, 8192], BF16, kind='ExternalInput').ap()

    xoT = nc.dram_tensor('xoT', [D, S], F32, kind='ExternalOutput').ap()
    x2s = nc.dram_tensor('x2s', [D, S], F32, kind='Internal').ap()
    kAs = nc.dram_tensor('kAs', [D, S], BF16, kind='Internal').ap()
    aoas = nc.dram_tensor('aoas', [D, S], BF16, kind='Internal').ap()

    dbg_list = []
    dbg_t = {}
    def dbg(name, shape, dtype=F32):
        if name in debug:
            dbg_t[name] = nc.dram_tensor('dbg_' + name, shape, dtype, kind='ExternalOutput').ap()
            dbg_list.append('dbg_' + name)
            return dbg_t[name]
        return None

    ctx = contextlib.ExitStack()

    def sbt(name, shape, dt, st=None):
        return (st or ctx).enter_context(nc.sbuf_tensor(name, shape, dt))

    def pst(name, shape, dt, st=None):
        return (st or ctx).enter_context(nc.psum_tensor(name, shape, dt))

    ident = sbt('ident', [128, 128], BF16)
    maskT = sbt('maskT', [128, 128], BF16)
    onestt = sbt('onestt', [128, 32], BF16)
    os32f = sbt('os32f', [32, 128], F32)
    os32b = sbt('os32b', [32, 128], BF16)
    onecolb = sbt('onecolb', [128, 1], BF16)
    iotacol = sbt('iotacol', [128, 1], F32)
    eArow = sbt('eArow', [1, 128], F32)
    eBcol = sbt('eBcol', [128, 4], F32)
    signA = sbt('signA', [1, 128], F32)
    identf = sbt('identf', [128, 128], F32)
    cbqd = sbt('cbqd', [128, 4], F32)
    cbkvd = sbt('cbkvd', [128, 4], F32)
    cbqu = sbt('cbqu', [128, 16], F32)
    cbkn = sbt('cbkn', [128, 8], F32)
    cbkr = sbt('cbkr', [128, 8], F32)
    cbo = sbt('cbo', [128, 16], F32)
    cfb = sbt('cfb', [128, 16], F32)
    cws = sbt('cws', [128, 16], F32)
    bvu32 = sbt('bvu32', [32, D], BF16)
    cpool = sbt('cpool', [128, 8], F32)
    CP_ROUND, CP_QUANT, CP_EPSLN, CP_NHALF, CP_SMAX, CP_EPSA, CP_MU = 0, 1, 2, 3, 4, 5, 6
    c1col = sbt('c1col', [128, 16], F32)
    itmp = sbt('itmp', [128, 512], I32)
    itmp2 = sbt('itmp2', [128, 512], I32)
    rows32 = sbt('rows32', [32, 1536], F32)
    rowt = sbt('rowt', [1, 1536], F32)
    RW_M, RW_RSTD, RW_T1 = 0, 512, 1024
    inv32 = sbt('inv32', [32, 128], F32)
    invfgnb = sbt('invfgnb', [128, 128], F32)
    invfb = sbt('invfb', [128, 128], F32)
    invBcol = sbt('invBcol', [128, 4], F32)
    gst = Mat(sbt('gst', [128, 2048], F32), 2, 1024)
    mstP = sbt('mstP', [128, 2 * 512], BF16)
    mstP2 = sbt('mstP2', [128, 2 * 512], BF16)
    f16stP = sbt('f16stP', [128, 512], F16)

    s_dW = ctx.enter_context(nc.semaphore('s_dW'))
    s_dG = ctx.enter_context(nc.semaphore('s_dG'))
    s_dO = ctx.enter_context(nc.semaphore('s_dO'))
    s_T = ctx.enter_context(nc.semaphore('s_T'))
    s_V = ctx.enter_context(nc.semaphore('s_V'))
    s_S = ctx.enter_context(nc.semaphore('s_S'))

    blk = ctx.enter_context(nc.Block())

    CN = {'T': 0, 'V': 0, 'S': 0, 'dW': 0, 'dG': 0, 'dO': 0}
    SEM = {'T': s_T, 'V': s_V, 'S': s_S, 'dW': s_dW, 'dG': s_dG, 'dO': s_dO}

    def sec(engine_name, f, inc=None):
        dec = {'T': blk.tensor, 'V': blk.vector, 'S': blk.scalar,
               'G': blk.gpsimd, 'Y': blk.sync}[engine_name]
        def wrapper(eng):
            last = f(eng)
            if inc is not None:
                assert last is not None, f'section for {engine_name} returned no instr'
                CN[inc] += 1
                last.then_inc(SEM[inc], 1)
        dec(wrapper)

    def T(f):
        sec('T', f, 'T')
    def V(f):
        sec('V', f, 'V')
    def Sc(f):
        sec('S', f, 'S')
    def G(f):
        sec('G', f, None)
    def Y(f):
        sec('Y', f, None)

    def dma(eng, out, in_, sem_key):
        eng.dma_start(out=out, in_=in_).then_inc(SEM[sem_key], 16)
        CN[sem_key] += 16

    def _loads(sy):
        for dst, src in [(ident[:], identin), (maskT[:], maskin), (onestt[:], onesttin),
                         (os32f[:], os32fin), (os32b[:], os32bin), (onecolb[:], onecolbin),
                         (iotacol[:], iotacolin), (eArow[:], eArowin),
                         (eBcol[:], eBcolin), (signA[:], signAin),
                         (cbqd[:], c_bqd), (cbkvd[:], c_bkvd), (cbqu[:], c_bqu),
                         (cbkn[:], c_bkn), (cbkr[:], c_bkr), (cbo[:], c_bo),
                         (cfb[:], c_fb), (cws[:], c_ws), (rowt[0:1, 0:D] if D <= 1536 else None, r_bvu)]:
            pass
        return None

    def _loads2(sy):
        for dst, src in [(ident[:], identin), (maskT[:], maskin), (onestt[:], onesttin),
                         (os32f[:], os32fin), (os32b[:], os32bin), (onecolb[:], onecolbin),
                         (iotacol[:], iotacolin), (eArow[:], eArowin),
                         (eBcol[:], eBcolin), (signA[:], signAin),
                         (cbqd[:], c_bqd), (cbkvd[:], c_bkvd), (cbqu[:], c_bqu),
                         (cbkn[:], c_bkn), (cbkr[:], c_bkr), (cbo[:], c_bo),
                         (cfb[:], c_fb), (cws[:], c_ws), (identf[:], identfin),
                         (gst.t_[0:1, 0:D], r_bvu)]:
            dma(sy, dst, src, 'dW')
        return None
    Y(_loads2)
    DW0 = CN['dW']

    def _setup(v):
        v.wait_ge(s_dW, DW0)
        v.memset(bvu32[:], 0.0)
        v.memset(rows32[:], 0.0)
        v.memset(inv32[:], 0.0)
        v.memset(cpool[:], 0.0)
        v.drain()
        v.memset(cpool[:, CP_EPSLN:CP_EPSLN + 1], EPS_LN)
        v.memset(cpool[:, CP_NHALF:CP_NHALF + 1], -0.5)
        v.memset(cpool[:, CP_SMAX:CP_SMAX + 1], SMAX)
        v.memset(cpool[:, CP_EPSA:CP_EPSA + 1], EPS_A)
        v.memset(cpool[:, CP_MU:CP_MU + 1], 0.7)
        v.tensor_copy(bvu32[0:1, 0:D], gst.t_[0:1, 0:D])
        v.drain()
        v.tensor_copy(itmp[:, 0:1], cpool[:, CP_MU:CP_MU + 1])
        v.drain()
        v.tensor_copy(cpool[:, CP_MU:CP_MU + 1], itmp[:, 0:1])
        v.drain()
        v.tensor_scalar(cpool[:, CP_ROUND:CP_ROUND + 1], cpool[:, CP_MU:CP_MU + 1], -0.5, 0.5, ALU.mult, ALU.add)
        v.tensor_scalar(cpool[:, CP_QUANT:CP_QUANT + 1], cpool[:, CP_MU:CP_MU + 1], -0.5, 128.5, ALU.mult, ALU.add)
        return v.drain()
    V(_setup)
    V_SETUP = CN['V']

    def _invf(sc):
        sc.wait_ge(s_V, V_SETUP)
        sc.activation(rowt[0:1, RW_T1:RW_T1 + 128], eArow[:], AF.Exp, scale=LN1E4)
        sc.activation(invBcol[:], eBcol[:], AF.Exp, scale=LN1E4)
        return sc.drain()
    Sc(_invf)
    sN = CN['S']

    def _inv1(v, sN=sN):
        v.wait_ge(s_S, sN)
        v.tensor_copy(inv32[0:1, :], rowt[0:1, RW_T1:RW_T1 + 128])
        return v.drain()
    V(_inv1)

    st_setup = ctx.enter_context(contextlib.ExitStack())
    psX = pst('psX', [128, 512], F32, st_setup)

    vN = CN['V']
    def _invb1(te, vN=vN):
        te.wait_ge(s_V, vN)
        return te.matmul(psX[:, 0:128], os32f[:], inv32[:], start=True, stop=True)
    T(_invb1)

    tN = CN['T']
    def _inv2(v, tN=tN):
        v.wait_ge(s_T, tN)
        v.tensor_copy(invfb[:], psX[:, 0:128])
        v.tensor_mul(inv32[0:1, :], rowt[0:1, RW_T1:RW_T1 + 128], signA[:])
        return v.drain()
    V(_inv2)

    vN = CN['V']
    def _invb2(te, vN=vN):
        te.wait_ge(s_V, vN)
        return te.matmul(psX[:, 0:128], os32f[:], inv32[:], start=True, stop=True)
    T(_invb2)

    tN = CN['T']
    def _inv3(v, tN=tN):
        v.wait_ge(s_T, tN)
        v.tensor_copy(invfgnb[:], psX[:, 0:128])
        return v.drain()
    V(_inv3)
    V_INV = CN['V']

    def tab_red(v, ang, b):
        p = 1 if b >= 1024 else 0
        u = gst.t_[:, b + 256:b + 384]
        r1 = gst.t_[:, b + 384:b + 512]
        ic = itmp[:, p * 128:(p + 1) * 128]
        v.tensor_scalar(u, ang, INV2PI, cpool[:, CP_ROUND:CP_ROUND + 1], ALU.mult, ALU.add)
        v.drain()
        v.tensor_copy(ic, u)
        v.drain()
        v.tensor_copy(u, ic)
        v.drain()
        v.scalar_tensor_tensor(r1, u, -PI_HI, ang, ALU.mult, ALU.add)
        v.drain()
        v.scalar_tensor_tensor(ang, u, -PI_LO, r1, ALU.mult, ALU.add)
        v.drain()

    return_env = dict(locals())
    return _stages(return_env, debug=debug, stop_after=stop_after)


def _stages(env, debug=(), stop_after=None):
    import contextlib
    g = env
    nc = g['nc']
    T, V, Sc, G, Y = g['T'], g['V'], g['Sc'], g['G'], g['Y']
    sec = g['sec']
    CN, SEM, dma = g['CN'], g['SEM'], g['dma']
    s_dW, s_dG, s_dO = g['s_dW'], g['s_dG'], g['s_dO']
    s_T, s_V, s_S = g['s_T'], g['s_V'], g['s_S']
    sbt, pst = g['sbt'], g['pst']
    gst, itmp, cpool, rowt, rows32 = g['gst'], g['itmp'], g['cpool'], g['rowt'], g['rows32']
    itmp2 = g['itmp2']
    RW_M, RW_RSTD, RW_T1 = g['RW_M'], g['RW_RSTD'], g['RW_T1']
    invfb, invfgnb, invBcol = g['invfb'], g['invfgnb'], g['invBcol']
    identf = g['identf']
    iotacol, ident, maskT, onestt, os32f, os32b, onecolb = (
        g['iotacol'], g['ident'], g['maskT'], g['onestt'], g['os32f'], g['os32b'], g['onecolb'])
    cbqd, cbkvd, cbqu, cbkn, cbkr, cbo, cfb, cws, bvu32, c1col = (
        g['cbqd'], g['cbkvd'], g['cbqu'], g['cbkn'], g['cbkr'], g['cbo'], g['cfb'], g['cws'],
        g['bvu32'], g['c1col'])
    mst = g['mstP']
    mst2 = g['mstP2']
    f16st = g['f16stP']
    tab_red = g['tab_red']
    dbg = g['dbg']
    dbg_list = g['dbg_list']
    ctx = g['ctx']
    CPR, CPQ, CPE, CPH, CPS, CPA, CPM = (g['CP_ROUND'], g['CP_QUANT'], g['CP_EPSLN'],
                                          g['CP_NHALF'], g['CP_SMAX'], g['CP_EPSA'], g['CP_MU'])
    xT, xoT, x2s, kAs, aoas = g['xT'], g['xoT'], g['x2s'], g['kAs'], g['aoas']
    wqT, wkT, wvT, woaT = g['wqT'], g['wkT'], g['wvT'], g['woaT']
    wqdT, wkvdT, wquT, wknT, wkrT, wvuT, wobT, wf8T = (
        g['wqdT'], g['wkvdT'], g['wquT'], g['wknT'], g['wkrT'], g['wvuT'], g['wobT'], g['wf8T'])
    iota32in = g['iota32in']
    tabAcin, tabAsin = g['tabAcin'], g['tabAsin']
    tabBcin, tabBsin = g['tabBcin'], g['tabBsin']
    psX = g['psX']
    st_setup = g['st_setup']
    S_ = S
    V_INV = g['V_INV']

    SEMH = {'T': s_T, 'V': s_V, 'S': s_S, 'dW': s_dW, 'dG': s_dG, 'dO': s_dO}


    def mkap(h, extra_off, free_dims):
        a = h[:, :] if not isinstance(h, bass.AP) else h
        return bass.AP(a.tensor, a.offset + extra_off, [list(a.ap[0])] + [list(d) for d in free_dims])

    def w(eng, *deps):
        for d in deps:
            if CN[d] > 0:
                eng.wait_ge(SEMH[d], CN[d])

    MK = {}
    def mset(name, key):
        MK[name] = (key, CN[key])
    def mwait(eng, *names):
        for nm in names:
            if nm in MK:
                k, c = MK[nm]
                if c > 0:
                    eng.wait_ge(SEMH[k], c)

    wsA2t = sbt('wsA2', [128, NT * 512], BF16)

    stA_scope = ctx.enter_context(contextlib.ExitStack())
    tabAc = sbt('tabAc', [128, 16 * 128], BF16, stA_scope)
    tabAs = sbt('tabAs', [128, 16 * 128], BF16, stA_scope)
    Xa = Mat(sbt('Xa', [128, NT * 512], F32, stA_scope), NT, 512)
    ybf = Mat(sbt('ybf', [128, NT * 512], BF16, stA_scope), NT, 512)
    qfB = Mat(sbt('qfB', [128, 4 * 2048], BF16, stA_scope), 4, 2048)
    qfA = Mat(sbt('qfA', [128, NT * 512], BF16, stA_scope), NT, 512)
    kfB = Mat(sbt('kfB', [128, 4 * 512], BF16, stA_scope), 4, 512)
    kfA = Mat(sbt('kfA', [128, 4 * 512], BF16, stA_scope), 4, 512)
    vaug = Mat(sbt('vaug', [128, 4 * 516], BF16, stA_scope), 4, 516)
    stLa = sbt('stLa', [128, KVH * 129], F32, stA_scope)
    stbf = sbt('stbf', [128, KVH * 129], BF16, stA_scope)
    ws0 = Mat(sbt('ws0A', [128, NT * 512], BF16, stA_scope), NT, 512)
    ws1 = Mat(sbt('ws1A', [128, NT * 512], BF16, stA_scope), NT, 512)
    WS = [ws0, ws1]

    def _tabinit(v):
        w(v, 'V')
        v.memset(stLa[:], 0.0)
        v.memset(stbf[:], 0.0)
        v.memset(vaug.t_[:, :], 0.0)
        v.drain()
        for tt in range(4):
            for h in range(KVH):
                c0 = tt * 516 + h * 129 + 128
                v.memset(vaug.t_[:, c0:c0 + 1], 1.0)
        return v.drain()
    V(_tabinit)

    def _ldtabA(sy):
        dma(sy, tabAc[:, :], tabAcin, 'dW')
        dma(sy, tabAs[:, :], tabAsin, 'dW')
        return None
    Y(_ldtabA)

    if stop_after == 'setup':
        def _fin(sy):
            w(sy, 'S', 'V', 'T')
            for nm, src, dt in (('tabAc', tabAc[:, :], BF16), ('tabAs', tabAs[:, :], BF16), ('cpool', cpool[:], F32)):
                d = dbg(nm, [128, src.shape[1]], dt)
                if d is not None:
                    dma(sy, d, src, 'dO')
            sy.wait_ge(s_dO, CN['dO'])
            return None
        Y(_fin)
        ctx.close()
        return nc, dbg_list

    st_setup.close()
    psP = [pst('psP0', [128, 512], F32), pst('psP1', [128, 512], F32)]
    psS_ = [pst('psS0', [128, 512], F32), pst('psS1', [128, 512], F32)]
    psV_ = [pst('psV0', [128, 512], F32), pst('psV1', [128, 512], F32)]
    psU = pst('psU', [128, 512], F32)
    psT_ = pst('psTt', [128, 1024], BF16)

    def ln_stats(Xsrc, ybf_dst, precopied=False, strip_dw=None):
        if strip_dw is not None:
            for st4 in range(4):
                def _cps(v, st4=st4, dwv=strip_dw[st4]):
                    v.wait_ge(s_dW, dwv)
                    w(v, 'V', 'T')
                    for j in range(st4 * 4, st4 * 4 + 4):
                        v.tensor_copy(ybf_dst.t(j), Xsrc.t(j))
                    return v.drain()
                V(_cps)
                def _mm_sum_s(te, st4=st4):
                    w(te, 'V')
                    last = None
                    for j in range(st4 * 4, st4 * 4 + 4):
                        last = te.matmul(psP[0][0:32, :], onestt[:], ybf_dst.t(j), start=(j == 0), stop=(j == NT - 1))
                    return last
                T(_mm_sum_s)
                for j2 in (st4 * 4, st4 * 4 + 2):
                    def _sq(v, j2=j2):
                        w(v, 'V', 'T')
                        v.tensor_mul(mst[:, 0:512], ybf_dst.t(j2), ybf_dst.t(j2))
                        v.tensor_mul(mst[:, 512:1024], ybf_dst.t(j2 + 1), ybf_dst.t(j2 + 1))
                        return v.drain()
                    V(_sq)
                    def _mm_sq(te, j2=j2):
                        w(te, 'V')
                        te.matmul(psP[1][0:32, :], onestt[:], mst[:, 0:512], start=(j2 == 0), stop=False)
                        return te.matmul(psP[1][0:32, :], onestt[:], mst[:, 512:1024], start=False, stop=(j2 == NT - 2))
                    T(_mm_sq)
        else:
            if not precopied:
                def _cp(v):
                    w(v, 'V', 'T')
                    for j in range(NT):
                        v.tensor_copy(ybf_dst.t(j), Xsrc.t(j))
                    return v.drain()
                V(_cp)
            def _mm_sum(te):
                w(te, 'V')
                last = None
                for j in range(NT):
                    last = te.matmul(psP[0][0:32, :], onestt[:], ybf_dst.t(j), start=(j == 0), stop=(j == NT - 1))
                return last
            T(_mm_sum)
            for j2 in range(0, NT, 2):
                def _sq(v, j2=j2):
                    w(v, 'V', 'T')
                    v.tensor_mul(mst[:, 0:512], ybf_dst.t(j2), ybf_dst.t(j2))
                    v.tensor_mul(mst[:, 512:1024], ybf_dst.t(j2 + 1), ybf_dst.t(j2 + 1))
                    return v.drain()
                V(_sq)
                def _mm_sq(te, j2=j2):
                    w(te, 'V')
                    te.matmul(psP[1][0:32, :], onestt[:], mst[:, 0:512], start=(j2 == 0), stop=False)
                    return te.matmul(psP[1][0:32, :], onestt[:], mst[:, 512:1024], start=False, stop=(j2 == NT - 2))
                T(_mm_sq)
        def _rowm(v):
            w(v, 'T')
            v.tensor_scalar(rowt[0:1, RW_M:RW_M + 512], psP[0][0:1, :], 1.0 / 2048.0, None, ALU.mult)
            v.tensor_scalar(rowt[0:1, RW_T1:RW_T1 + 512], psP[1][0:1, :], 1.0 / 2048.0, None, ALU.mult)
            v.drain()
            v.tensor_mul(rowt[0:1, RW_RSTD:RW_RSTD + 512], rowt[0:1, RW_M:RW_M + 512], rowt[0:1, RW_M:RW_M + 512])
            v.drain()
            v.tensor_sub(rowt[0:1, RW_T1:RW_T1 + 512], rowt[0:1, RW_T1:RW_T1 + 512], rowt[0:1, RW_RSTD:RW_RSTD + 512])
            return v.drain()
        V(_rowm)
        def _rstd(sc):
            w(sc, 'V')
            sc.activation(rowt[0:1, RW_RSTD:RW_RSTD + 512], rowt[0:1, RW_T1:RW_T1 + 512], AF.Ln, bias=cpool[0:1, CPE:CPE + 1])
            sc.drain()
            sc.activation(rowt[0:1, RW_RSTD:RW_RSTD + 512], rowt[0:1, RW_RSTD:RW_RSTD + 512], AF.Exp, scale=cpool[0:1, CPH:CPH + 1])
            return sc.drain()
        Sc(_rstd)

    def ln_apply(Xsrc, out_mat, bb=None):
        if bb is None:
            bb = (psV_[0], psV_[1])
        def _rows(v):
            w(v, 'S', 'T')
            v.scalar_tensor_tensor(rowt[0:1, RW_T1:RW_T1 + 512], rowt[0:1, RW_M:RW_M + 512], -1.0,
                                   rowt[0:1, RW_RSTD:RW_RSTD + 512], ALU.mult, ALU.mult)
            v.drain()
            v.tensor_copy(rows32[0:1, 0:512], rowt[0:1, RW_RSTD:RW_RSTD + 512])
            v.tensor_copy(rows32[0:1, 512:1024], rowt[0:1, RW_T1:RW_T1 + 512])
            return v.drain()
        V(_rows)
        def _b12(te):
            w(te, 'V')
            te.matmul(bb[0][:, :], os32f[:], rows32[:, 0:512], start=True, stop=True)
            return te.matmul(bb[1][:, :], os32f[:], rows32[:, 512:1024], start=True, stop=True)
        T(_b12)
        marks = []
        for j2 in range(0, NT, 2):
            def _apply(v, j2=j2):
                w(v, 'T')
                for j in (j2, j2 + 1):
                    p = (j % 2) * 1024
                    v.tensor_mul(gst.t_[:, p:p + 512], Xsrc.t(j), bb[0][:, :])
                    v.drain()
                    v.tensor_add(out_mat.t(j), gst.t_[:, p:p + 512], bb[1][:, :])
                    v.drain()
                return v.drain()
            V(_apply)
            marks.append(CN['V'])
        return marks

    def stream_weight(sy, src_ap, dst_mat, n_dtiles, cols, col0, sem_key='dG'):
        src = src_ap[0:n_dtiles * 128, col0:col0 + cols]
        src3 = src.rearrange('(n p) c -> p n c', p=128)
        dst3 = dst_mat.t_[:, 0:n_dtiles * cols].rearrange('p (n c) -> p n c', c=cols)
        dma(sy, dst3, src3, sem_key)

    def stream_weight_rt(sy, src_ap, dst_mat, sd0, n, dd0, cols, col0, sem_key='dG'):
        src = src_ap[sd0 * 128:(sd0 + n) * 128, col0:col0 + cols]
        src3 = src.rearrange('(n p) c -> p n c', p=128)
        dst3 = dst_mat.t_[:, dd0 * cols:(dd0 + n) * cols].rearrange('p (n c) -> p n c', c=cols)
        dma(sy, dst3, src3, sem_key)

    def rope_elu(v, pp, dst, b, tt, kind):
        cosb = tabAc[:, (b * 4 + tt) * 128:(b * 4 + tt + 1) * 128]
        sinb = tabAs[:, (b * 4 + tt) * 128:(b * 4 + tt + 1) * 128]
        cos4 = mkap(cosb, 0, [[0, 4], [64, 2], [1, 64]])
        sin4 = mkap(sinb, 0, [[0, 4], [64, 2], [1, 64]])
        swp = mkap(pp, 64, [[128, 4], [-64, 2], [1, 64]])
        pp4 = mkap(pp, 0, [[128, 4], [64, 2], [1, 64]])
        m0 = mkap(mst, 0, [[128, 4], [64, 2], [1, 64]])
        m1 = mkap(mst, 512, [[128, 4], [64, 2], [1, 64]])
        v.tensor_tensor(m0, pp4, cos4, ALU.mult)
        v.tensor_tensor(m1, swp, sin4, ALU.mult)
        v.drain()
        v.tensor_add(mst[:, 0:512], mst[:, 0:512], mst[:, 512:1024])
        v.drain()
        v.tensor_scalar(mst[:, 512:1024], mst[:, 0:512], 0.0, None, ALU.min)
        v.drain()

    def rope_elu_fin(v, dst):
        v.scalar_tensor_tensor(dst, mst[:, 0:512], 0.0, mst[:, 512:1024], ALU.max, ALU.add)
        v.drain()

    def emit_ffn(Xcur, ybuf, ws, first_block, prefused=False):
        BKF = [psS_[0], psS_[1], psP[0], psP[1]]
        dqm = []
        ln_stats(Xcur, ybuf, precopied=prefused)
        if not prefused:
            def _tree(v):
                w(v, 'T', 'S', 'V')
                v.tensor_max(gst.t_[:, 0:512], Xcur.t(0), Xcur.t(1))
                v.tensor_tensor(gst.t_[:, 1024:1536], Xcur.t(0), Xcur.t(1), ALU.min)
                v.drain()
                for j in range(2, NT):
                    v.tensor_max(gst.t_[:, 0:512], gst.t_[:, 0:512], Xcur.t(j))
                    v.tensor_tensor(gst.t_[:, 1024:1536], gst.t_[:, 1024:1536], Xcur.t(j), ALU.min)
                    v.drain()
                return v.drain()
            V(_tree)
        def _trc(te):
            w(te, 'V')
            last = None
            for c4 in range(4):
                last = te.matmul(psS_[0][:, c4 * 128:(c4 + 1) * 128],
                                 gst.t_[:, c4 * 128:(c4 + 1) * 128], identf[:, :],
                                 start=True, stop=True)
                last = te.matmul(psS_[1][:, c4 * 128:(c4 + 1) * 128],
                                 gst.t_[:, c4 * 128 + 1024:(c4 + 1) * 128 + 1024], identf[:, :],
                                 start=True, stop=True)
            return last
        T(_trc)
        def _red(v):
            w(v, 'T')
            for c4 in range(4):
                v.reduce_max(gst.t_[:, 512 + c4:512 + c4 + 1], psS_[0][:, c4 * 128:(c4 + 1) * 128], mybir.AxisListType.X)
                v.tensor_reduce(gst.t_[:, 516 + c4:516 + c4 + 1], psS_[1][:, c4 * 128:(c4 + 1) * 128], mybir.AxisListType.X, ALU.min)
            return v.drain()
        V(_red)
        def _trr(te):
            w(te, 'V')
            last = None
            for c4 in range(4):
                last = te.matmul(psS_[0][0:1, c4 * 128:(c4 + 1) * 128], gst.t_[:, 512 + c4:513 + c4],
                                 identf[:, :], start=True, stop=True)
                last = te.matmul(psS_[1][0:1, c4 * 128:(c4 + 1) * 128], gst.t_[:, 516 + c4:517 + c4],
                                 identf[:, :], start=True, stop=True)
            return last
        T(_trr)
        def _rows_pre(v):
            w(v, 'T', 'S')
            v.tensor_copy(rows32[0:1, 1024:1536], rowt[0:1, RW_M:RW_M + 512])
            v.tensor_copy(rows32[0:1, 512:1024], rowt[0:1, RW_RSTD:RW_RSTD + 512])
            return v.drain()
        V(_rows_pre)
        def _trm(te):
            w(te, 'V')
            last = None
            for c4 in range(4):
                last = te.transpose(psV_[0][:, c4 * 32:(c4 + 1) * 32],
                                    rows32[0:32, 1024 + c4 * 128:1024 + (c4 + 1) * 128], identf[0:32, 0:32])
                last = te.transpose(psV_[1][:, c4 * 32:(c4 + 1) * 32],
                                    rows32[0:32, 512 + c4 * 128:512 + (c4 + 1) * 128], identf[0:32, 0:32])
            return last
        T(_trm)
        def _rows_col(v):
            w(v, 'T')
            mcol = mkap(psV_[0], 0, [[32, 4]])
            rcol = mkap(psV_[1], 0, [[32, 4]])
            tmax = gst.t_[:, 512:516]
            tmin = gst.t_[:, 516:520]
            a1 = gst.t_[:, 520:524]
            a2 = gst.t_[:, 524:528]
            a3 = gst.t_[:, 528:532]
            rqc = gst.t_[:, 532:536]
            inscc = gst.t_[:, 536:540]
            nmc = gst.t_[:, 540:544]
            v.tensor_tensor(a1, tmax, mcol, ALU.subtract)
            v.tensor_tensor(a2, mcol, tmin, ALU.subtract)
            v.drain()
            v.tensor_max(a1, a1, a2)
            v.drain()
            v.reciprocal(a2, a1)
            v.tensor_tensor(a3, a1, rcol, ALU.mult)
            v.drain()
            v.tensor_scalar(rqc, a2, 127.0, None, ALU.mult)
            v.tensor_scalar(inscc, a3, 1e-7, 1.0 / 127.0, ALU.max, ALU.mult)
            v.drain()
            v.tensor_tensor(nmc, rqc, mcol, ALU.mult)
            v.drain()
            v.tensor_scalar(nmc, nmc, -1.0, None, ALU.mult)
            v.drain()
            v.tensor_scalar(nmc, nmc, cpool[:, CPQ:CPQ + 1], None, ALU.add)
            return v.drain()
        V(_rows_col)
        def _trr2(te):
            w(te, 'V')
            last = None
            for c4 in range(4):
                last = te.transpose(psS_[0][0:1, c4 * 128:(c4 + 1) * 128], gst.t_[:, 532 + c4:533 + c4], identf[:, :])
                last = te.transpose(psS_[1][0:1, c4 * 128:(c4 + 1) * 128], gst.t_[:, 540 + c4:541 + c4], identf[:, :])
                last = te.transpose(psV_[0][0:1, c4 * 128:(c4 + 1) * 128], gst.t_[:, 536 + c4:537 + c4], identf[:, :])
            return last
        T(_trr2)
        def _rows_fin(v):
            w(v, 'T')
            v.tensor_copy(rows32[0:1, 0:512], psS_[0][0:1, 0:512])
            v.tensor_copy(rows32[0:1, 512:1024], psS_[1][0:1, 0:512])
            v.tensor_copy(rows32[0:1, 1024:1536], psV_[0][0:1, 0:512])
            return v.drain()
        V(_rows_fin)
        mset('ppqf1', 'V')
        def _bx(te):
            w(te, 'V')
            te.matmul(psP[0][:, :], os32f[:], rows32[:, 0:512], start=True, stop=True)
            te.matmul(psP[1][:, :], os32f[:], rows32[:, 512:1024], start=True, stop=True)
            return te.matmul(psS_[0][:, :], os32f[:], rows32[:, 1024:1536], start=True, stop=True)
        T(_bx)
        def _b3c(v):
            w(v, 'T')
            v.tensor_copy(gst.t_[:, 0:512], psS_[0][:, :])
            return v.drain()
        V(_b3c)
        mset('ppqf0', 'V')
        qbase_s = CN['S']
        vq_marks = []
        for j in range(NT):
            def _qz(v, j=j, qs=qbase_s):
                w(v, 'T')
                if j >= 2:
                    v.wait_ge(SEMH['S'], qs + j - 1)
                p = 1024
                v.tensor_mul(gst.t_[:, p:p + 512], Xcur.t(j), psP[0][:, :])
                v.drain()
                it = itmp if j % 2 == 0 else itmp2
                v.tensor_tensor(it[:, 0:512], gst.t_[:, p:p + 512], psP[1][:, :], ALU.add)
                return v.drain()
            V(_qz)
            vq_marks.append(CN['V'])
        def _qcasts(sc, vq_marks=tuple(vq_marks)):
            for j in range(NT):
                sc.wait_ge(SEMH['V'], vq_marks[j])
                it = itmp if j % 2 == 0 else itmp2
                sc.copy(ybuf.t(j), it[:, 0:512]).then_inc(SEMH['S'], 1)
            return None
        sec('S', _qcasts, None)
        CN['S'] += NT
        mset('q128', 'S')
        mset('ppqf2', 'V')
        mset('ppqf3', 'V')
        for s_ in range(4):
            def _ldf(gp, s_=s_):
                mwait(gp, 'wsb%d' % (s_ % 2))
                stream_weight(gp, wf8T, ws[s_ % 2], NT, 512, s_ * 512)
                return None
            G(_ldf)
            for ot in range(4):
                og = s_ * 4 + ot
                def _mmf(te, s_=s_, ot=ot, og=og, first_block=first_block, qs=qbase_s):
                    w(te, 'dG')
                    mwait(te, 'ppqf%d' % (og % 4))
                    pp = BKF[og % 4]
                    last = None
                    if s_ == 0:
                        for j in range(NT):
                            te.wait_ge(SEMH['S'], qs + j + 1)
                            last = te.matmul(pp[:, :], ws[s_ % 2].tc(j, ot * 128, (ot + 1) * 128),
                                             ybuf.t(j), start=(j == 0), stop=(j == NT - 1))
                    else:
                        mwait(te, 'q128')
                        for j in range(NT):
                            last = te.matmul(pp[:, :], ws[s_ % 2].tc(j, ot * 128, (ot + 1) * 128),
                                             ybuf.t(j), start=(j == 0), stop=(j == NT - 1))
                    if first_block:
                        for j in range(NT):
                            last = te.matmul(psU[:, og:og + 1], ws[s_ % 2].tc(j, ot * 128, (ot + 1) * 128),
                                             onecolb[:], start=(j == 0), stop=(j == NT - 1))
                    return last
                T(_mmf)
                if first_block:
                    def _c1(v, og=og):
                        w(v, 'T')
                        v.tensor_scalar(c1col[:, og:og + 1], psU[:, og:og + 1], -128.0, None, ALU.mult)
                        return v.drain()
                    V(_c1)
                    mset('psU', 'V')
                def _dq(v, og=og):
                    w(v, 'T')
                    pp = BKF[og % 4]
                    v.scalar_tensor_tensor(gst.t_[:, 1024:1536], pp[:, :], c1col[:, og:og + 1], gst.t_[:, 0:512], ALU.add, ALU.mult)
                    v.drain()
                    v.tensor_scalar(f16st[:], gst.t_[:, 1024:1536], cws[:, og:og + 1], cfb[:, og:og + 1], ALU.mult, ALU.add)
                    v.drain()
                    v.tensor_copy(gst.t_[:, 1024:1536], f16st[:])
                    v.drain()
                    v.tensor_add(Xcur.t(og), Xcur.t(og), gst.t_[:, 1024:1536])
                    return v.drain()
                V(_dq)
                dqm.append(CN['V'])
                mset('ppqf%d' % (og % 4), 'V')
                if ot == 3:
                    mset('wsb%d' % (s_ % 2), 'T')
        mset('ppq0', 'V')
        mset('ppq1', 'V')
        return [dqm[st4 * 4 + 3] for st4 in range(4)]


    def emit_stageA_head(b, xdw=None):
        if xdw is None:
            def _ldx(sy, b=b):
                src3 = xT[:, b * BLK:(b + 1) * BLK].rearrange('(n p) c -> p n c', p=128)
                dst3 = Xa.t_[:, :].rearrange('p (n c) -> p n c', c=512)
                dma(sy, dst3, src3, 'dW')
                return None
            Y(_ldx)

            def _wx(v):
                w(v, 'V', 'dW')
                return v.drain()
            V(_wx)

            ln_stats(Xa, ybf)
        else:
            ln_stats(Xa, ybf, strip_dw=xdw)
        aplA = ln_apply(Xa, ybf)
        mset('yA', 'V')

        def eluq(q):
            t = mst if q < 2 else mst2
            return t[:, (q % 2) * 512:(q % 2) * 512 + 512]

        def rope2(v, pp, dst, tt, q):
            cosb = tabAc[:, (b * 4 + tt) * 128:(b * 4 + tt + 1) * 128]
            sinb = tabAs[:, (b * 4 + tt) * 128:(b * 4 + tt + 1) * 128]
            nh4 = dst.shape[1] // 128
            cos4 = mkap(cosb, 0, [[0, nh4], [64, 2], [1, 64]])
            sin4 = mkap(sinb, 0, [[0, nh4], [64, 2], [1, 64]])
            swp = mkap(pp, 64, [[128, nh4], [-64, 2], [1, 64]])
            pp4 = mkap(pp, 0, [[128, nh4], [64, 2], [1, 64]])
            d4 = mkap(dst, 0, [[128, nh4], [64, 2], [1, 64]])
            mh = eluq(q)
            m1 = mkap(mh, 0, [[128, nh4], [64, 2], [1, 64]])
            v.tensor_tensor(d4, pp4, cos4, ALU.mult)
            v.tensor_tensor(m1, swp, sin4, ALU.mult)
            v.drain()
            v.tensor_add(dst, dst, mh)
            v.drain()
            v.tensor_scalar(mh, dst, 0.0, None, ALU.min)
            v.drain()

        for s in range(4):
            def _ldw(gp, s=s):
                mwait(gp, 'wsb%d' % (s % 2))
                stream_weight(gp, wqT, WS[s % 2], NT, 512, s * 512)
                return None
            G(_ldw)
            vmarks = []
            sbase_e = CN['S']
            for tt in range(4):
                def _mmq(te, s=s, tt=tt, se=sbase_e, apl=tuple(aplA)):
                    w(te, 'dG')
                    if s == 0:
                        mwait(te, 'ppq%d' % (tt % 2))
                    else:
                        mwait(te, 'yA', 'ppq%d' % (tt % 2))
                    last = None
                    pp = psP[tt % 2]
                    for j in range(NT):
                        if s == 0:
                            te.wait_ge(SEMH['V'], apl[j // 2])
                        last = te.matmul(pp[:, :], ybf.tc(j, tt * 128, (tt + 1) * 128), WS[s % 2].t(j),
                                         start=(j == 0), stop=(j == NT - 1))
                    return last
                T(_mmq)
                def _rope_q(v, s=s, tt=tt, se=sbase_e):
                    w(v, 'T')
                    rope2(v, psP[tt % 2], qfB.tc(tt, s * 512, (s + 1) * 512), tt, tt)
                    return v.drain()
                V(_rope_q)
                mset('ppq%d' % (tt % 2), 'V')
                vmarks.append(CN['V'])
            def _ees(sc, vmarks=tuple(vmarks)):
                for tt in range(4):
                    sc.wait_ge(SEMH['V'], vmarks[tt])
                    sc.activation(eluq(tt), eluq(tt), AF.Exp).then_inc(SEMH['S'], 1)
                return None
            sec('S', _ees, None)
            CN['S'] += 4
            def _efs(v, s=s, se=sbase_e):
                for tt in range(4):
                    v.wait_ge(SEMH['S'], se + tt + 1)
                    v.scalar_tensor_tensor(qfB.tc(tt, s * 512, (s + 1) * 512),
                                           qfB.tc(tt, s * 512, (s + 1) * 512), 0.0,
                                           eluq(tt), ALU.max, ALU.add)
                    v.drain()
                return v.drain()
            V(_efs)
            mset('wsb%d' % (s % 2), 'T')

        def _ldk(gp):
            mwait(gp, 'wsb0')
            stream_weight(gp, wkT, WS[0], NT, 512, 0)
            return None
        G(_ldk)
        vmarks = []
        sbase_e = CN['S']
        for tt in range(4):
            def _mmk(te, tt=tt, se=sbase_e):
                w(te, 'dG')
                mwait(te, 'yA', 'ppq%d' % (tt % 2))
                last = None
                pp = psP[tt % 2]
                for j in range(NT):
                    last = te.matmul(pp[:, :], ybf.tc(j, tt * 128, (tt + 1) * 128), WS[0].t(j),
                                     start=(j == 0), stop=(j == NT - 1))
                return last
            T(_mmk)
            def _rope_k(v, tt=tt, se=sbase_e):
                w(v, 'T')
                rope2(v, psP[tt % 2], kfB.t(tt), tt, tt)
                return v.drain()
            V(_rope_k)
            mset('ppq%d' % (tt % 2), 'V')
            vmarks.append(CN['V'])
        def _eeks(sc, vmarks=tuple(vmarks)):
            for tt in range(4):
                sc.wait_ge(SEMH['V'], vmarks[tt])
                sc.activation(eluq(tt), eluq(tt), AF.Exp).then_inc(SEMH['S'], 1)
            return None
        sec('S', _eeks, None)
        CN['S'] += 4
        def _efks(v, se=sbase_e):
            for tt in range(4):
                v.wait_ge(SEMH['S'], se + tt + 1)
                v.scalar_tensor_tensor(kfB.t(tt), kfB.t(tt), 0.0,
                                       eluq(tt), ALU.max, ALU.add)
                v.drain()
            return v.drain()
        V(_efks)

        mset('wsb0', 'T')
        def _ldv(gp):
            mwait(gp, 'wsb1')
            stream_weight(gp, wvT, WS[1], NT, 512, 0)
            return None
        G(_ldv)
        for tt in range(4):
            def _mmv(te, tt=tt):
                w(te, 'dG')
                mwait(te, 'yA', 'ppq%d' % (tt % 2))
                last = None
                pp = psP[tt % 2]
                for j in range(NT):
                    last = te.matmul(pp[:, :], ybf.tc(j, tt * 128, (tt + 1) * 128), WS[1].t(j),
                                     start=(j == 0), stop=(j == NT - 1))
                return last
            T(_mmv)
            def _ev_v(v, tt=tt):
                w(v, 'T')
                pp = psP[tt % 2]
                src = pp[:, :].rearrange('p (h c) -> p h c', c=128)
                dstv = mkap(vaug.t_, tt * 516, [[129, 4], [1, 128]])
                v.tensor_copy(dstv, src)
                return v.drain()
            V(_ev_v)
            mset('ppq%d' % (tt % 2), 'V')

        mset('wsb1', 'T')
        for h2 in range(0, NH, 2):
            def _tr_q(te, h2=h2):
                w(te, 'V')
                last = None
                for hh in range(2):
                    for tt in range(4):
                        last = te.transpose(psT_[:, hh * 512 + tt * 128:hh * 512 + (tt + 1) * 128],
                                            qfB.tc(tt, (h2 + hh) * 128, (h2 + hh + 1) * 128), ident[:])
                return last
            T(_tr_q)
            def _ev_q(v, h2=h2):
                w(v, 'T')
                v.tensor_copy(qfA.t(h2), psT_[:, 0:512])
                v.tensor_copy(qfA.t(h2 + 1), psT_[:, 512:1024])
                return v.drain()
            V(_ev_q)
        for h2 in range(0, KVH, 2):
            def _tr_k(te, h2=h2):
                w(te, 'V')
                last = None
                for hh in range(2):
                    for tt in range(4):
                        last = te.transpose(psT_[:, hh * 512 + tt * 128:hh * 512 + (tt + 1) * 128],
                                            kfB.tc(tt, (h2 + hh) * 128, (h2 + hh + 1) * 128), ident[:])
                return last
            T(_tr_k)
            def _ev_k(v, h2=h2):
                w(v, 'T')
                v.tensor_copy(kfA.t(h2), psT_[:, 0:512])
                v.tensor_copy(kfA.t(h2 + 1), psT_[:, 512:1024])
                return v.drain()
            V(_ev_k)
        mset('qkvA', 'V')



    scdiv = [0, 0]
    scS_pred = [0, 0]
    def emit_scan(bglob):
        AVB = [psV_[0], psV_[1], psP[0], psP[1]]
        for cc in range(4):
            sdbase = CN['S']
            tbase = CN['T']
            vbase = CN['V']
            sdw = [(sdbase + g - 1) if g >= 2 else scdiv[g % 2] for g in range(4)]
            scw = list(scS_pred)
            def _scanT(te, cc=cc, tb=tbase, vb=vbase, sdw=tuple(sdw), scw=tuple(scw)):
                mwait(te, 'qkvA')
                for g in range(4):
                    h0 = g * 4
                    kv = g
                    sw = (vb + 2 * (g - 2) + 1) if g >= 2 else scw[g % 2]
                    if sw > 0:
                        te.wait_ge(SEMH['V'], sw)
                    pp = psS_[g % 2]
                    last = None
                    for i in range(4):
                        last = te.matmul(pp[:, i * 128:(i + 1) * 128], kfA.tc(kv, cc * 128, (cc + 1) * 128),
                                         qfA.tc(h0 + i, cc * 128, (cc + 1) * 128), start=True, stop=True)
                    last.then_inc(SEMH['T'], 1)
                    te.wait_ge(SEMH['V'], vb + 2 * g + 1)
                    if sdw[g] > 0:
                        te.wait_ge(SEMH['S'], sdw[g])
                    for i in range(4):
                        ppa = AVB[2 * (g % 2) + i // 2]
                        c0 = (i % 2) * 256
                        msi = mst[:, (g % 2) * 512 + i * 128:(g % 2) * 512 + (i + 1) * 128]
                        te.matmul(ppa[:, c0:c0 + 129], msi,
                                  vaug.t_[:, cc * 516 + kv * 129: cc * 516 + kv * 129 + 129],
                                  start=True, stop=False)
                        last = te.matmul(ppa[:, c0:c0 + 129], qfA.tc(h0 + i, cc * 128, (cc + 1) * 128),
                                         stbf[:, kv * 129:(kv + 1) * 129], start=False, stop=True)
                    last.then_inc(SEMH['T'], 1)
                return None
            sec('T', _scanT, None)
            CN['T'] += 8
            for g in range(4):
                def _mask(v, g=g, tb=tbase):
                    v.wait_ge(SEMH['T'], tb + 2 * g + 1)
                    pp = psS_[g % 2]
                    src = mkap(pp, 0, [[128, 4], [1, 128]])
                    mk = mkap(maskT, 0, [[0, 4], [1, 128]])
                    dst = mkap(mst, (g % 2) * 512, [[128, 4], [1, 128]])
                    v.tensor_tensor(dst, src, mk, ALU.mult)
                    return v.drain()
                V(_mask)
                def _recip(v, g=g, tb=tbase):
                    v.wait_ge(SEMH['T'], tb + 2 * g + 2)
                    base = 1536 + g * 16
                    for i2 in range(2):
                        pp = AVB[2 * (g % 2) + i2]
                        dn = mkap(pp, 128, [[256, 2], [1, 1]])
                        v.tensor_scalar(gst.t_[:, base + i2 * 4:base + i2 * 4 + 2], dn, EPS_A, None, ALU.add)
                    v.drain()
                    for i2 in range(2):
                        v.reciprocal(gst.t_[:, base + i2 * 4 + 2:base + i2 * 4 + 4],
                                     gst.t_[:, base + i2 * 4:base + i2 * 4 + 2])
                    return v.drain()
                V(_recip)
            for g in range(4):
                scS_pred[g % 2] = vbase + 2 * g + 1
            def _sdivs(sc, cc=cc, vb=vbase):
                for g in range(4):
                    h0 = g * 4
                    sc.wait_ge(SEMH['V'], vb + 2 * g + 2)
                    for i in range(4):
                        ppa = AVB[2 * (g % 2) + i // 2]
                        c0 = (i % 2) * 256
                        scol = 1536 + g * 16 + (i // 2) * 4 + 2 + (i % 2)
                        ins = sc.activation(qfB.tc(cc, (h0 + i) * 128, (h0 + i + 1) * 128),
                                            ppa[:, c0:c0 + 128], AF.Identity,
                                            scale=gst.t_[:, scol:scol + 1])
                        if i == 3:
                            ins.then_inc(SEMH['S'], 1)
                return None
            sec('S', _sdivs, None)
            CN['S'] += 4
            scdiv[0] = sdbase + 3
            scdiv[1] = sdbase + 4
            for rnd in range(2):
                def _mmu(te, cc=cc, rnd=rnd):
                    mwait(te, 'psU')
                    kv0, kv1 = rnd * 2, rnd * 2 + 1
                    te.matmul(psU[:, 0:129], kfB.tc(cc, kv0 * 128, (kv0 + 1) * 128),
                              vaug.t_[:, cc * 516 + kv0 * 129: cc * 516 + kv0 * 129 + 129],
                              start=True, stop=True)
                    return te.matmul(psU[:, 256:385], kfB.tc(cc, kv1 * 128, (kv1 + 1) * 128),
                                     vaug.t_[:, cc * 516 + kv1 * 129: cc * 516 + kv1 * 129 + 129],
                                     start=True, stop=True)
                T(_mmu)
                def _upd(v, rnd=rnd):
                    w(v, 'T')
                    kv0, kv1 = rnd * 2, rnd * 2 + 1
                    v.tensor_add(stLa[:, kv0 * 129:(kv0 + 1) * 129], stLa[:, kv0 * 129:(kv0 + 1) * 129], psU[:, 0:129])
                    v.tensor_add(stLa[:, kv1 * 129:(kv1 + 1) * 129], stLa[:, kv1 * 129:(kv1 + 1) * 129], psU[:, 256:385])
                    v.drain()
                    v.tensor_copy(stbf[:, kv0 * 129:(kv0 + 1) * 129], stLa[:, kv0 * 129:(kv0 + 1) * 129])
                    v.tensor_copy(stbf[:, kv1 * 129:(kv1 + 1) * 129], stLa[:, kv1 * 129:(kv1 + 1) * 129])
                    return v.drain()
                V(_upd)
                mset('psU', 'V')

    def emit_out_wo(bglob):
        for h2 in range(0, NH, 2):
            def _tr_o(te, h2=h2):
                w(te, 'V', 'S')
                last = None
                for hh in range(2):
                    for tt in range(4):
                        last = te.transpose(psT_[:, hh * 512 + tt * 128:hh * 512 + (tt + 1) * 128],
                                            qfB.tc(tt, (h2 + hh) * 128, (h2 + hh + 1) * 128), ident[:])
                return last
            T(_tr_o)
            def _ev_o(v, h2=h2):
                w(v, 'T')
                v.tensor_copy(qfA.t(h2), psT_[:, 0:512])
                v.tensor_copy(qfA.t(h2 + 1), psT_[:, 512:1024])
                return v.drain()
            V(_ev_o)
        mset('outA', 'V')
        for s_ in range(4):
            def _ldwo(gp, s_=s_):
                mwait(gp, 'wsb%d' % (s_ % 2))
                stream_weight(gp, woaT, WS[s_ % 2], NT, 512, s_ * 512)
                return None
            G(_ldwo)
            for dt_ in range(4):
                dmt = s_ * 4 + dt_
                def _mmo(te, s_=s_, dt_=dt_, dmt=dmt):
                    w(te, 'dG')
                    mwait(te, 'outA', 'ppq%d' % (dmt % 2))
                    pp = psP[dmt % 2]
                    last = None
                    for j in range(NT):
                        last = te.matmul(pp[:, :], WS[s_ % 2].tc(j, dt_ * 128, (dt_ + 1) * 128),
                                         qfA.t(j), start=(j == 0), stop=(j == NT - 1))
                    return last
                T(_mmo)
                def _res(v, dmt=dmt):
                    w(v, 'T')
                    v.tensor_add(Xa.t(dmt), Xa.t(dmt), psP[dmt % 2][:, :])
                    v.drain()
                    v.tensor_copy(ybf.t(dmt), Xa.t(dmt))
                    if dmt == 0:
                        v.tensor_copy(gst.t_[:, 0:512], Xa.t(0))
                        v.tensor_copy(gst.t_[:, 1024:1536], Xa.t(0))
                    else:
                        v.tensor_max(gst.t_[:, 0:512], gst.t_[:, 0:512], Xa.t(dmt))
                        v.tensor_tensor(gst.t_[:, 1024:1536], gst.t_[:, 1024:1536], Xa.t(dmt), ALU.min)
                    return v.drain()
                V(_res)
                mset('ppq%d' % (dmt % 2), 'V')
            mset('wsb%d' % (s_ % 2), 'T')


    def emit_stageA(milestone):
        xdw = None
        for b in range(NB):
            emit_stageA_head(b, xdw)
            if milestone == 'proj' and b == 0:
                return 'stop'
            emit_scan(b)
            mset('ppq0', 'S')
            mset('ppq1', 'S')
            emit_out_wo(b)
            if milestone == 'attnA' and b == 0:
                return 'stop'
            dqm = emit_ffn(Xa, ybf, WS, first_block=(b == 0), prefused=True)
            if milestone == 'ffn1' and b == 0:
                return 'stop'
            xdw = [] if b + 1 < NB else None
            for st4 in range(4):
                def _stx(sy, b=b, st4=st4, vw=dqm[st4]):
                    sy.wait_ge(s_V, vw)
                    src3 = Xa.t_[:, st4 * 2048:(st4 + 1) * 2048].rearrange('p (n c) -> p n c', c=512)
                    dst3 = x2s[st4 * 512:(st4 + 1) * 512, b * BLK:(b + 1) * BLK].rearrange('(n p) c -> p n c', p=128)
                    dma(sy, dst3, src3, 'dO')
                    return None
                Y(_stx)
                if b + 1 < NB:
                    def _ldxs(sy, b=b, st4=st4):
                        src3 = xT[st4 * 512:(st4 + 1) * 512, (b + 1) * BLK:(b + 2) * BLK].rearrange('(n p) c -> p n c', p=128)
                        dst3 = Xa.t_[:, st4 * 2048:(st4 + 1) * 2048].rearrange('p (n c) -> p n c', c=512)
                        dma(sy, dst3, src3, 'dW')
                        return None
                    Y(_ldxs)
                    xdw.append(CN['dW'])
        return None

    r = emit_stageA(stop_after)
    if r == 'stop':
        if stop_after == 'proj':
            def _fin(sy):
                w(sy, 'S', 'V', 'T')
                for nm, src, sh, dt in (('qfA', qfA.t_[:, :], [128, NT * 512], BF16),
                                    ('kfA', kfA.t_[:, :], [128, 4 * 512], BF16),
                                    ('vaug', vaug.t_[:, :], [128, 4 * 516], BF16),
                                    ('ybf', ybf.t_[:, :], [128, NT * 512], BF16),
                                    ('m', rowt[0:1, 0:512], [1, 512], F32),
                                    ('rstd', rowt[0:1, 512:1024], [1, 512], F32)):
                    d = dbg(nm, sh, dt)
                    if d is not None:
                        dma(sy, d, src, 'dO')
                sy.wait_ge(s_dO, CN['dO'])
                return None
            Y(_fin)
            ctx.close()
            return nc, dbg_list


        if stop_after == 'attnA':
            def _fin(sy):
                w(sy, 'S', 'V', 'T')
                d = dbg('X1', [128, NT * 512], F32)
                if d is not None:
                    dma(sy, d, Xa.t_[:, :], 'dO')
                sy.wait_ge(s_dO, CN['dO'])
                return None
            Y(_fin)
            ctx.close()
            return nc, dbg_list


        if stop_after == 'ffn1':
            def _fin(sy):
                w(sy, 'S', 'V', 'T')
                d = dbg('X2', [128, NT * 512], F32)
                if d is not None:
                    dma(sy, d, Xa.t_[:, :], 'dO')
                sy.wait_ge(s_dO, CN['dO'])
                return None
            Y(_fin)
            ctx.close()
            return nc, dbg_list


    if stop_after == 'stageA':
        def _fin(sy):
            w(sy, 'S', 'V', 'T')
            sy.wait_ge(SEMH['dO'], CN['dO'])
            d = dbg('X2full', [D, S_], F32)
            if d is not None:
                dma(sy, d, x2s[:, :], 'dO')
            sy.wait_ge(s_dO, CN['dO'])
            return None
        Y(_fin)
        ctx.close()
        return nc, dbg_list

    stA_scope.close()
    stAB = ctx.enter_context(contextlib.ExitStack())
    latq = Mat(sbt('latq', [128, 4 * 2048], BF16, stAB), 4, 2048)
    vaugB = Mat(sbt('vaugB', [128, 16 * 2064], BF16, stAB), 16, 2064)

    stA2 = ctx.enter_context(contextlib.ExitStack())
    Xa2 = Mat(sbt('Xa2', [128, NT * 512], F32, stA2), NT, 512)
    y3 = Mat(sbt('y3', [128, NT * 512], BF16, stA2), NT, 512)
    kvlat = Mat(sbt('kvlat', [128, 4 * 512], BF16, stA2), 4, 512)
    tabBc = sbt('tabBc', [128, 4 * 512], BF16, stA2)
    tabBs = sbt('tabBs', [128, 4 * 512], BF16, stA2)
    wsA2 = Mat(wsA2t, NT, 512)
    ws_kn8 = Mat(wsA2t, 8, 1024)
    ws_vu = Mat(wsA2t, 4, 2048)
    BK4 = [psP[0], psP[1], psS_[0], psS_[1]]

    def _a2init(v):
        w(v, 'V', 'T', 'S', 'dO', 'dW', 'dG')
        v.memset(mkap(vaugB.t_, 128, [[2064, 16], [129, 16], [1, 1]]), 1.0)
        return v.drain()
    V(_a2init)
    for b in range(NB):
        def _ldtb(sy, b=b):
            if b > 0:
                mwait(sy, 'ekrdone')
            dma(sy, tabBc[:, :], tabBcin[:, b * 2048:(b + 1) * 2048], 'dW')
            dma(sy, tabBs[:, :], tabBsin[:, b * 2048:(b + 1) * 2048], 'dW')
            return None
        Y(_ldtb)
        if b == 0:
            def _ldx2(sy, b=b):
                w(sy, 'V', 'dO')
                src3 = x2s[:, b * BLK:(b + 1) * BLK].rearrange('(n p) c -> p n c', p=128)
                dst3 = Xa2.t_[:, :].rearrange('p (n c) -> p n c', c=512)
                dma(sy, dst3, src3, 'dW')
                return None
            Y(_ldx2)
            def _wx2(v):
                w(v, 'V', 'dW')
                return v.drain()
            V(_wx2)
            ln_stats(Xa2, y3)
            vmst_a2 = CN['V']
            aplA2 = ln_apply(Xa2, y3)
            mset('yA', 'V')
            def _ldx2n(sy):
                mwait(sy, 'yA')
                src3 = x2s[:, 1 * BLK:2 * BLK].rearrange('(n p) c -> p n c', p=128)
                dst3 = Xa2.t_[:, :].rearrange('p (n c) -> p n c', c=512)
                dma(sy, dst3, src3, 'dW')
                return None
            Y(_ldx2n)
        for wsrc, bias, dst in ((wqdT, cbqd, 'q'), (wkvdT, cbkvd, 'kv')):
            def _ldd0(gp, wsrc=wsrc):
                mwait(gp, 'a2h0')
                stream_weight_rt(gp, wsrc, wsA2, 0, 8, 0, 512, 0)
                return None
            G(_ldd0)
            dg0 = CN['dG']
            def _ldd1(gp, wsrc=wsrc):
                mwait(gp, 'a2h1')
                stream_weight_rt(gp, wsrc, wsA2, 8, 8, 8, 512, 0)
                return None
            G(_ldd1)
            dg1 = CN['dG']
            for lt in range(4):
                def _mmd0(te, lt=lt, dg0=dg0, first=(dst == 'q'), apl=tuple(aplA2)):
                    te.wait_ge(SEMH['dG'], dg0)
                    if first:
                        mwait(te, 'ppq%d' % lt)
                    else:
                        mwait(te, 'yA', 'ppq%d' % lt)
                    pp = BK4[lt]
                    last = None
                    for j in range(8):
                        if first:
                            te.wait_ge(SEMH['V'], apl[j // 2])
                        last = te.matmul(pp[:, :], wsA2.tc(j, lt * 128, (lt + 1) * 128), y3.t(j),
                                         start=(j == 0), stop=False)
                    return last
                T(_mmd0)
            mset('a2h0', 'T')
            for lt in range(4):
                def _mmd1(te, lt=lt, dg1=dg1, first=(dst == 'q'), apl=tuple(aplA2)):
                    te.wait_ge(SEMH['dG'], dg1)
                    pp = BK4[lt]
                    last = None
                    for j in range(8, NT):
                        if first:
                            te.wait_ge(SEMH['V'], apl[j // 2])
                        last = te.matmul(pp[:, :], wsA2.tc(j, lt * 128, (lt + 1) * 128), y3.t(j),
                                         start=False, stop=(j == NT - 1))
                    return last
                T(_mmd1)
                def _evd(v, lt=lt, bias=bias, dst=dst, b=b):
                    w(v, 'T')
                    pp = BK4[lt]
                    if dst == 'q':
                        v.tensor_scalar(latq.tc(lt, b * 512, (b + 1) * 512), pp[:, :], bias[:, lt:lt + 1], None, ALU.add)
                    else:
                        v.tensor_scalar(kvlat.t(lt), pp[:, :], bias[:, lt:lt + 1], None, ALU.add)
                    return v.drain()
                V(_evd)
                mset('ppq%d' % lt, 'V')
            mset('a2h1', 'T')
        mset('kvlat', 'V')
        def _ldkn(gp):
            mwait(gp, 'a2h0')
            stream_weight_rt(gp, wknT, ws_kn8, 0, 4, 0, 1024, 0)
            return None
        G(_ldkn)
        dgm_kn = CN['dG']
        sbase_kn = CN['S']
        tpred_kn = []
        for ot in range(8):
            def _mkn(te, ot=ot, dgm=dgm_kn, sb=sbase_kn):
                te.wait_ge(SEMH['dG'], dgm)
                if ot >= 4:
                    te.wait_ge(SEMH['S'], sb + ot - 3)
                    mwait(te, 'kvlat')
                else:
                    mwait(te, 'kvlat', 'ppq%d' % (ot % 4))
                pp = BK4[ot % 4]
                last = None
                for lt in range(4):
                    last = te.matmul(pp[:, :], ws_kn8.tc(lt, ot * 128, (ot + 1) * 128), kvlat.t(lt),
                                     start=(lt == 0), stop=(lt == 3))
                return last
            T(_mkn)
            tpred_kn.append(CN['T'])
            if ot == 7:
                mset('a2h0', 'T')
        dO0_kn = CN['dO']
        def _ekns(sc, tpred=tuple(tpred_kn), sb=sbase_kn, dO0=dO0_kn, vm=vmst_a2):
            sc.wait_ge(SEMH['V'], vm)
            sc.wait_ge(s_dO, dO0)
            for ot in range(8):
                sc.wait_ge(SEMH['T'], tpred[ot])
                if ot >= 2:
                    sc.wait_ge(s_dO, dO0 + (ot - 1) * 16)
                sc.activation(mst[:, (ot % 2) * 512:(ot % 2) * 512 + 512], BK4[ot % 4][:, :], AF.Identity,
                              bias=cbkn[:, ot:ot + 1]).then_inc(SEMH['S'], 1)
            return None
        sec('S', _ekns, None)
        CN['S'] += 8
        for ot in range(4, 8):
            MK['ppq%d' % (ot % 4)] = ('S', sbase_kn + ot + 1)
        for ot in range(8):
            def _skn(sy, ot=ot, b=b, sb=sbase_kn):
                sy.wait_ge(SEMH['S'], sb + ot + 1)
                dma(sy, kAs[ot * 128:(ot + 1) * 128, b * BLK:(b + 1) * BLK], mst[:, (ot % 2) * 512:(ot % 2) * 512 + 512], 'dO')
                return None
            Y(_skn)
        def _ldkr(gp):
            mwait(gp, 'a2h1')
            stream_weight_rt(gp, wkrT, ws_kn8, 0, 4, 4, 1024, 0)
            return None
        G(_ldkr)
        dgm_kr = CN['dG']
        for jt in range(4):
            def _mkr(te, jt=jt, dgm=dgm_kr):
                te.wait_ge(SEMH['dG'], dgm)
                mwait(te, 'kvlat', 'ppq0', 'ppq1')
                last = None
                for lt in range(4):
                    last = te.matmul(psP[0][:, :], ws_kn8.tc(4 + lt, jt * 128, (jt + 1) * 128), kvlat.t(lt),
                                     start=(lt == 0), stop=(lt == 3))
                for lt in range(4):
                    last = te.matmul(psP[1][:, :], ws_kn8.tc(4 + lt, (jt + 4) * 128, (jt + 5) * 128), kvlat.t(lt),
                                     start=(lt == 0), stop=(lt == 3))
                return last
            T(_mkr)
            if jt == 3:
                mset('a2h1', 'T')
            def _ekr(v, jt=jt):
                w(v, 'T', 'dO', 'S', 'dW')
                v.tensor_scalar(gst.t_[:, 0:512], psP[0][:, :], cbkr[:, jt:jt + 1], None, ALU.add)
                v.tensor_scalar(gst.t_[:, 512:1024], psP[1][:, :], cbkr[:, jt + 4:jt + 5], None, ALU.add)
                v.drain()
                v.tensor_mul(gst.t_[:, 1024:1536], gst.t_[:, 0:512], tabBc[:, jt * 512:(jt + 1) * 512])
                v.tensor_mul(gst.t_[:, 1536:2048], gst.t_[:, 512:1024], tabBs[:, jt * 512:(jt + 1) * 512])
                v.drain()
                v.tensor_sub(mst[:, 0:512], gst.t_[:, 1024:1536], gst.t_[:, 1536:2048])
                v.drain()
                v.tensor_mul(gst.t_[:, 1024:1536], gst.t_[:, 512:1024], tabBc[:, jt * 512:(jt + 1) * 512])
                v.tensor_mul(gst.t_[:, 1536:2048], gst.t_[:, 0:512], tabBs[:, jt * 512:(jt + 1) * 512])
                v.drain()
                v.tensor_add(mst[:, 512:1024], gst.t_[:, 1024:1536], gst.t_[:, 1536:2048])
                return v.drain()
            V(_ekr)
            mset('ppq0', 'V')
            mset('ppq1', 'V')
            def _skr(sy, jt=jt, b=b):
                w(sy, 'V')
                dma(sy, kAs[(8 + jt) * 128:(9 + jt) * 128, b * BLK:(b + 1) * BLK], mst[:, 0:512], 'dO')
                dma(sy, kAs[(12 + jt) * 128:(13 + jt) * 128, b * BLK:(b + 1) * BLK], mst[:, 512:1024], 'dO')
                return None
            Y(_skr)
            if jt == 3:
                mset('ekrdone', 'V')
        if b + 1 < NB:
            def _wko(v):
                w(v, 'dO', 'dW')
                return v.drain()
            V(_wko)
            ln_stats(Xa2, y3)
            vmst_a2 = CN['V']
            mset('ppq0', 'V')
            mset('ppq1', 'V')
            aplA2 = ln_apply(Xa2, y3)
            mset('yA', 'V')
            if b + 2 < NB:
                def _ldx2n2(sy, b=b):
                    mwait(sy, 'yA')
                    src3 = x2s[:, (b + 2) * BLK:(b + 3) * BLK].rearrange('(n p) c -> p n c', p=128)
                    dst3 = Xa2.t_[:, :].rearrange('p (n c) -> p n c', c=512)
                    dma(sy, dst3, src3, 'dW')
                    return None
                Y(_ldx2n2)

        def _ldvu0(gp):
            mwait(gp, 'a2h0')
            stream_weight_rt(gp, wvuT, ws_vu, 0, 2, 0, 2048, 0)
            return None
        G(_ldvu0)
        dgv0 = CN['dG']
        def _ldvu1(gp):
            mwait(gp, 'a2h1')
            stream_weight_rt(gp, wvuT, ws_vu, 2, 2, 2, 2048, 0)
            return None
        G(_ldvu1)
        dgv1 = CN['dG']
        sbase_vu = CN['S']
        tpred_vu = []
        for tt in range(4):
            for sv in range(4):
                bank = (tt * 4 + sv) % 4
                idx = tt * 4 + sv
                def _mvu0(te, tt=tt, sv=sv, bank=bank, idx=idx, dgv0=dgv0, sb=sbase_vu):
                    te.wait_ge(SEMH['dG'], dgv0)
                    if idx >= 4:
                        te.wait_ge(SEMH['S'], sb + idx - 3)
                        mwait(te, 'kvlat')
                    else:
                        mwait(te, 'kvlat', 'ppq%d' % bank)
                    pp = BK4[bank]
                    last = None
                    for lt in range(2):
                        last = te.matmul(pp[:, :], kvlat.tc(lt, tt * 128, (tt + 1) * 128),
                                         ws_vu.tc(lt, sv * 512, (sv + 1) * 512), start=(lt == 0), stop=False)
                    return last
                T(_mvu0)
                if tt == 3 and sv == 3:
                    mset('a2h0', 'T')
                def _mvu1(te, tt=tt, sv=sv, bank=bank, dgv1=dgv1):
                    te.wait_ge(SEMH['dG'], dgv1)
                    pp = BK4[bank]
                    for lt in range(2, 4):
                        te.matmul(pp[:, :], kvlat.tc(lt, tt * 128, (tt + 1) * 128),
                                  ws_vu.tc(lt, sv * 512, (sv + 1) * 512), start=False, stop=False)
                    return te.matmul(pp[:, :], os32b[:], bvu32[:, sv * 512:(sv + 1) * 512], start=False, stop=True)
                T(_mvu1)
                tpred_vu.append(CN['T'])
        mset('a2h1', 'T')
        def _evus(sc, tpred=tuple(tpred_vu), sb=sbase_vu, b=b, vm=vmst_a2):
            sc.wait_ge(SEMH['V'], vm)
            for tt in range(4):
                for sv in range(4):
                    idx = tt * 4 + sv
                    bank = idx % 4
                    sc.wait_ge(SEMH['T'], tpred[idx])
                    pp = BK4[bank]
                    src = pp[:, :].rearrange('p (h c) -> p h c', c=128)
                    dstv = mkap(vaugB.t_, (b * 4 + tt) * 2064 + 4 * sv * 129, [[129, 4], [1, 128]])
                    sc.copy(dstv, src).then_inc(SEMH['S'], 1)
            return None
        sec('S', _evus, None)
        CN['S'] += 16
        for idx in range(12, 16):
            MK['ppq%d' % (idx % 4)] = ('S', sbase_vu + idx + 1)

    stA2.close()

    if stop_after == 'a2':
        def _fin(sy):
            w(sy, 'S', 'V', 'T')
            sy.wait_ge(SEMH['dO'], CN['dO'])
            for nm, src, sh, dt in (('kAfull', kAs[:, :], [D, S_], BF16),
                                     ('latq', latq.t_[:, :], [128, 4 * 2048], BF16),
                                     ('vaugB', vaugB.t_[:, :], [128, 16 * 2064], BF16)):
                d = dbg(nm, sh, dt)
                if d is not None:
                    dma(sy, d, src, 'dO')
            sy.wait_ge(s_dO, CN['dO'])
            return None
        Y(_fin)
        ctx.close()
        return nc, dbg_list

    stB = ctx.enter_context(contextlib.ExitStack())
    wqut = sbt('wqut', [128, 4 * 2048], BF16, stB)
    wqu = Mat(wqut, 4, 2048)
    kAh = sbt('kAh', [128, 2048], BF16, stB)
    kAh2 = sbt('kAh2', [128, 2048], BF16, stB)
    KAHB = [kAh, kAh2]
    qAh = sbt('qAh', [128, 2048], BF16, stB)
    expb = sbt('expb', [128, 16 * 512], BF16, stB)
    aostg = sbt('aostg', [128, 512], BF16, stB)

    def _ldqu(gp):
        w(gp, 'T', 'V', 'S', 'dO')
        stream_weight(gp, wquT, wqu, 4, 2048, 0)
        return None
    G(_ldqu)

    SCB = [psP[0], psP[1], psS_[0], psS_[1]]
    scbmark = [0, 0, 0, 0]
    dw_kh = [0, 0]

    def emit_ldkh(h):
        def _ldkh(sy, h=h):
            mwait(sy, 'kAh%d' % (h % 2))
            w(sy, 'T', 'V', 'S', 'dO') if h == 0 else None
            dma(sy, KAHB[h % 2][:, :], kAs[h * 128:(h + 1) * 128, :], 'dW')
            return None
        Y(_ldkh)
        dw_kh[h % 2] = CN['dW']

    emit_ldkh(0)
    for h in range(NH):
        kbuf = KAHB[h % 2]
        dwm_kh = dw_kh[h % 2]
        for qb in range(4):
            def _mqu(te, h=h, qb=qb):
                w(te, 'dG')
                mwait(te, 'psVav', 'psVq%d' % (qb % 2))
                if h == 0 and qb == 0:
                    w(te, 'V', 'S')
                pp = psV_[qb % 2]
                last = None
                for lt in range(4):
                    last = te.matmul(pp[:, :], wqu.tc(lt, h * 128, (h + 1) * 128),
                                     latq.tc(lt, qb * 512, (qb + 1) * 512), start=(lt == 0), stop=(lt == 3))
                return last
            T(_mqu)
            def _equ(v, h=h, qb=qb):
                w(v, 'T')
                v.tensor_scalar(qAh[:, qb * 512:(qb + 1) * 512], psV_[qb % 2][:, :], cbqu[:, h:h + 1], None, ALU.add)
                return v.drain()
            V(_equ)
            mset('psVq%d' % (qb % 2), 'V')
        mset('qAh', 'V')
        if h + 1 < NH:
            emit_ldkh(h + 1)
        for qb in range(4):
            sbase = CN['S']
            tbase = CN['T']
            smw = [(sbase + kt - 3) if kt >= 4 else scbmark[kt % 4] for kt in range(16)]
            def _mscs(te, h=h, qb=qb, smw=tuple(smw), kbuf=kbuf, dwm_kh=dwm_kh):
                te.wait_ge(SEMH['dW'], dwm_kh)
                mwait(te, 'qAh')
                for kt in range(16):
                    if smw[kt] > 0:
                        te.wait_ge(SEMH['S'], smw[kt])
                    te.matmul(SCB[kt % 4][:, :], kbuf[:, kt * 128:(kt + 1) * 128],
                              qAh[:, qb * 512:(qb + 1) * 512], start=True,
                              stop=True).then_inc(SEMH['T'], 1)
                return None
            sec('T', _mscs, None)
            CN['T'] += 16
            def _exps(sc, h=h, qb=qb, tbase=tbase):
                for kt in range(16):
                    sc.wait_ge(SEMH['T'], tbase + kt + 1)
                    sc.activation(expb[:, kt * 512:(kt + 1) * 512], SCB[kt % 4][:, :], AF.Exp,
                                  scale=cpool[:, CPS:CPS + 1]).then_inc(SEMH['S'], 1)
                return None
            sec('S', _exps, None)
            CN['S'] += 16
            for kt in range(16):
                scbmark[kt % 4] = sbase + kt + 1
            if qb == 3:
                mset('kAh%d' % (h % 2), 'T')
            savb = CN['S']
            def _mavs(te, h=h, qb=qb, savb=savb):
                mwait(te, 'psVav', 'qAh')
                for qt in range(4):
                    pp = psV_[qt // 2]
                    c0 = (qt % 2) * 256
                    for kt in range(16):
                        if qt == 0 or kt == 15:
                            te.wait_ge(SEMH['S'], savb - 16 + kt + 1)
                        last = te.matmul(pp[:, c0:c0 + 129], expb[:, kt * 512 + qt * 128: kt * 512 + (qt + 1) * 128],
                                         vaugB.t_[:, kt * 2064 + h * 129: kt * 2064 + h * 129 + 129],
                                         start=(kt == 0), stop=(kt == 15))
                    last.then_inc(SEMH['T'], 1)
                return None
            sec('T', _mavs, None)
            CN['T'] += 4
            def _divb(v, qb=qb):
                w(v, 'T')
                for bank in range(2):
                    pp = psV_[bank]
                    dn = mkap(pp, 128, [[256, 2], [1, 1]])
                    v.reciprocal(gst.t_[:, 1536 + bank * 2:1538 + bank * 2], dn)
                    v.drain()
                    src = mkap(pp, 0, [[256, 2], [1, 128]])
                    rc = mkap(gst.t_, 1536 + bank * 2, [[1, 2], [0, 128]])
                    dst = mkap(aostg, bank * 256, [[128, 2], [1, 128]])
                    v.tensor_tensor(dst, src, rc, ALU.mult)
                    v.drain()
                return v.drain()
            V(_divb)
            mset('psVav', 'V')
            def _trb(te, qb=qb):
                w(te, 'V')
                last = None
                for qt in range(4):
                    last = te.transpose(psT_[:, qt * 128:(qt + 1) * 128], aostg[:, qt * 128:(qt + 1) * 128], ident[:])
                return last
            T(_trb)
            def _evb(v, qb=qb):
                w(v, 'T', 'dO')
                v.tensor_copy(mst[:, 0:512], psT_[:, 0:512])
                return v.drain()
            V(_evb)
            def _stb(sy, h=h, qb=qb):
                w(sy, 'V')
                dma(sy, aoas[h * 128:(h + 1) * 128, qb * 512:(qb + 1) * 512], mst[:, 0:512], 'dO')
                return None
            Y(_stb)

    stB.close()
    stAB.close()

    if stop_after == 'b':
        def _fin(sy):
            w(sy, 'S', 'V', 'T')
            sy.wait_ge(SEMH['dO'], CN['dO'])
            d = dbg('aofull', [D, S_], BF16)
            if d is not None:
                dma(sy, d, aoas[:, :], 'dO')
            sy.wait_ge(s_dO, CN['dO'])
            return None
        Y(_fin)
        ctx.close()
        return nc, dbg_list

    stC = ctx.enter_context(contextlib.ExitStack())
    Xc = Mat(sbt('Xc', [128, NT * 512], F32, stC), NT, 512)
    Xc2 = Mat(sbt('Xc2', [128, NT * 512], F32, stC), NT, 512)
    ybfC = Mat(sbt('ybfC', [128, NT * 512], BF16, stC), NT, 512)
    aoAc = Mat(sbt('aoAc', [128, NT * 512], BF16, stC), NT, 512)
    aoAc2 = Mat(sbt('aoAc2', [128, NT * 512], BF16, stC), NT, 512)
    ws0C = Mat(sbt('ws0C', [128, NT * 512], BF16, stC), NT, 512)
    ws1C = Mat(sbt('ws1C', [128, NT * 512], BF16, stC), NT, 512)
    WSC = [ws0C, ws1C]
    XCB = [Xc, Xc2]
    AOB = [aoAc, aoAc2]
    dw_xc = [0, 0]

    def emit_ldxc(b):
        def _ldxc(sy, b=b):
            if b == 0:
                w(sy, 'V', 'T', 'S', 'dG', 'dO')
            else:
                mwait(sy, 'xcst%d' % (b % 2), 'aofree%d' % (b % 2))
            src3 = x2s[:, b * BLK:(b + 1) * BLK].rearrange('(n p) c -> p n c', p=128)
            dst3 = XCB[b % 2].t_[:, :].rearrange('p (n c) -> p n c', c=512)
            dma(sy, dst3, src3, 'dW')
            src4 = aoas[:, b * BLK:(b + 1) * BLK].rearrange('(n p) c -> p n c', p=128)
            dst4 = AOB[b % 2].t_[:, :].rearrange('p (n c) -> p n c', c=512)
            dma(sy, dst4, src4, 'dW')
            return None
        Y(_ldxc)
        dw_xc[b % 2] = CN['dW']

    mset('ppq0', 'S')
    mset('ppq1', 'S')
    emit_ldxc(0)
    for b in range(NB):
        Xcur = XCB[b % 2]
        aoc = AOB[b % 2]
        dwm_xc = dw_xc[b % 2]
        for s_ in range(4):
            def _ldwoc(gp, s_=s_, first=(b == 0 and s_ < 2)):
                if first:
                    w(gp, 'T', 'V', 'S', 'dO')
                mwait(gp, 'wsb%d' % (s_ % 2))
                stream_weight(gp, wobT, WSC[s_ % 2], NT, 512, s_ * 512)
                return None
            G(_ldwoc)
            for dt_ in range(4):
                dmt = s_ * 4 + dt_
                def _mmoc(te, s_=s_, dt_=dt_, dmt=dmt, aoc=aoc, dwm_xc=dwm_xc):
                    w(te, 'dG')
                    te.wait_ge(SEMH['dW'], dwm_xc)
                    mwait(te, 'ppq%d' % (dmt % 2))
                    pp = psP[dmt % 2]
                    last = None
                    for j in range(NT):
                        last = te.matmul(pp[:, :], WSC[s_ % 2].tc(j, dt_ * 128, (dt_ + 1) * 128),
                                         aoc.t(j), start=(j == 0), stop=(j == NT - 1))
                    return last
                T(_mmoc)
                def _resc(v, dmt=dmt, Xcur=Xcur):
                    w(v, 'T')
                    v.scalar_tensor_tensor(Xcur.t(dmt), psP[dmt % 2][:, :], cbo[:, dmt:dmt + 1], Xcur.t(dmt), ALU.add, ALU.add)
                    v.drain()
                    v.tensor_copy(ybfC.t(dmt), Xcur.t(dmt))
                    if dmt == 0:
                        v.tensor_copy(gst.t_[:, 0:512], Xcur.t(0))
                        v.tensor_copy(gst.t_[:, 1024:1536], Xcur.t(0))
                    else:
                        v.tensor_max(gst.t_[:, 0:512], gst.t_[:, 0:512], Xcur.t(dmt))
                        v.tensor_tensor(gst.t_[:, 1024:1536], gst.t_[:, 1024:1536], Xcur.t(dmt), ALU.min)
                    return v.drain()
                V(_resc)
                mset('ppq%d' % (dmt % 2), 'V')
            mset('wsb%d' % (s_ % 2), 'T')
        mset('aofree%d' % (b % 2), 'T')
        if b + 1 < NB:
            emit_ldxc(b + 1)
        emit_ffn(Xcur, ybfC, WSC, first_block=False, prefused=True)
        def _stout(sy, b=b, Xcur=Xcur):
            w(sy, 'V', 'T')
            src3 = Xcur.t_[:, :].rearrange('p (n c) -> p n c', c=512)
            dst3 = xoT[:, b * BLK:(b + 1) * BLK].rearrange('(n p) c -> p n c', p=128)
            dma(sy, dst3, src3, 'dO')
            return None
        Y(_stout)
        mset('xcst%d' % (b % 2), 'dO')

    def _final(sy):
        w(sy, 'S', 'V', 'T')
        sy.wait_ge(s_dO, CN['dO'])
        return None
    Y(_final)
    ctx.close()
    return nc, dbg_list


import ml_dtypes


def _host_prep(inputs, core):
    d = {}
    d['xT'] = np.ascontiguousarray(np.asarray(inputs['L0_attn_a_x'])[core].T)
    d['wqT'] = np.ascontiguousarray(np.asarray(inputs['L0_attn_a_Wq']).T)
    d['wkT'] = np.ascontiguousarray(np.asarray(inputs['L0_attn_a_Wk']).T)
    d['wvT'] = np.ascontiguousarray(np.asarray(inputs['L0_attn_a_Wv']).T)
    d['woaT'] = np.ascontiguousarray(np.asarray(inputs['L0_attn_a_Wo']).T)
    d['wqdT'] = np.ascontiguousarray(np.asarray(inputs['L0_attn_b_Wq_down']).T)
    d['wkvdT'] = np.ascontiguousarray(np.asarray(inputs['L0_attn_b_Wkv_down']).T)
    d['wquT'] = np.ascontiguousarray(np.asarray(inputs['L0_attn_b_Wq_up']).T)
    d['wknT'] = np.ascontiguousarray(np.asarray(inputs['L0_attn_b_Wk_nope']).T)
    d['wkrT'] = np.ascontiguousarray(np.asarray(inputs['L0_attn_b_Wk_rope']).T)
    d['wvuT'] = np.ascontiguousarray(np.asarray(inputs['L0_attn_b_Wv_up']).T)
    d['wobT'] = np.ascontiguousarray(np.asarray(inputs['L0_attn_b_Wo']).T)
    d['wf8T'] = np.ascontiguousarray(np.asarray(inputs['L0_ffn_weight']).T)

    def cols(vec, n):
        return np.ascontiguousarray(np.asarray(vec, np.float32).reshape(n, 128).T)
    d['c_bqd'] = cols(inputs['L0_attn_b_bq_down'], 4)
    d['c_bkvd'] = cols(inputs['L0_attn_b_bkv_down'], 4)
    d['c_bqu'] = cols(inputs['L0_attn_b_bq_up'], 16)
    d['c_bkn'] = cols(inputs['L0_attn_b_bk_nope'], 8)
    d['c_bkr'] = cols(inputs['L0_attn_b_bk_rope'], 8)
    d['c_bo'] = cols(inputs['L0_attn_b_bo'], 16)
    d['c_fb'] = cols(inputs['L0_ffn_bias'], 16)
    d['c_ws'] = cols(np.asarray(inputs['L0_ffn_w_scale'])[:, 0], 16)
    d['r_bvu'] = np.asarray(inputs['L0_attn_b_bv_up'], np.float32).reshape(1, -1)

    bf = ml_dtypes.bfloat16
    d['identin'] = np.eye(128, dtype=bf)
    kk = np.arange(128)
    d['maskin'] = (kk[:, None] <= kk[None, :]).astype(bf)
    d['onesttin'] = np.ones((128, 32), dtype=bf)
    o32 = np.zeros((32, 128), np.float32); o32[0, :] = 1.0
    d['os32fin'] = o32
    d['os32bin'] = o32.astype(bf)
    d['onecolbin'] = np.ones((128, 1), dtype=bf)
    d['iotacolin'] = np.arange(128, dtype=np.float32).reshape(128, 1)
    ii = np.arange(128)
    d['eArowin'] = (-2.0 * (ii % 64) / 128.0).astype(np.float32).reshape(1, 128)
    jj = np.arange(512)
    d['eBcolin'] = np.ascontiguousarray((-2.0 * jj / 1024.0).astype(np.float32).reshape(4, 128).T)
    d['signAin'] = np.where(ii % 128 < 64, -1.0, 1.0).astype(np.float32).reshape(1, 128)
    d['identfin'] = np.eye(128, dtype=np.float32)
    io = np.zeros((32, 2048), np.float32); io[0, :] = np.arange(2048)
    d['iota32in'] = io
    t64 = np.arange(2048, dtype=np.float64)
    dA = np.arange(128)
    invA = np.float64(np.float32(10000.0) ** np.float32(-2.0 * (dA % 64) / 128.0))
    angA = np.float64(np.float32(t64[:, None] * invA[None, :]))
    cosA = np.cos(angA)
    sinA = np.sin(angA) * np.where(dA < 64, -1.0, 1.0)[None, :]
    d['tabAcin'] = cosA.reshape(16, 128, 128).transpose(1, 0, 2).reshape(128, 2048).astype(ml_dtypes.bfloat16)
    d['tabAsin'] = sinA.reshape(16, 128, 128).transpose(1, 0, 2).reshape(128, 2048).astype(ml_dtypes.bfloat16)
    jB = np.arange(512)
    invB = np.float64(np.float32(10000.0) ** np.float32(-2.0 * jB / 1024.0))
    angB = np.float64(np.float32(t64[:, None] * invB[None, :]))
    cosB, sinB = np.cos(angB), np.sin(angB)
    d['tabBcin'] = cosB.reshape(4, 512, 4, 128).transpose(3, 0, 2, 1).reshape(128, 8192).astype(ml_dtypes.bfloat16)
    d['tabBsin'] = sinB.reshape(4, 512, 4, 128).transpose(3, 0, 2, 1).reshape(128, 8192).astype(ml_dtypes.bfloat16)
    return d


def kernel(**inputs):
    from concourse.bass_utils import run_bass_kernel_spmd
    nc, _ = build()
    in_maps = [_host_prep(inputs, core) for core in range(8)]
    res = run_bass_kernel_spmd(nc, in_maps, core_ids=list(range(8)))
    out = np.empty((8, 2048, 2048), dtype=np.float32)
    for core in range(8):
        out[core] = np.asarray(res.results[core]['xoT']).T
    return out
